# version 07
# baseline (speedup 1.0000x reference)
import numpy as np
import concourse.bass as bass
import concourse.mybir as mybir
from concourse.bass_utils import run_bass_kernel_spmd
from concourse import library_config
from contextlib import ExitStack

F32 = mybir.dt.float32
F32R = mybir.dt.float32r
BF16 = mybir.dt.bfloat16
I16 = mybir.dt.int16
I32 = mybir.dt.int32
U32 = mybir.dt.uint32
AF = mybir.ActivationFunctionType
ALU = mybir.AluOpType
AX = mybir.AxisListType

T, D = 4096, 1024
H16, CS = 16, 16
HEADS, NK, DK, DI, NE = 8, 128, 512, 2048, 16384
NA, NB = 512, 128
EPS = 1e-5
NCORES = 8



_VEC_IO = {
    "tensor_scalar": (("out",), ("in0", "scalar1", "scalar2")),
    "tensor_tensor": (("out",), ("in0", "in1")),
    "tensor_tensor_scan": (("out",), ("data0", "data1", "initial")),
    "scalar_tensor_tensor": (("out", "accum_out"), ("in0", "scalar", "in1")),
    "tensor_reduce": (("out",), ("in_",)),
    "reduce_max": (("out",), ("in_",)),
    "reduce_sum": (("out",), ("in_",)),
    "reciprocal": (("out",), ("in_",)),
    "memset": (("ap",), ()),
    "tensor_copy": (("dest",), ("src",)),
    "bn_stats": (("out",), ("in_",)),
    "bn_aggr": (("out",), ("in_",)),
    "max_index": (("out",), ("in_max", "in_values")),
    "max": (("out",), ("in_",)),
}
_ACT_IO = {
    "activation": (("out", "accum_out"), ("in_", "bias", "scale")),
    "copy": (("out",), ("in_",)),
}


class EngGuard:

    def __init__(self, eng, iomap):
        self._eng = eng
        self._iomap = iomap
        self._dirty = set()
        import inspect as _insp
        self._sigs = {}
        for m in iomap:
            try:
                self._sigs[m] = _insp.signature(getattr(eng, m))
            except (ValueError, TypeError):
                self._sigs[m] = None

    @staticmethod
    def _name_of(v):
        t = getattr(v, "tensor", None)
        return getattr(t, "name", None)

    def drain(self):
        self._dirty.clear()
        return self._eng.drain()

    def __getattr__(self, name):
        fn = getattr(self._eng, name)
        io = self._iomap.get(name)
        if io is None:
            return fn
        outs_k, ins_k = io
        sig = self._sigs.get(name)

        def wrapped(*args, **kwargs):
            if sig is not None:
                ba = sig.bind_partial(*args, **kwargs)
                argmap = ba.arguments
            else:
                argmap = kwargs
            rd = set()
            for k in ins_k:
                n = self._name_of(argmap.get(k))
                if n:
                    rd.add(n)
            if rd & self._dirty:
                self._eng.drain()
                self._dirty.clear()
            r = fn(*args, **kwargs)
            for k in outs_k:
                n = self._name_of(argmap.get(k))
                if n:
                    self._dirty.add(n)
            return r

        return wrapped


class Sem:
    def __init__(self, nc, ctx, name):
        self.h = ctx.enter_context(nc.semaphore(name))
        self.n = 0

    def inc(self, instr, k=1):
        instr.then_inc(self.h, k)
        self.n += k
        return instr

    def wait(self, eng, at=None):
        eng.wait_ge(self.h, self.n if at is None else at)

    def mark(self):
        return (self, self.n)


def build(n_tokens=T, stages='b', dbg_x2=False):
    alev = {'a1': 1, 'a2': 2, 'a3': 3, 'a4': 4}.get(stages, 5)
    if stages.startswith('a') and stages != 'a':
        stages = 'apart'
    nc = bass.Bass("TRN2", target_bir_lowering=False,
                   detect_race_conditions=False)
    TA = n_tokens // NA
    TB = n_tokens // NB

    dram = lambda name, shape, dt, kind: nc.dram_tensor(name, shape, dt, kind=kind).ap()
    xin = dram("xin", [T, D], F32, "ExternalInput")
    w_in = dram("w_in", [D, D], F32, "ExternalInput")
    ln_gd = dram("ln_gd", [1, D], F32, "ExternalInput")
    ln_bd = dram("ln_bd", [1, D], F32, "ExternalInput")
    w_gate = dram("w_gate", [32, D], F32, "ExternalInput")
    b_gated = dram("b_gated", [1, 32], F32, "ExternalInput")
    eig_d = dram("eig_d", [1, H16], F32, "ExternalInput")
    w_out = dram("w_out", [D, D], F32, "ExternalInput")
    wq = dram("wq", [8192, D], F32, "ExternalInput")
    keysd = dram("keysd", [2048, DK], F32, "ExternalInput")
    w_inf = dram("w_inf", [DI, D], F32, "ExternalInput")
    w_outf = dram("w_outf", [D, DI], F32, "ExternalInput")
    ptab = dram("ptab", [NE, 6144], F32, "ExternalInput")
    cvec = dram("cvec", [1, D], F32, "ExternalInput")
    wmod = dram("wmod", [D, 6144], F32, "ExternalInput")
    outd = dram("outd", [T, D], F32, "ExternalOutput")
    x2buf = dram("x2buf", [D, T], F32,
                 "ExternalOutput" if dbg_x2 else "Internal")
    ptab_bf = dram("ptab_bf", [NE, 6144], BF16, "Internal")
    modscr = dram("modscr", [1, 6144], F32, "Internal")
    widx = dram("widx", [4, NB * HEADS], I16, "Internal")
    if dbg_x2:
        d_xp = dram("d_xp", [T // NA, 128, 8, NA], BF16, "ExternalOutput")
        d_xpln = dram("d_xpln", [T // NA, 128, 8, NA], BF16, "ExternalOutput")
        d_cum = dram("d_cum", [T // NA, 16, NA], BF16, "ExternalOutput")
        d_b = dram("d_b", [T // NA, 128, 8, NA], BF16, "ExternalOutput")
        d_hall = dram("d_hall", [T // NA, 128, 8, NA], BF16, "ExternalOutput")
        d_d0 = dram("d_d0", [128, NA], BF16, "ExternalOutput")
        d_d1 = dram("d_d1", [128, NA], BF16, "ExternalOutput")
        d_tex = dram("d_tex", [128, NA], BF16, "ExternalOutput")
        d_u = dram("d_u", [128, 8, NA], BF16, "ExternalOutput")
        d_mr = dram("d_mr", [16, NA], F32, "ExternalOutput")
        d_cp = dram("d_cp", [16, NA], F32, "ExternalOutput")
        d_bet = dram("d_bet", [16, 4, 128], F32, "ExternalOutput")
        d_wout = dram("d_wout", [128, 8, 8, 128], BF16, "ExternalOutput")
        d_xfm = dram("d_xfm", [128, 8, NA], F32, "ExternalOutput")

    x2buf_v = x2buf.rearrange("(jb p) t -> p jb t", p=128)

    ctx = ExitStack()
    sb = lambda name, shape, dt: ctx.enter_context(nc.sbuf_tensor(name, shape, dt))

    ctxW = ExitStack()
    sw = lambda name, shape, dt: ctxW.enter_context(nc.sbuf_tensor(name, shape, dt, side="right"))
    it32 = sw("it32", [128, 128], I32)
    ehd = sw("ehd", [16, 8, 128], BF16)
    ones_col = sw("ones_col", [1, 128], F32)
    idn_bf = sb("idn_bf", [128, 128], BF16)
    idn_f32 = sb("idn_f32", [128, 128], F32)
    ones_bf = sb("ones_bf", [128, 128], BF16)
    c16512 = sb("c16512", [128, 8], I32)
    eps128 = sb("eps128", [128, 1], F32)
    ca1_fm = sb("ca1_fm", [128, 8], F32)
    sa_bf = sb("sa_bf", [128, 8], BF16)
    ga_fm = sb("ga_fm", [128, 8], F32)
    lng_fm = sb("lng_fm", [128, 8], F32)
    lnb_fm = sb("lnb_fm", [128, 8], F32)
    bgA = sb("bgA", [16, 1], F32)
    bgB = sb("bgB", [16, 1], F32)
    eig_v = sb("eig_v", [16, 1], F32)
    cbtm = sb("cbtm", [128, D], BF16)
    sbtm = sb("sbtm", [128, D], BF16)
    gbtm = sb("gbtm", [128, D], F32)

    ps = ctx.enter_context(nc.psum_tensor("ps", [128, 4096], F32))
    psbf = ps.bitcast(BF16)

    def bank(i, cols=512):
        return ps[:, 512 * i: 512 * i + cols]

    S = lambda name: Sem(nc, ctx, name)
    sem_names = ["s_ld", "s_ldg", "s_tp", "s_ev", "s_mm", "s_dv", "s_ac",
                 "s_x", "s_x2", "s_bx", "s_bidx", "s_bg", "s_gf", "s_out",
                 "s_ph", "s_gp0", "s_conv"]
    SM = {n: S(n) for n in sem_names}
    s_ld, s_ldg, s_tp, s_ev, s_mm, s_dv, s_ac = (SM[k] for k in sem_names[:7])
    (s_x, s_x2, s_bx, s_bidx, s_bg, s_gf, s_out, s_ph,
     s_gp0, s_conv) = (SM[k] for k in sem_names[7:])

    bankW = {}
    bankR = {}

    def guard_write(eng, i):
        for m in bankR.get(i, []):
            m[0].wait(eng, m[1])
        bankR[i] = []

    def guard_read(eng, i):
        m = bankW.get(i)
        if m is not None:
            m[0].wait(eng, m[1])

    TEN, GPS, SYN = nc.tensor, nc.gpsimd, nc.sync
    VEC = EngGuard(nc.vector, _VEC_IO)
    ACT2 = EngGuard(nc.scalar, _ACT_IO)

    def copy_ts(eng, out, in0, sem, k=1, dst=None):
        return sem.inc(eng.tensor_scalar(out=out, in0=in0, scalar1=0.0,
                                         scalar2=None, op0=ALU.add), k)

    block = ctx.enter_context(nc.Block())

    with ctx:
        GPS.iota(it32[:], pattern=[[1, 128]], base=0, channel_multiplier=-1)
        s_gp0.inc(GPS.iota(c16512[:], pattern=[[16512, 8]], base=0,
                           channel_multiplier=0))
        s_gp0.wait(VEC)
        VEC.tensor_scalar(out=idn_bf[:], in0=it32[:], scalar1=0.0,
                          scalar2=None, op0=ALU.is_equal)
        VEC.tensor_scalar(out=idn_f32[:], in0=it32[:], scalar1=0.0,
                          scalar2=None, op0=ALU.is_equal)
        VEC.memset(ones_bf[:], 1.0)
        VEC.memset(eps128[:], float(EPS))
        s_dv.inc(VEC.memset(ones_col[:], 1.0))
        s_dv.wait(GPS)
        for c in range(8):
            s_gp0.inc(GPS.iota(it32[0:16, 0:128], pattern=[[1, 2], [0, 64]],
                               base=2 * c, channel_multiplier=-1))
            s_gp0.wait(VEC)
            s_dv.inc(VEC.tensor_scalar(out=ehd[:, c, :], in0=it32[0:16, 0:128],
                                       scalar1=0.0, scalar2=None,
                                       op0=ALU.is_equal))
            s_dv.wait(GPS)
        GPS.load_library(library_config.mlp)
        conv_emitted = [0]

        def emit_conv(k):
            for _ in range(k):
                rb = conv_emitted[0]
                if rb >= 16:
                    return
                conv_emitted[0] += 1
                s_conv.inc(GPS.dma_start(
                    out=ptab_bf[1024 * rb:1024 * rb + 1024, :],
                    in_=ptab[1024 * rb:1024 * rb + 1024, :],
                    max_dma_last_dim=8192), 16)

        ctx0 = ExitStack()
        crow = ctx0.enter_context(nc.sbuf_tensor("crow", [8, 128], F32, side="right"))
        scfm_bf = ctx0.enter_context(nc.sbuf_tensor("scfm_bf", [128, 8], BF16, side="right"))
        wmod_sb = ctx0.enter_context(nc.sbuf_tensor("wmod_sb", [128, 8, 6144], BF16, side="right"))
        modrow = ctx0.enter_context(nc.sbuf_tensor("modrow", [1, 6144], F32, side="right"))

        s_ld.inc(SYN.dma_start(out=crow[:],
                               in_=cvec.rearrange("o (p f) -> (o p) f", p=8)), 16)
        s_ld.wait(ACT2)
        s_ac.inc(ACT2.activation(out=crow[:], in_=crow[:], func=AF.Silu))
        s_ac.wait(TEN)
        s_dv.wait(TEN)
        s_tp.inc(TEN.matmul(bank(6)[:, 0:8], crow[:], idn_f32[0:8, 0:8],
                            is_transpose=True, start=True, stop=True))
        bankW[6] = s_tp.mark()
        s_tp.wait(VEC)
        s_dv.inc(VEC.tensor_scalar(out=scfm_bf[:], in0=bank(6)[:, 0:8],
                                   scalar1=0.0, scalar2=None, op0=ALU.add))
        bankR.setdefault(6, []).append(s_dv.mark())

        for r in range(8):
            s_ldg.inc(GPS.dma_start(out=wmod_sb[:, r, :],
                                    in_=wmod[128 * r:128 * r + 128, :],
                                    max_dma_last_dim=8192), 16)
        s_ldg.wait(TEN)
        s_dv.wait(TEN)
        for s_i in range(12):
            bki = s_i % 2
            guard_write(TEN, bki)
            for r in range(8):
                mmi = TEN.matmul(bank(bki)[0:1, :], scfm_bf[:, r:r + 1],
                                 wmod_sb[:, r, 512 * s_i:512 * s_i + 512],
                                 start=(r == 0), stop=(r == 7))
            s_mm.inc(mmi)
            bankW[bki] = s_mm.mark()
            guard_read(VEC, bki)
            s_dv.inc(VEC.tensor_scalar(out=modrow[:, 512 * s_i:512 * s_i + 512],
                                       in0=bank(bki)[0:1, :], scalar1=0.0,
                                       scalar2=None, op0=ALU.add))
            bankR.setdefault(bki, []).append(s_dv.mark())
        s_dv.wait(SYN)
        s_ld.inc(SYN.dma_start(out=modscr, in_=modrow[:]), 16)
        s_ld.wait(SYN)
        s_ld.wait(GPS)
        ctx0.close()

        ctx0 = ExitStack()
        vrow = ctx0.enter_context(nc.sbuf_tensor("vrow", [8, 128], F32, side="right"))
        rowt = ctx0.enter_context(nc.sbuf_tensor("rowt", [1, 2, D], F32, side="right"))
        tmp8 = ctx0.enter_context(nc.sbuf_tensor("tmp8", [128, 4, 8], F32, side="right"))

        def vec_to_fm(src_ap, post):
            s_dv.wait(SYN)
            s_ld.inc(SYN.dma_start(out=vrow[:],
                                   in_=src_ap.rearrange("o (p f) -> (o p) f", p=8)), 16)
            s_ld.wait(TEN)
            guard_write(TEN, 7)
            s_tp.inc(TEN.matmul(bank(7)[:, 0:8], vrow[:], idn_f32[0:8, 0:8],
                                is_transpose=True, start=True, stop=True))
            bankW[7] = s_tp.mark()
            s_tp.wait(VEC)
            post(bank(7)[:, 0:8])
            bankR.setdefault(7, []).append(s_dv.mark())

        vec_to_fm(modscr[:, 0:D],
                  lambda p: copy_ts(VEC, sa_bf[:], p, s_dv))
        vec_to_fm(modscr[:, D:2 * D],
                  lambda p: s_dv.inc(VEC.tensor_scalar(out=ca1_fm[:], in0=p,
                                                       scalar1=1.0, scalar2=None,
                                                       op0=ALU.add)))
        vec_to_fm(ln_gd, lambda p: copy_ts(VEC, lng_fm[:], p, s_dv))
        vec_to_fm(ln_bd, lambda p: copy_ts(VEC, lnb_fm[:], p, s_dv))

        def ga_post(p):
            g0 = tmp8[:, 0, :]
            VEC.tensor_scalar(out=g0, in0=p, scalar1=0.0, scalar2=None, op0=ALU.add)
            VEC.tensor_scalar(out=tmp8[:, 1, :], in0=g0, scalar1=-1.0,
                              scalar2=None, op0=ALU.mult)
            VEC.tensor_tensor(out=tmp8[:, 1, :], in0=g0, in1=tmp8[:, 1, :], op=ALU.max)
            VEC.tensor_scalar(out=tmp8[:, 1, :], in0=tmp8[:, 1, :], scalar1=0.25,
                              scalar2=None, op0=ALU.max)
            VEC.tensor_scalar(out=tmp8[:, 2, :], in0=g0, scalar1=0.0, scalar2=2.0,
                              op0=ALU.is_ge, op1=ALU.mult)
            VEC.tensor_scalar(out=tmp8[:, 2, :], in0=tmp8[:, 2, :], scalar1=-1.0,
                              scalar2=None, op0=ALU.add)
            s_dv.inc(VEC.tensor_tensor(out=ga_fm[:], in0=tmp8[:, 1, :],
                                       in1=tmp8[:, 2, :], op=ALU.mult))

        vec_to_fm(modscr[:, 2 * D:3 * D], ga_post)

        def row_to_tm(src_ap, dst, mode):
            s_dv.wait(SYN)
            s_ld.inc(SYN.dma_start(out=rowt[:, 0, :], in_=src_ap), 16)
            s_ld.wait(VEC)
            r = rowt[:, 0, :]
            if mode == "add1":
                VEC.tensor_scalar(out=r, in0=r, scalar1=1.0, scalar2=None,
                                  op0=ALU.add)
            elif mode == "clamp":
                r2 = rowt[:, 1, :]
                VEC.tensor_scalar(out=r2, in0=r, scalar1=-1.0, scalar2=None,
                                  op0=ALU.mult)
                VEC.tensor_tensor(out=r2, in0=r, in1=r2, op=ALU.max)
                VEC.tensor_scalar(out=r2, in0=r2, scalar1=0.25, scalar2=None,
                                  op0=ALU.max)
                VEC.tensor_scalar(out=r, in0=r, scalar1=0.0, scalar2=2.0,
                                  op0=ALU.is_ge, op1=ALU.mult)
                VEC.tensor_scalar(out=r, in0=r, scalar1=-1.0, scalar2=None,
                                  op0=ALU.add)
                VEC.tensor_tensor(out=r, in0=r, in1=r2, op=ALU.mult)
            s_dv.inc(VEC.nop())
            s_dv.wait(TEN)
            for h in range(2):
                guard_write(TEN, 7)
                s_mm.inc(TEN.matmul(bank(7), ones_col[:],
                                    rowt[:, 0, 512 * h:512 * h + 512],
                                    start=True, stop=True))
                bankW[7] = s_mm.mark()
                s_mm.wait(VEC)
                copy_ts(VEC, dst[:, 512 * h:512 * h + 512], bank(7), s_dv)
                bankR.setdefault(7, []).append(s_dv.mark())

        row_to_tm(modscr[:, 4 * D:5 * D], cbtm, "add1")
        row_to_tm(modscr[:, 3 * D:4 * D], sbtm, "copy")
        row_to_tm(modscr[:, 5 * D:6 * D], gbtm, "clamp")

        s_dv.wait(SYN)
        s_ld.inc(SYN.dma_start(out=rowt[0:1, 0, 0:32], in_=b_gated), 16)
        s_ld.inc(SYN.dma_start(out=rowt[0:1, 1, 0:16], in_=eig_d), 16)
        s_ld.wait(TEN)
        guard_write(TEN, 7)
        TEN.matmul(bank(7)[0:16, 0:1], rowt[0:1, 0, 0:16], idn_f32[0:1, 0:1],
                   is_transpose=True, start=True, stop=False)
        TEN.matmul(bank(7)[0:16, 1:2], rowt[0:1, 0, 16:32], idn_f32[0:1, 0:1],
                   is_transpose=True, start=False, stop=False)
        s_tp.inc(TEN.matmul(bank(7)[0:16, 2:3], rowt[0:1, 1, 0:16],
                            idn_f32[0:1, 0:1], is_transpose=True, start=False,
                            stop=True))
        bankW[7] = s_tp.mark()
        s_tp.wait(VEC)
        copy_ts(VEC, bgA[:], bank(7)[0:16, 0:1], s_dv)
        copy_ts(VEC, bgB[:], bank(7)[0:16, 1:2], s_dv)
        copy_ts(VEC, eig_v[:], bank(7)[0:16, 2:3], s_dv)
        bankR.setdefault(7, []).append(s_dv.mark())
        s_dv.wait(ACT2)
        s_ac.inc(ACT2.activation(out=eig_v[:], in_=eig_v[:], func=AF.Tanh))
        ctx0.close()

        wp_sb = ctxW.enter_context(nc.sbuf_tensor("wp_sb", [128, 8, 8, 128], BF16, side="right"))
        woutT = ctxW.enter_context(nc.sbuf_tensor("woutT", [128, 8, 8, 128], BF16, side="right"))
        wgT = ctxW.enter_context(nc.sbuf_tensor("wgT", [128, 8, 32], BF16, side="right"))
        augw1 = ctxW.enter_context(nc.sbuf_tensor("augw1", [1, 8, 128], BF16, side="right"))
        augw2 = ctxW.enter_context(nc.sbuf_tensor("augw2", [1, 8, 128], BF16, side="right"))
        wblk = ctxW.enter_context(nc.sbuf_tensor("wblk", [128, 2, 128], BF16, side="right"))
        wub = ctxW.enter_context(nc.sbuf_tensor("wub", [128, 2, 128], BF16, side="right"))
        w1f = ctxW.enter_context(nc.sbuf_tensor("w1f", [128, 16], BF16, side="right"))
        wcT = ctxW.enter_context(nc.sbuf_tensor("wcT", [16, 128], BF16, side="right"))
        wgrow = ctxW.enter_context(nc.sbuf_tensor("wgrow", [32, D], BF16, side="right"))
        wblk_guard = [None, None]

        def tr_block(i, src_ap, evac):
            par = i % 2
            bki = 2 * par
            if wblk_guard[par] is not None:
                m = wblk_guard[par]
                m[0].wait(GPS, m[1])
            s_ldg.inc(GPS.dma_start(out=wblk[:, par, :], in_=src_ap,
                                    max_dma_last_dim=8192), 16)
            s_ldg.wait(TEN)
            guard_write(TEN, bki)
            s_tp.inc(TEN.matmul(psbf[:, 1024 * bki:1024 * bki + 128],
                                wblk[:, par, :], idn_bf[:], is_transpose=True,
                                start=True, stop=True))
            wblk_guard[par] = s_tp.mark()
            bankW[bki] = s_tp.mark()
            guard_read(VEC, bki)
            evac(psbf[:, 1024 * bki:1024 * bki + 128])
            bankR.setdefault(bki, []).append(s_dv.mark())

        for jb in range(8):
            for db in range(8):
                i = jb * 8 + db
                par = i % 2

                def evac(pslice, jb=jb, db=db, par=par):
                    VEC.tensor_scalar(out=wp_sb[:, db, jb, :], in0=pslice,
                                      scalar1=ca1_fm[:, db:db + 1],
                                      scalar2=None, op0=ALU.mult)
                    s_dv.inc(VEC.tensor_scalar(out=wub[:, par, :], in0=pslice,
                                               scalar1=0.0, scalar2=None,
                                               op0=ALU.add))

                tr_block(i, w_in[128 * jb:128 * jb + 128,
                                 128 * db:128 * db + 128], evac)
                s_dv.wait(TEN)
                guard_write(TEN, 4)
                s_mm.inc(TEN.matmul(ps[:, 2048 + jb:2049 + jb], wub[:, par, :],
                                    sa_bf[:, db:db + 1],
                                    start=(jb == 0 and db == 0),
                                    stop=(jb == 7 and db == 7)))
        bankW[4] = s_mm.mark()

        guard_write(TEN, 5)
        for jb in range(8):
            for db in range(8):
                mmi = TEN.matmul(ps[:, 2560 + jb:2561 + jb], wp_sb[:, db, jb, :],
                                 ones_bf[:, 0:1],
                                 start=(jb == 0 and db == 0),
                                 stop=(jb == 7 and db == 7))
            s_mm.inc(mmi)
        bankW[5] = s_mm.mark()
        guard_read(VEC, 4)
        guard_read(VEC, 5)
        copy_ts(VEC, w1f[:, 8:16], ps[:, 2048:2056], s_dv)
        copy_ts(VEC, w1f[:, 0:8], ps[:, 2560:2568], s_dv)
        bankR.setdefault(4, []).append(s_dv.mark())
        bankR.setdefault(5, []).append(s_dv.mark())
        s_dv.wait(TEN)
        guard_write(TEN, 6)
        s_tp.inc(TEN.matmul(psbf[0:16, 6144:6272], w1f[:], idn_bf[:],
                            is_transpose=True, start=True, stop=True))
        bankW[6] = s_tp.mark()
        s_tp.wait(VEC)
        copy_ts(VEC, wcT[:], psbf[0:16, 6144:6272], s_dv)
        bankR.setdefault(6, []).append(s_dv.mark())
        s_dv.wait(SYN)
        s_ld.inc(SYN.dma_start(out=augw1[:], in_=wcT[0:8, :]), 16)
        s_ld.inc(SYN.dma_start(out=augw2[:], in_=wcT[8:16, :]), 16)
        s_ld.wait(SYN)

        for jb in range(8):
            for db in range(8):
                def evac(pslice, jb=jb, db=db):
                    s_dv.inc(VEC.tensor_scalar(out=woutT[:, db, jb, :],
                                               in0=pslice, scalar1=0.0,
                                               scalar2=None, op0=ALU.add))
                tr_block(jb * 8 + db, w_out[128 * jb:128 * jb + 128,
                                            128 * db:128 * db + 128], evac)

        s_ldg.inc(GPS.dma_start(out=wgrow[:], in_=w_gate,
                                max_dma_last_dim=8192), 16)
        s_ldg.wait(TEN)
        for db in range(8):
            guard_write(TEN, 6)
            s_tp.inc(TEN.matmul(psbf[:, 6144:6176],
                                wgrow[:, 128 * db:128 * db + 128],
                                idn_bf[0:32, 0:32], is_transpose=True,
                                start=True, stop=True))
            bankW[6] = s_tp.mark()
            s_tp.wait(VEC)
            copy_ts(VEC, wgT[:, db, :], psbf[:, 6144:6176], s_dv)
            bankR.setdefault(6, []).append(s_dv.mark())

        def barrier():
            tgt = s_ph.n + 5
            for eng in (GPS, VEC, ACT2, TEN, SYN):
                s_ph.inc(eng.nop())
            for eng in (GPS, VEC, ACT2, TEN, SYN):
                eng.wait_ge(s_ph.h, tgt)
            bankW.clear()
            bankR.clear()

        barrier()
        if stages == 'p0':
            s_ld.wait(SYN)
            s_ldg.wait(GPS)
            s_conv.wait(GPS)
            mybir.codegen_inst_isa_subclasses(nc)
            return nc

        ctxA = ExitStack()
        sbA = lambda name, shape, dt: ctxA.enter_context(nc.sbuf_tensor(name, shape, dt, side="right"))
        x_tm = sbA("x_tm", [128, 2, 4, D], F32)
        xfm32 = sbA("xfm32", [128, 8, NA], F32)
        xfm_bf = sbA("xfm_bf", [128, 8, NA], BF16)
        x2_bf = sbA("x2_bf", [128, 8, NA], BF16)
        xp_bf = sbA("xp_bf", [128, 8, NA], BF16)
        xpln = sbA("xpln", [128, 8, NA], BF16)
        hall = sbA("hall", [128, 8, NA], BF16)
        b_bf = sbA("b_bf", [128, 8, NA], BF16)
        cb_bf = sbA("cb_bf", [128, 8, NA], BF16)
        u_bf = sbA("u_bf", [128, 8, NA], BF16)
        d0t = sbA("d0t", [128, NA], BF16)
        d1t = sbA("d1t", [128, NA], BF16)
        texq = sbA("texq", [128, NA], BF16)
        t5q = sbA("t5q", [128, NA], BF16)
        carry_cb = sbA("carry_cb", [128, 8], BF16)
        carry_b = sbA("carry_b", [128, 8], BF16)
        scanst = sbA("scanst", [128, 8], BF16)
        mu_t = sbA("mu_t", [128, NA], F32)
        ms_t = sbA("ms_t", [128, NA], F32)
        rs_t = sbA("rs_t", [128, NA], F32)
        sq_t = sbA("sq_t", [128, NA], F32)
        sdev_bf = sbA("sdev_bf", [128, NA], BF16)
        negmu_bf = sbA("negmu_bf", [128, NA], BF16)
        rs2_t = sbA("rs2_t", [128, NA], F32)
        m2r_t = sbA("m2r_t", [128, NA], F32)
        gtm = sbA("gtm", [128, 32], F32)
        alph = sbA("alph", [16, 4, 128], F32)
        bet = sbA("bet", [16, 4, 128], F32)
        a_t = sbA("a_t", [16, NA], F32)
        cum_t = sbA("cum_t", [16, NA], F32)
        cp_t = sbA("cp_t", [16, NA], F32)
        msk_t = sbA("msk_t", [16, NA], F32)
        mr_t = sbA("mr_t", [16, NA], F32)
        cum_bf = sbA("cum_bf", [16, NA], BF16)
        mrb_bf = sbA("mrb_bf", [16, NA], BF16)
        beta_bf = sbA("beta_bf", [16, NA], BF16)
        x2st = sbA("x2st", [128, 2, NA], F32)

        cum3 = cum_t.rearrange("p (n c) -> p n c", c=CS)
        cp3 = cp_t.rearrange("p (n c) -> p n c", c=CS)

        n_x2_stores = 0
        for a in range(TA):
            par = a % 2
            if a >= 2:
                s_tp.wait(SYN, s_tp.n)
            s_x.inc(SYN.dma_start(
                out=x_tm[:, par, :, :],
                in_=xin[NA * a:NA * a + NA, :].rearrange("(s p) d -> p s d", p=128)), 16)
            emit_conv((16 + TA - 1) // TA)

            s_x.wait(TEN)
            if a > 0:
                s_dv.wait(TEN, s_dv.n)
            for grp in range(8):
                s_i, hf = grp // 2, grp % 2
                bki = grp % 2
                guard_write(TEN, bki)
                for q in range(4):
                    db = hf * 4 + q
                    mmi = TEN.matmul(bank(bki)[:, 128 * q:128 * q + 128],
                                     x_tm[:, par, s_i, 128 * db:128 * db + 128],
                                     idn_f32[:], is_transpose=True,
                                     start=(q == 0), stop=(q == 3))
                s_tp.inc(mmi)
                bankW[bki] = s_tp.mark()
                src3 = bank(bki).rearrange("p (a b) -> p a b", b=128)
                guard_read(ACT2, bki)
                dst32 = xfm32[:, 4 * hf:4 * hf + 4, 128 * s_i:128 * s_i + 128]
                s_ac.inc(ACT2.activation(out=dst32, in_=src3, func=AF.Copy,
                                         bias=0.0))
                bankR.setdefault(bki, []).append(s_ac.mark())
                s_ac.wait(VEC)
                dstbf = xfm_bf[:, 4 * hf:4 * hf + 4, 128 * s_i:128 * s_i + 128]
                s_dv.inc(VEC.tensor_scalar(out=dstbf, in0=dst32, scalar1=0.0,
                                           scalar2=None, op0=ALU.add))

            if alev < 2:
                continue
            s_ac.wait(ACT2)
            for db in range(8):
                ii = ACT2.activation(out=x2_bf[:, db, :], in_=xfm32[:, db, :],
                                     func=AF.Square)
            s_ac.inc(ii)

            s_dv.wait(TEN)
            s_ac.wait(TEN)
            guard_write(TEN, 4)
            for db in range(8):
                m1 = TEN.matmul(bank(4), ones_bf[:], xfm_bf[:, db, :],
                                start=(db == 0), stop=(db == 7))
            s_mm.inc(m1)
            bankW[4] = s_mm.mark()
            guard_write(TEN, 5)
            for db in range(8):
                m2 = TEN.matmul(bank(5), ones_bf[:], x2_bf[:, db, :],
                                start=(db == 0), stop=(db == 7))
            s_mm.inc(m2)
            bankW[5] = s_mm.mark()

            def stats_chain(rs_dst):
                guard_read(VEC, 4)
                guard_read(VEC, 5)
                VEC.tensor_scalar(out=mu_t[:], in0=bank(4), scalar1=1.0 / D,
                                  scalar2=None, op0=ALU.mult)
                s_dv.inc(VEC.tensor_scalar(out=ms_t[:], in0=bank(5),
                                           scalar1=1.0 / D, scalar2=None,
                                           op0=ALU.mult))
                bankR.setdefault(4, []).append(s_dv.mark())
                bankR.setdefault(5, []).append(s_dv.mark())
                s_dv.wait(ACT2)
                s_ac.inc(ACT2.activation(out=sq_t[:], in_=mu_t[:], func=AF.Square))
                s_ac.wait(VEC)
                s_dv.inc(VEC.tensor_tensor(out=ms_t[:], in0=ms_t[:], in1=sq_t[:],
                                           op=ALU.subtract))
                s_dv.wait(ACT2)
                s_ac.inc(ACT2.activation(out=ms_t[:], in_=ms_t[:], func=AF.Sqrt,
                                         bias=eps128[:]))
                s_ac.wait(VEC)
                s_dv.inc(VEC.reciprocal(out=rs_dst[:], in_=ms_t[:]))

            stats_chain(rs_t)
            VEC.tensor_scalar(out=sdev_bf[:], in0=ms_t[:], scalar1=0.0,
                              scalar2=None, op0=ALU.add)
            s_dv.inc(VEC.tensor_scalar(out=negmu_bf[:], in0=mu_t[:],
                                       scalar1=-1.0, scalar2=None, op0=ALU.mult))

            s_dv.wait(TEN)
            for jb in range(8):
                bki = 2 + jb % 2
                guard_write(TEN, bki)
                for db in range(8):
                    TEN.matmul(bank(bki), wp_sb[:, db, jb, :], xfm_bf[:, db, :],
                               start=(db == 0), stop=False)
                TEN.matmul(bank(bki), augw1[0:1, jb, :], negmu_bf[0:1, :],
                           start=False, stop=False)
                s_mm.inc(TEN.matmul(bank(bki), augw2[0:1, jb, :],
                                    sdev_bf[0:1, :], start=False, stop=True))
                bankW[bki] = s_mm.mark()
                guard_read(VEC, bki)
                s_dv.inc(VEC.tensor_tensor(out=xp_bf[:, jb, :], in0=bank(bki),
                                           in1=rs_t[:], op=ALU.mult))
                bankR.setdefault(bki, []).append(s_dv.mark())

            if alev < 3:
                continue
            s_dv.wait(ACT2)
            for db in range(8):
                ii = ACT2.activation(out=x2_bf[:, db, :], in_=xp_bf[:, db, :],
                                     func=AF.Square)
            s_ac.inc(ii)
            s_ac.wait(TEN)
            s_dv.wait(TEN)
            guard_write(TEN, 4)
            for db in range(8):
                m1 = TEN.matmul(bank(4), ones_bf[:], xp_bf[:, db, :],
                                start=(db == 0), stop=(db == 7))
            s_mm.inc(m1)
            bankW[4] = s_mm.mark()
            guard_write(TEN, 5)
            for db in range(8):
                m2 = TEN.matmul(bank(5), ones_bf[:], x2_bf[:, db, :],
                                start=(db == 0), stop=(db == 7))
            s_mm.inc(m2)
            bankW[5] = s_mm.mark()
            stats_chain(rs2_t)
            VEC.tensor_scalar(out=m2r_t[:], in0=mu_t[:], scalar1=-1.0,
                              scalar2=None, op0=ALU.mult)
            s_dv.inc(VEC.tensor_tensor(out=m2r_t[:], in0=m2r_t[:], in1=rs2_t[:],
                                       op=ALU.mult))

            for jb in range(8):
                VEC.tensor_tensor(out=xpln[:, jb, :], in0=xp_bf[:, jb, :],
                                  in1=rs2_t[:], op=ALU.mult)
                VEC.tensor_tensor(out=xpln[:, jb, :], in0=xpln[:, jb, :],
                                  in1=m2r_t[:], op=ALU.add)
                iv = VEC.tensor_scalar(out=xpln[:, jb, :], in0=xpln[:, jb, :],
                                       scalar1=lng_fm[:, jb:jb + 1],
                                       scalar2=lnb_fm[:, jb:jb + 1],
                                       op0=ALU.mult, op1=ALU.add)
            s_dv.inc(iv)

            for s_i in range(4):
                s_dv.wait(TEN)
                guard_write(TEN, 6)
                for db in range(8):
                    mg = TEN.matmul(ps[:, 3072:3104],
                                    xpln[:, db, 128 * s_i:128 * s_i + 128],
                                    wgT[:, db, :], start=(db == 0),
                                    stop=(db == 7))
                s_mm.inc(mg)
                bankW[6] = s_mm.mark()
                guard_read(VEC, 6)
                copy_ts(VEC, gtm[:], ps[:, 3072:3104], s_dv)
                bankR.setdefault(6, []).append(s_dv.mark())
                s_dv.wait(TEN)
                TEN.matmul(ps[0:16, 3104:3232], gtm[:, 0:16], idn_f32[:],
                           is_transpose=True, start=True, stop=False)
                s_mm.inc(TEN.matmul(ps[0:16, 3232:3360], gtm[:, 16:32],
                                    idn_f32[:], is_transpose=True, start=False,
                                    stop=True))
                bankW[6] = s_mm.mark()
                guard_read(ACT2, 6)
                ACT2.activation(out=alph[:, s_i, :], in_=ps[0:16, 3104:3232],
                                func=AF.Sigmoid, bias=bgA[:])
                s_ac.inc(ACT2.activation(out=bet[:, s_i, :],
                                         in_=ps[0:16, 3232:3360],
                                         func=AF.Sigmoid, bias=bgB[:]))
                bankR.setdefault(6, []).append(s_ac.mark())

            if alev < 4:
                continue
            s_ac.wait(VEC)
            av = alph.rearrange("p s c -> p (s c)")
            bv = bet.rearrange("p s c -> p (s c)")
            VEC.tensor_scalar(out=a_t[:], in0=av, scalar1=eig_v[:],
                              scalar2=None, op0=ALU.mult)
            VEC.tensor_scalar(out=cum_t[:], in0=a_t[:], scalar1=0.0,
                              scalar2=None, op0=ALU.add)
            for i in range(1, CS):
                VEC.tensor_tensor(out=cum3[:, :, i], in0=cum3[:, :, i],
                                  in1=cum3[:, :, i - 1], op=ALU.mult)
            VEC.tensor_scalar(out=cp_t[:, 1:NA], in0=cum_t[:, 0:NA - 1],
                              scalar1=0.0, scalar2=None, op0=ALU.add)
            s_dv.inc(VEC.memset(cp3[:, :, 0], 1.0))
            s_dv.wait(ACT2)
            s_ac.inc(ACT2.activation(out=msk_t[:], in_=cp_t[:], func=AF.Abs))
            s_ac.wait(VEC)
            VEC.tensor_scalar(out=msk_t[:], in0=msk_t[:], scalar1=1e-8,
                              scalar2=None, op0=ALU.is_gt)
            VEC.tensor_scalar(out=mr_t[:], in0=msk_t[:], scalar1=-1.0,
                              scalar2=1.0, op0=ALU.mult, op1=ALU.add)
            VEC.tensor_tensor(out=cp_t[:], in0=cp_t[:], in1=mr_t[:], op=ALU.add)
            VEC.reciprocal(out=mr_t[:], in_=cp_t[:])
            VEC.tensor_tensor(out=mr_t[:], in0=mr_t[:], in1=msk_t[:], op=ALU.mult)
            VEC.tensor_scalar(out=beta_bf[:], in0=bv, scalar1=0.0,
                              scalar2=None, op0=ALU.add)
            VEC.tensor_tensor(out=mrb_bf[:], in0=mr_t[:], in1=bv, op=ALU.mult)
            s_dv.inc(VEC.tensor_scalar(out=cum_bf[:], in0=cum_t[:], scalar1=0.0,
                                       scalar2=None, op0=ALU.add))

            s_dv.wait(TEN)
            for db in range(8):
                bki = 2 + db % 2
                for srcv, dstv, isb in ((beta_bf, b_bf, True),
                                        (mrb_bf, u_bf, True),
                                        (cum_bf, cb_bf, False)):
                    guard_write(TEN, bki)
                    s_mm.inc(TEN.matmul(bank(bki), ehd[:, db, :], srcv[:],
                                        start=True, stop=True))
                    bankW[bki] = s_mm.mark()
                    guard_read(VEC, bki)
                    if isb:
                        s_dv.inc(VEC.tensor_tensor(out=dstv[:, db, :],
                                                   in0=bank(bki),
                                                   in1=xpln[:, db, :],
                                                   op=ALU.mult))
                    else:
                        copy_ts(VEC, dstv[:, db, :], bank(bki), s_dv)
                    bankR.setdefault(bki, []).append(s_dv.mark())

            for db in range(8):
                d03 = d0t.rearrange("p (n c) -> p n c", c=CS)
                d13 = d1t.rearrange("p (n c) -> p n c", c=CS)
                cbv = cb_bf[:, db, :]
                cb3 = cbv.rearrange("p (n c) -> p n c", c=CS)
                bv3 = b_bf[:, db, :].rearrange("p (n c) -> p n c", c=CS)
                VEC.memset(d0t[:], 1.0)
                VEC.tensor_scalar(out=d03[:, 1:NA // CS, 0],
                                  in0=cb3[:, 0:NA // CS - 1, CS - 1],
                                  scalar1=0.0, scalar2=None, op0=ALU.add)
                VEC.tensor_scalar(out=d1t[:, 1:NA], in0=u_bf[:, db, 0:NA - 1],
                                  scalar1=0.0, scalar2=None, op0=ALU.add)
                VEC.tensor_scalar(out=d13[:, 1:NA // CS, 0],
                                  in0=bv3[:, 0:NA // CS - 1, CS - 1],
                                  scalar1=0.0, scalar2=None, op0=ALU.add)
                if a == 0:
                    VEC.memset(d0t[:, 0:1], 0.0)
                    VEC.memset(d1t[:, 0:1], 0.0)
                else:
                    VEC.tensor_scalar(out=d0t[:, 0:1], in0=carry_cb[:, db:db + 1],
                                      scalar1=0.0, scalar2=None, op0=ALU.add)
                    VEC.tensor_scalar(out=d1t[:, 0:1], in0=carry_b[:, db:db + 1],
                                      scalar1=0.0, scalar2=None, op0=ALU.add)
                if dbg_x2 and a == 0 and db == 0:
                    s_dv.inc(VEC.nop())
                    s_dv.wait(SYN)
                    s_ld.inc(SYN.dma_start(out=d_d0, in_=d0t[:]), 16)
                    s_ld.inc(SYN.dma_start(out=d_d1, in_=d1t[:]), 16)
                    s_ld.inc(SYN.dma_start(out=d_u, in_=u_bf[:]), 16)
                    s_ld.inc(SYN.dma_start(out=d_mr, in_=mr_t[:]), 16)
                    s_ld.inc(SYN.dma_start(out=d_cp, in_=cp_t[:]), 16)
                    s_ld.inc(SYN.dma_start(out=d_bet, in_=bet[:]), 16)
                    s_ld.wait(SYN)
                    s_ld.wait(VEC)
                VEC.tensor_tensor_scan(out=texq[:], data0=d0t[:], data1=d1t[:],
                                       initial=(0.0 if a == 0
                                                else scanst[:, db:db + 1]),
                                       op0=ALU.mult, op1=ALU.add)
                if dbg_x2 and a == 0 and db == 0:
                    s_dv.inc(VEC.nop())
                    s_dv.wait(SYN)
                    s_ld.inc(SYN.dma_start(out=d_tex, in_=texq[:]), 16)
                    s_ld.wait(SYN)
                    s_ld.wait(VEC)
                VEC.tensor_scalar(out=scanst[:, db:db + 1],
                                  in0=texq[:, NA - 1:NA], scalar1=0.0,
                                  scalar2=None, op0=ALU.add)
                VEC.tensor_tensor(out=t5q[:], in0=cbv, in1=texq[:], op=ALU.mult)
                VEC.tensor_tensor(out=hall[:, db, :], in0=t5q[:],
                                  in1=b_bf[:, db, :], op=ALU.add)
            cb4 = cb_bf.rearrange("p a (n c) -> p a n c", c=CS)
            b4 = b_bf.rearrange("p a (n c) -> p a n c", c=CS)
            VEC.tensor_scalar(out=carry_cb[:], in0=cb4[:, :, NA // CS - 1, CS - 1],
                              scalar1=0.0, scalar2=None, op0=ALU.add)
            s_dv.inc(VEC.tensor_scalar(out=carry_b[:],
                                       in0=b4[:, :, NA // CS - 1, CS - 1],
                                       scalar1=0.0, scalar2=None, op0=ALU.add))

            if alev < 5:
                continue
            if dbg_x2:
                s_dv.wait(SYN)
                s_ac.wait(SYN)
                s_ld.inc(SYN.dma_start(out=d_wout, in_=woutT[:]), 16)
                s_ld.inc(SYN.dma_start(out=d_xfm, in_=xfm32[:]), 16)
                for nm, tile in ((d_xp, xp_bf), (d_xpln, xpln), (d_b, b_bf),
                                 (d_hall, hall)):
                    s_ld.inc(SYN.dma_start(out=nm[a], in_=tile[:]), 16)
                s_ld.inc(SYN.dma_start(out=d_cum[a], in_=cum_bf[:]), 16)
                s_ld.wait(SYN)
            s_dv.wait(TEN)
            for jb in range(8):
                bki = 2 + jb % 2
                guard_write(TEN, bki)
                for db in range(8):
                    mmi = TEN.matmul(bank(bki), woutT[:, db, jb, :],
                                     hall[:, db, :], start=(db == 0),
                                     stop=(db == 7))
                s_mm.inc(mmi)
                bankW[bki] = s_mm.mark()
                guard_read(VEC, bki)
                if n_x2_stores >= 2:
                    s_x2.wait(VEC, 16 * (n_x2_stores - 1))
                s_dv.inc(VEC.scalar_tensor_tensor(
                    out=x2st[:, jb % 2, :], in0=bank(bki),
                    scalar=ga_fm[:, jb:jb + 1], in1=xfm32[:, jb, :],
                    op0=ALU.mult, op1=ALU.add))
                bankR.setdefault(bki, []).append(s_dv.mark())
                s_dv.wait(SYN)
                s_x2.inc(SYN.dma_start(out=x2buf_v[:, jb, NA * a:NA * a + NA],
                                       in_=x2st[:, jb % 2, :]), 16)
                n_x2_stores += 1

        s_x.wait(SYN)
        s_x2.wait(SYN)
        ctxA.close()
        ctxW.close()
        barrier()
        if stages in ('a', 'apart'):
            mybir.codegen_inst_isa_subclasses(nc)
            return nc

        kwT = sb("kwT", [128, 8, 2048], BF16)
        winfT = sb("winfT", [128, 8, DI], BF16)
        woutfT = sb("woutfT", [128, 16, D], BF16)
        ctx0 = ExitStack()
        fwblk = ctx0.enter_context(nc.sbuf_tensor("fwblk", [128, 2, 128], BF16, side="right"))
        kblk = ctx0.enter_context(nc.sbuf_tensor("kblk", [128, 512], BF16, side="right"))
        ktT = ctx0.enter_context(nc.sbuf_tensor("ktT", [128, 4, 128], BF16, side="right"))
        wqs = ctx0.enter_context(nc.sbuf_tensor("wqs", [128, 4, 1024], BF16, side="right"))
        kwrow = ctx0.enter_context(nc.sbuf_tensor("kwrow", [128, 1024], BF16, side="right"))
        fw_guard = [None, None]

        def ftr_block(i, src_ap, dst_ap):
            par = i % 2
            bki = 2 * par
            if fw_guard[par] is not None:
                m = fw_guard[par]
                m[0].wait(GPS, m[1])
            s_ldg.inc(GPS.dma_start(out=fwblk[:, par, :], in_=src_ap,
                                    max_dma_last_dim=8192), 16)
            s_ldg.wait(TEN)
            guard_write(TEN, bki)
            s_tp.inc(TEN.matmul(psbf[:, 1024 * bki:1024 * bki + 128],
                                fwblk[:, par, :], idn_bf[:], is_transpose=True,
                                start=True, stop=True))
            fw_guard[par] = s_tp.mark()
            bankW[bki] = s_tp.mark()
            guard_read(VEC, bki)
            copy_ts(VEC, dst_ap, psbf[:, 1024 * bki:1024 * bki + 128], s_dv)
            bankR.setdefault(bki, []).append(s_dv.mark())

        for jb in range(16):
            for db in range(8):
                ftr_block(jb * 8 + db,
                          w_inf[128 * jb:128 * jb + 128, 128 * db:128 * db + 128],
                          winfT[:, db, 128 * jb:128 * jb + 128])
        for db in range(8):
            for jb in range(16):
                ftr_block(db * 16 + jb,
                          w_outf[128 * db:128 * db + 128, 128 * jb:128 * jb + 128],
                          woutfT[:, jb, 128 * db:128 * db + 128])

        keys_v = keysd.rearrange("(h k p) d -> h k p d", h=8, k=128, p=2)
        for hp in range(16):
            h_, p_ = hp // 2, hp % 2
            s_dv.wait(GPS)
            s_ldg.inc(GPS.dma_start(out=kblk[:], in_=keys_v[h_, :, p_, :],
                                    max_dma_last_dim=8192), 16)
            f0 = p_ * 4096 + h_ * 512
            s_ldg.inc(GPS.dma_start(
                out=wqs[:],
                in_=wq[f0:f0 + 512, :].rearrange("(b q) c -> q b c", q=128),
                max_dma_last_dim=8192), 16)
            s_ldg.wait(TEN)
            guard_write(TEN, 4)
            for dkc in range(4):
                mmi = TEN.matmul(psbf[:, 4096 + 128 * dkc:4096 + 128 * dkc + 128],
                                 kblk[:, 128 * dkc:128 * dkc + 128], idn_bf[:],
                                 is_transpose=True, start=(dkc == 0),
                                 stop=(dkc == 3))
            s_tp.inc(mmi)
            bankW[4] = s_tp.mark()
            guard_read(VEC, 4)
            copy_ts(VEC, ktT.rearrange("a b c -> a (b c)"), psbf[:, 4096:4608],
                    s_dv)
            bankR.setdefault(4, []).append(s_dv.mark())
            s_dv.wait(TEN)
            for dt2 in range(2):
                bki = dt2
                guard_write(TEN, bki)
                for dkc in range(4):
                    mmi = TEN.matmul(bank(bki), ktT[:, dkc, :],
                                     wqs[:, dkc, 512 * dt2:512 * dt2 + 512],
                                     start=(dkc == 0), stop=(dkc == 3))
                s_mm.inc(mmi)
                bankW[bki] = s_mm.mark()
                guard_read(VEC, bki)
                copy_ts(VEC, kwrow[:, 512 * dt2:512 * dt2 + 512], bank(bki), s_dv)
                bankR.setdefault(bki, []).append(s_dv.mark())
            s_dv.wait(TEN)
            for db in range(8):
                guard_write(TEN, 5)
                s_tp.inc(TEN.matmul(psbf[:, 5120:5248],
                                    kwrow[:, 128 * db:128 * db + 128],
                                    idn_bf[:], is_transpose=True, start=True,
                                    stop=True))
                bankW[5] = s_tp.mark()
                guard_read(VEC, 5)
                sp = p_ * 8 + h_
                copy_ts(VEC, kwT[:, db, 128 * sp:128 * sp + 128],
                        psbf[:, 5120:5248], s_dv)
                bankR.setdefault(5, []).append(s_dv.mark())
        ctx0.close()
        barrier()
        if stages == 'w':
            mybir.codegen_inst_isa_subclasses(nc)
            return nc

        ctxB = ExitStack()
        sbB = lambda name, shape, dt: ctxB.enter_context(nc.sbuf_tensor(name, shape, dt, side="right"))
        x2fm = sbB("x2fm", [128, 8, NB], F32)
        x2tm = sbB("x2tm", [128, 2, D], F32)
        h2T = sbB("h2T", [128, 2, D], BF16)
        h2fm = sbB("h2fm", [128, 2, 8, NB], BF16)
        bnout = sbB("bnout", [128, 12], F32)
        mvst = sbB("mvst", [128, 2], F32)
        rs3 = sbB("rs3", [128, 1], F32)
        p3 = sbB("p3", [128, 1], F32)
        mx = sbB("mx", [128, 16], F32)
        mix = sbB("mix", [128, 16], U32)
        gatev = sbB("gatev", [128, 2, 8], F32)
        ef = sbB("ef", [128, 8], F32)
        e16 = sbB("e16", [128, 8], I16)
        eT = sbB("eT", [8, 128], I16)
        w64 = sbB("w64", [64, 16], I16)
        idx16 = sbB("idx16", [128, 2, 64], I16)
        lin = sbB("lin", [128, 8], F32)
        ling = sbB("ling", [128, 8], F32)
        dmat = sbB("dmat", [128, 8, 128], BF16)
        ghT = sbB("ghT", [128, DI], BF16)
        gfm = sbB("gfm", [128, 16, 128], BF16)
        inA = sbB("inA", [128, 8, 1024], BF16)
        inBr = sbB("inBr", [128, 16384], BF16)
        x3t = ghT.bitcast(F32)
        outbuf = sbB("outbuf", [128, 3, 3072], BF16)
        vB = inBr.rearrange("p (h c) -> p h c", c=2048)
        vO = inBr[:, 0:15360].rearrange("p (h c) -> p h c", c=3072)
        nidx_reg = GPS.alloc_register()
        GPS.reg_mov(nidx_reg, 1024)
        nidx_reg2 = GPS.alloc_register()
        GPS.reg_mov(nidx_reg2, 384)
        nidx_reg3 = GPS.alloc_register()
        GPS.reg_mov(nidx_reg3, 640)

        M = {}

        def setm(name, sem):
            M[name] = sem.mark()

        def wm(eng, name):
            mk = M.get(name)
            if mk is not None:
                mk[0].wait(eng, mk[1])

        VEC.memset(idx16[:], 0)
        thf = x2fm.rearrange("p a b -> p (a b)")

        dscr = sbB("dscr", [128, 2048], BF16)

        def front(b):
            par = b % 2
            wm(SYN, "thf_done")
            s_bx.inc(SYN.dma_start(out=x2fm[:],
                                   in_=x2buf_v[:, :, NB * b:NB * b + NB]), 16)
            setm("x2fm_prod", s_bx)
            wm(TEN, "x2fm_prod")
            for hf in range(2):
                bki = 4 + hf
                guard_write(TEN, bki)
                for q in range(4):
                    db = hf * 4 + q
                    mmi = TEN.matmul(bank(bki)[:, 128 * q:128 * q + 128],
                                     x2fm[:, db, :], idn_f32[:],
                                     is_transpose=True, start=(q == 0),
                                     stop=(q == 3))
                s_tp.inc(mmi)
                bankW[bki] = s_tp.mark()
                guard_read(ACT2, bki)
                wm(ACT2, "x2tm_cons%d" % par)
                s_ac.inc(ACT2.activation(out=x2tm[:, par, 512 * hf:512 * hf + 512],
                                         in_=bank(bki), func=AF.Copy, bias=0.0))
                bankR.setdefault(bki, []).append(s_ac.mark())
            setm("x2tm_prod%d" % par, s_ac)
            wm(VEC, "x2tm_prod%d" % par)
            VEC.bn_stats(out=bnout[:, 0:6], in_=x2tm[:, par, 0:512])
            VEC.bn_stats(out=bnout[:, 6:12], in_=x2tm[:, par, 512:1024])
            s_dv.inc(VEC.bn_aggr(out=mvst[:], in_=bnout[:]))
            s_dv.wait(ACT2, s_dv.n)
            s_ac.inc(ACT2.activation(out=rs3[:], in_=mvst[:, 1:2], func=AF.Sqrt,
                                     bias=eps128[:]))
            s_ac.wait(VEC, s_ac.n)
            VEC.reciprocal(out=rs3[:], in_=rs3[:])
            VEC.tensor_tensor(out=p3[:], in0=mvst[:, 0:1], in1=rs3[:], op=ALU.mult)
            VEC.tensor_scalar(out=p3[:], in0=p3[:], scalar1=-1.0, scalar2=None,
                              op0=ALU.mult)
            VEC.tensor_scalar(out=thf[:], in0=x2tm[:, par, :], scalar1=rs3[:],
                              scalar2=p3[:], op0=ALU.mult, op1=ALU.add)
            VEC.tensor_tensor(out=thf[:], in0=thf[:], in1=cbtm[:], op=ALU.mult)
            s_dv.inc(VEC.tensor_tensor(out=h2T[:, par, :], in0=thf[:],
                                       in1=sbtm[:], op=ALU.add))
            setm("thf_done", s_dv)
            setm("h2T_prod%d" % par, s_dv)
            wm(TEN, "h2T_prod%d" % par)
            wm(TEN, "h2fm_cons%d" % par)
            for hf in range(2):
                guard_write(TEN, 4)
                for q in range(4):
                    db = hf * 4 + q
                    mmi = TEN.matmul(psbf[:, 4096 + 128 * q:4096 + 128 * q + 128],
                                     h2T[:, par, 128 * db:128 * db + 128],
                                     idn_bf[:], is_transpose=True,
                                     start=(q == 0), stop=(q == 3))
                s_tp.inc(mmi)
                bankW[4] = s_tp.mark()
                guard_read(VEC, 4)
                s_dv.inc(VEC.tensor_scalar(
                    out=h2fm[:, par, 4 * hf:4 * hf + 4, :],
                    in0=psbf[:, 4096:4608].rearrange("p (a b) -> p a b", b=128),
                    scalar1=0.0, scalar2=None, op0=ALU.add))
                bankR.setdefault(4, []).append(s_dv.mark())
            setm("h2fm_prod%d" % par, s_dv)
            wm(TEN, "h2fm_prod%d" % par)
            for half in range(2):
                b0 = 4 - 2 * half
                guard_write(TEN, b0)
                guard_write(TEN, b0 + 1)
                for ks in range(2):
                    for db in range(8):
                        mmi = TEN.matmul(
                            bank(b0 + ks), h2fm[:, par, db, :],
                            kwT[:, db, 1024 * half + 512 * ks:
                                1024 * half + 512 * ks + 512],
                            start=(db == 0), stop=(db == 7))
                    s_mm.inc(mmi)
                    bankW[b0 + ks] = s_mm.mark()
                guard_read(VEC, b0)
                guard_read(VEC, b0 + 1)
                VEC.tensor_reduce(out=mx[:, 8 * half:8 * half + 8],
                                  in_=ps[:, 512 * b0:512 * b0 + 1024].rearrange(
                                      "p (s k) -> p s k", k=128),
                                  axis=AX.X, op=ALU.max)
                s_dv.inc(VEC.max_index(out=mix[:, 8 * half:8 * half + 8],
                                       in_max=mx[:, 8 * half:8 * half + 8],
                                       in_values=ps[:, 512 * b0:512 * b0 + 1024]))
                bankR.setdefault(b0, []).append(s_dv.mark())
                bankR.setdefault(b0 + 1, []).append(s_dv.mark())
            VEC.tensor_tensor(out=gatev[:, par, :], in0=mx[:, 0:8],
                              in1=mx[:, 8:16], op=ALU.add)
            VEC.tensor_scalar(out=gatev[:, par, :], in0=gatev[:, par, :],
                              scalar1=0.0, scalar2=None, op0=ALU.max)
            VEC.tensor_scalar(out=ef[:], in0=mix[:, 0:8], scalar1=128.0,
                              scalar2=None, op0=ALU.mult)
            VEC.tensor_tensor(out=ef[:], in0=ef[:], in1=mix[:, 8:16], op=ALU.add)
            VEC.tensor_tensor(out=ef[:], in0=ef[:], in1=c16512[:], op=ALU.subtract)
            s_dv.inc(VEC.tensor_scalar(out=e16[:], in0=ef[:], scalar1=0,
                                       scalar2=None, op0=ALU.add))
            s_dv.wait(TEN, s_dv.n)
            guard_write(TEN, 5)
            s_tp.inc(TEN.matmul(psbf[0:8, 5120:5248], e16.bitcast(BF16)[:],
                                idn_bf[:], is_transpose=True, start=True,
                                stop=True))
            bankW[5] = s_tp.mark()

        def fidx(b):
            par = b % 2
            s_tp.wait(VEC, s_tp.n)
            wm(VEC, "eT_cons")
            s_dv.inc(VEC.tensor_scalar(out=eT[:],
                                       in0=ps.bitcast(I16)[0:8, 5120:5248],
                                       scalar1=0, scalar2=None, op0=ALU.add))
            bankR.setdefault(5, []).append(s_dv.mark())
            s_dv.wait(SYN, s_dv.n)
            wm(SYN, "w64_cons")
            s_bidx.inc(SYN.dma_start(out=w64[:], in_=eT[:]), 16)
            setm("eT_cons", s_bidx)
            s_bidx.wait(TEN)
            guard_write(TEN, 5)
            s_tp.inc(TEN.matmul(psbf[0:16, 5376:5440], w64.bitcast(BF16)[:],
                                idn_bf[0:64, 0:64], is_transpose=True,
                                start=True, stop=True))
            setm("w64_cons", s_tp)
            bankW[5] = s_tp.mark()
            s_tp.wait(VEC, s_tp.n)
            wm(VEC, "idx_cons%d" % par)
            s_dv.inc(VEC.tensor_scalar(out=idx16[0:16, par, :],
                                       in0=ps.bitcast(I16)[0:16, 5376:5440],
                                       scalar1=0, scalar2=None, op0=ALU.add))
            bankR.setdefault(5, []).append(s_dv.mark())
            s_dv.wait(SYN, s_dv.n)
            s_bidx.inc(SYN.dma_start(out=idx16[16:32, par, :],
                                     in_=idx16[0:16, par, :]), 16)
            setm("idx_prod%d" % par, s_bidx)

        def gather_A(b):
            par = b % 2
            wm(GPS, "idx_prod%d" % par)
            wm(GPS, "inA_free")
            if b == 0:
                emit_conv(16)
                s_conv.wait(GPS)
            s_bg.inc(GPS.dma_gather(out_ap=inA[:], in_ap=ptab_bf[:, 0:1024],
                                    idxs_ap=idx16[:, par, :], num_idxs=1024,
                                    num_idxs_reg=nidx_reg, elem_size=1024,
                                    elem_step=6144, single_packet=False), 16)
            setm("gA_prod", s_bg)

        def gather_B_out0(b):
            par = b % 2
            wm(GPS, "idx_prod%d" % par)
            wm(GPS, "inB_free")
            s_bg.inc(GPS.dma_gather(out_ap=vB, in_ap=ptab_bf[:, 1024:3072],
                                    idxs_ap=idx16[:, par, :], num_idxs=1024,
                                    num_idxs_reg=nidx_reg, elem_size=2048,
                                    elem_step=6144, single_packet=False), 16)
            setm("gB_prod", s_bg)
            wm(GPS, "outbuf_free")
            s_bg.inc(GPS.dma_gather(out_ap=outbuf[:],
                                    in_ap=ptab_bf[:, 3072:6144],
                                    idxs_ap=idx16[:, par, 0:24], num_idxs=384,
                                    num_idxs_reg=nidx_reg2, elem_size=3072,
                                    elem_step=6144, single_packet=False), 16)
            setm("gout0_prod", s_bg)

        def emit_final(b):
            par = b % 2
            guard_read(VEC, 6)
            guard_read(VEC, 7)
            if b >= 1:
                s_out.wait(VEC, 16 * b)
            VEC.tensor_tensor(out=x3t[:], in0=ps[:, 3072:4096], in1=gbtm[:],
                              op=ALU.mult)
            s_dv.inc(VEC.tensor_tensor(out=x3t[:], in0=x3t[:],
                                       in1=x2tm[:, par, :], op=ALU.add))
            bankR.setdefault(6, []).append(s_dv.mark())
            bankR.setdefault(7, []).append(s_dv.mark())
            setm("x2tm_cons%d" % par, s_dv)
            s_dv.wait(SYN, s_dv.n)
            s_out.inc(SYN.dma_start(out=outd[NB * b:NB * b + NB, :], in_=x3t[:]),
                      16)
            setm("x3_store", s_out)

        def back1(b):
            par = b % 2
            wm(VEC, "gA_prod")
            for h in range(8):
                VEC.scalar_tensor_tensor(out=dscr[:, 0:1024],
                                         in0=inA[:, h, :], scalar=1.0,
                                         in1=h2T[:, par, :], op0=ALU.mult,
                                         op1=ALU.mult, accum_out=lin[:, h:h + 1])
            VEC.tensor_tensor(out=ling[:], in0=lin[:], in1=gatev[:, par, :],
                              op=ALU.mult)
            wm(VEC, "dmat_cons")
            for h in range(8):
                iv = VEC.tensor_scalar(out=dmat[:, h, :], in0=idn_bf[:],
                                       scalar1=ling[:, h:h + 1], scalar2=None,
                                       op0=ALU.mult)
            s_dv.inc(iv)
            setm("dmat_prod", s_dv)
            setm("inA_free", s_dv)
            if b >= 1:
                emit_final(b - 1)
            wm(TEN, "dmat_prod")
            wm(TEN, "h2fm_prod%d" % par)
            for js in range(4):
                guard_write(TEN, js)
                for db in range(8):
                    TEN.matmul(bank(js), h2fm[:, par, db, :],
                               winfT[:, db, 512 * js:512 * js + 512],
                               start=(db == 0), stop=False)
            wm(TEN, "gB_prod")
            for js in range(4):
                for h in range(8):
                    mmi = TEN.matmul(bank(js), dmat[:, h, :],
                                     vB[:, h, 512 * js:512 * js + 512],
                                     start=False, stop=(h == 7))
                s_mm.inc(mmi)
                bankW[js] = s_mm.mark()
            setm("h2fm_cons%d" % par, s_mm)
            setm("dmat_cons", s_mm)
            setm("vB_read", s_mm)
            wm(GPS, "vB_read")
            s_bg.inc(GPS.dma_gather(out_ap=vO,
                                    in_ap=ptab_bf[:, 3072:6144],
                                    idxs_ap=idx16[:, par, 24:64], num_idxs=640,
                                    num_idxs_reg=nidx_reg3, elem_size=3072,
                                    elem_step=6144, single_packet=False), 16)
            setm("gout1_prod", s_bg)
            setm("idx_cons%d" % par, s_bg)
            wm(ACT2, "ghT_cons_t")
            wm(ACT2, "ghT_cons_v")
            wm(ACT2, "x3_store")
            for k in range(4):
                guard_read(ACT2, k)
                s_ac.inc(ACT2.activation(out=ghT[:, 512 * k:512 * k + 512],
                                         in_=bank(k), func=AF.Gelu))
                bankR.setdefault(k, []).append(s_ac.mark())
            setm("ghT_prod", s_ac)

        def back2pre(b):
            wm(VEC, "ghT_prod")
            wm(VEC, "gout0_prod")
            for h in range(3):
                VEC.scalar_tensor_tensor(out=dscr[:],
                                         in0=outbuf[:, h, 0:2048], scalar=1.0,
                                         in1=ghT[:], op0=ALU.mult,
                                         op1=ALU.mult, accum_out=lin[:, h:h + 1])
            wm(TEN, "ghT_prod")
            for bki in (6, 7):
                guard_write(TEN, bki)
                for jb in range(8):
                    j = (bki - 6) * 8 + jb
                    mmi = TEN.matmul(
                        psbf[:, 1024 * bki + 128 * jb:1024 * bki + 128 * jb + 128],
                        ghT[:, 128 * j:128 * j + 128], idn_bf[:],
                        is_transpose=True, start=(jb == 0), stop=(jb == 7))
                s_tp.inc(mmi)
                bankW[bki] = s_tp.mark()
            setm("ghT_cons_t", s_tp)

        def back2(b):
            par = b % 2
            wm(VEC, "gout1_prod")
            for h in range(5):
                VEC.scalar_tensor_tensor(out=dscr[:],
                                         in0=vO[:, h, 0:2048], scalar=1.0,
                                         in1=ghT[:], op0=ALU.mult,
                                         op1=ALU.mult,
                                         accum_out=lin[:, 3 + h:4 + h])
            VEC.tensor_tensor(out=ling[:], in0=lin[:], in1=gatev[:, par, :],
                              op=ALU.mult)
            wm(VEC, "dmat_cons")
            for h in range(8):
                iv = VEC.tensor_scalar(out=dmat[:, h, :], in0=idn_bf[:],
                                       scalar1=ling[:, h:h + 1], scalar2=None,
                                       op0=ALU.mult)
            s_dv.inc(iv)
            setm("dmat_prod", s_dv)
            setm("dots_out_done", s_dv)
            setm("ghT_cons_v", s_dv)
            for bki in (6, 7):
                guard_read(VEC, bki)
                s_dv.inc(VEC.tensor_scalar(
                    out=gfm[:, 8 * (bki - 6):8 * (bki - 6) + 8, :],
                    in0=psbf[:, 1024 * bki:1024 * bki + 1024].rearrange(
                        "p (a b) -> p a b", b=128),
                    scalar1=0.0, scalar2=None, op0=ALU.add))
                bankR.setdefault(bki, []).append(s_dv.mark())
            setm("gfm_prod", s_dv)
            wm(TEN, "gfm_prod")
            wm(TEN, "dmat_prod")
            wm(TEN, "gout0_prod")
            wm(TEN, "gout1_prod")
            for js in range(2):
                guard_write(TEN, 6 + js)
                for h in range(3):
                    TEN.matmul(bank(6 + js), dmat[:, h, :],
                               outbuf[:, h, 2048 + 512 * js:
                                      2048 + 512 * js + 512],
                               start=(h == 0), stop=False)
                for h in range(5):
                    mmi = TEN.matmul(bank(6 + js), dmat[:, 3 + h, :],
                                     vO[:, h, 2048 + 512 * js:
                                        2048 + 512 * js + 512],
                                     start=False, stop=False)
            s_mm.inc(mmi)
            setm("outbuf_free", s_mm)
            setm("inB_free", s_mm)
            setm("dmat_cons", s_mm)
            for js in range(2):
                for jb in range(16):
                    mmi = TEN.matmul(bank(6 + js), gfm[:, jb, :],
                                     woutfT[:, jb, 512 * js:512 * js + 512],
                                     start=False, stop=(jb == 15))
                s_mm.inc(mmi)
                bankW[6 + js] = s_mm.mark()

        front(0)
        fidx(0)
        gather_A(0)
        gather_B_out0(0)
        for b in range(TB):
            back1(b)
            if b + 1 < TB:
                front(b + 1)
            back2pre(b)
            if b + 1 < TB:
                fidx(b + 1)
            back2(b)
            if b + 1 < TB:
                gather_A(b + 1)
                gather_B_out0(b + 1)
        emit_final(TB - 1)

        s_out.wait(SYN)
        ctxB.close()

    mybir.codegen_inst_isa_subclasses(nc)
    return nc


_NC_CACHE = {}


def kernel(L0_attn_x, L0_attn_W_in, L0_attn_ln_g, L0_attn_ln_b, L0_attn_W_gate,
           L0_attn_b_gate, L0_attn_eig_raw, L0_attn_W_out, L0_ffn_Wq,
           L0_ffn_keys, L0_ffn_W_in, L0_ffn_W_out, L0_ffn_in_a, L0_ffn_in_b,
           L0_ffn_out_a, L0_ffn_out_b, L0_c, L0_wmod):
    cc = lambda x: np.ascontiguousarray(np.asarray(x, dtype=np.float32))
    ptab = np.ascontiguousarray(np.concatenate(
        [cc(L0_ffn_in_a), cc(L0_ffn_in_b), cc(L0_ffn_out_a), cc(L0_ffn_out_b)],
        axis=1))
    shared = {
        "w_in": cc(L0_attn_W_in), "ln_gd": cc(L0_attn_ln_g).reshape(1, D),
        "ln_bd": cc(L0_attn_ln_b).reshape(1, D),
        "w_gate": cc(L0_attn_W_gate),
        "b_gated": cc(L0_attn_b_gate).reshape(1, 32),
        "eig_d": cc(L0_attn_eig_raw).reshape(1, H16),
        "w_out": cc(L0_attn_W_out), "wq": cc(L0_ffn_Wq),
        "keysd": cc(L0_ffn_keys).reshape(2048, DK),
        "w_inf": cc(L0_ffn_W_in), "w_outf": cc(L0_ffn_W_out), "ptab": ptab,
        "wmod": cc(L0_wmod),
    }
    x = cc(L0_attn_x)
    c = cc(L0_c)
    if "nc" not in _NC_CACHE:
        _NC_CACHE["nc"] = build()
    nc = _NC_CACHE["nc"]
    in_maps = []
    for b in range(NCORES):
        m = dict(shared)
        m["xin"] = np.ascontiguousarray(x[b])
        m["cvec"] = np.ascontiguousarray(c[b:b + 1])
        in_maps.append(m)
    res = run_bass_kernel_spmd(nc, in_maps, core_ids=list(range(NCORES)))
    return np.stack([np.asarray(r["outd"], dtype=np.float32)
                     for r in res.results])


# version 09
# speedup vs baseline: 1.1528x; 1.1528x over previous
import numpy as np
import concourse.bass as bass
import concourse.mybir as mybir
from concourse.bass_utils import run_bass_kernel_spmd
from concourse import library_config
from contextlib import ExitStack

F32 = mybir.dt.float32
F32R = mybir.dt.float32r
BF16 = mybir.dt.bfloat16
I16 = mybir.dt.int16
I32 = mybir.dt.int32
U32 = mybir.dt.uint32
AF = mybir.ActivationFunctionType
ALU = mybir.AluOpType
AX = mybir.AxisListType

T, D = 4096, 1024
H16, CS = 16, 16
HEADS, NK, DK, DI, NE = 8, 128, 512, 2048, 16384
NA, NB = 512, 128
EPS = 1e-5
NCORES = 8



_VEC_IO = {
    "tensor_scalar": (("out",), ("in0", "scalar1", "scalar2")),
    "tensor_tensor": (("out",), ("in0", "in1")),
    "tensor_tensor_scan": (("out",), ("data0", "data1", "initial")),
    "scalar_tensor_tensor": (("out", "accum_out"), ("in0", "scalar", "in1")),
    "tensor_reduce": (("out",), ("in_",)),
    "reduce_max": (("out",), ("in_",)),
    "reduce_sum": (("out",), ("in_",)),
    "reciprocal": (("out",), ("in_",)),
    "memset": (("ap",), ()),
    "tensor_copy": (("dest",), ("src",)),
    "bn_stats": (("out",), ("in_",)),
    "bn_aggr": (("out",), ("in_",)),
    "max_index": (("out",), ("in_max", "in_values")),
    "max": (("out",), ("in_",)),
}
_ACT_IO = {
    "activation": (("out", "accum_out"), ("in_", "bias", "scale")),
    "copy": (("out",), ("in_",)),
}


class EngGuard:

    def __init__(self, eng, iomap):
        self._eng = eng
        self._iomap = iomap
        self._dirty = set()
        import inspect as _insp
        self._sigs = {}
        for m in iomap:
            try:
                self._sigs[m] = _insp.signature(getattr(eng, m))
            except (ValueError, TypeError):
                self._sigs[m] = None

    @staticmethod
    def _name_of(v):
        t = getattr(v, "tensor", None)
        return getattr(t, "name", None)

    def drain(self):
        self._dirty.clear()
        return self._eng.drain()

    def __getattr__(self, name):
        fn = getattr(self._eng, name)
        io = self._iomap.get(name)
        if io is None:
            return fn
        outs_k, ins_k = io
        sig = self._sigs.get(name)

        def wrapped(*args, **kwargs):
            if sig is not None:
                ba = sig.bind_partial(*args, **kwargs)
                argmap = ba.arguments
            else:
                argmap = kwargs
            rd = set()
            for k in ins_k:
                n = self._name_of(argmap.get(k))
                if n:
                    rd.add(n)
            if rd & self._dirty:
                self._eng.drain()
                self._dirty.clear()
            r = fn(*args, **kwargs)
            for k in outs_k:
                n = self._name_of(argmap.get(k))
                if n:
                    self._dirty.add(n)
            return r

        return wrapped


class Sem:
    def __init__(self, nc, ctx, name):
        self.h = ctx.enter_context(nc.semaphore(name))
        self.n = 0

    def inc(self, instr, k=1):
        instr.then_inc(self.h, k)
        self.n += k
        return instr

    def wait(self, eng, at=None):
        eng.wait_ge(self.h, self.n if at is None else at)

    def mark(self):
        return (self, self.n)


def build(n_tokens=T, stages='b', dbg_x2=False):
    alev = {'a1': 1, 'a2': 2, 'a3': 3, 'a4': 4}.get(stages, 5)
    if stages.startswith('a') and stages != 'a':
        stages = 'apart'
    nc = bass.Bass("TRN2", target_bir_lowering=False,
                   detect_race_conditions=False)
    TA = n_tokens // NA
    TB = n_tokens // NB

    dram = lambda name, shape, dt, kind: nc.dram_tensor(name, shape, dt, kind=kind).ap()
    xin = dram("xin", [T, D], F32, "ExternalInput")
    w_in = dram("w_in", [D, D], F32, "ExternalInput")
    ln_gd = dram("ln_gd", [1, D], F32, "ExternalInput")
    ln_bd = dram("ln_bd", [1, D], F32, "ExternalInput")
    w_gate = dram("w_gate", [32, D], F32, "ExternalInput")
    b_gated = dram("b_gated", [1, 32], F32, "ExternalInput")
    eig_d = dram("eig_d", [1, H16], F32, "ExternalInput")
    w_out = dram("w_out", [D, D], F32, "ExternalInput")
    wq = dram("wq", [8192, D], F32, "ExternalInput")
    keysd = dram("keysd", [2048, DK], F32, "ExternalInput")
    w_inf = dram("w_inf", [DI, D], F32, "ExternalInput")
    w_outf = dram("w_outf", [D, DI], F32, "ExternalInput")
    ptab = dram("ptab", [NE, 6144], F32, "ExternalInput")
    cvec = dram("cvec", [1, D], F32, "ExternalInput")
    wmod = dram("wmod", [D, 6144], F32, "ExternalInput")
    outd = dram("outd", [T, D], F32, "ExternalOutput")
    x2buf = dram("x2buf", [D, T], F32,
                 "ExternalOutput" if dbg_x2 else "Internal")
    ptab_bf = dram("ptab_bf", [NE, 6144], BF16, "Internal")
    modscr = dram("modscr", [1, 6144], F32, "Internal")
    widx = dram("widx", [4, NB * HEADS], I16, "Internal")
    if dbg_x2:
        d_xp = dram("d_xp", [T // NA, 128, 8, NA], BF16, "ExternalOutput")
        d_xpln = dram("d_xpln", [T // NA, 128, 8, NA], BF16, "ExternalOutput")
        d_cum = dram("d_cum", [T // NA, 16, NA], BF16, "ExternalOutput")
        d_b = dram("d_b", [T // NA, 128, 8, NA], BF16, "ExternalOutput")
        d_hall = dram("d_hall", [T // NA, 128, 8, NA], BF16, "ExternalOutput")
        d_d0 = dram("d_d0", [128, NA], BF16, "ExternalOutput")
        d_d1 = dram("d_d1", [128, NA], BF16, "ExternalOutput")
        d_tex = dram("d_tex", [128, NA], BF16, "ExternalOutput")
        d_u = dram("d_u", [128, 8, NA], BF16, "ExternalOutput")
        d_mr = dram("d_mr", [16, NA], F32, "ExternalOutput")
        d_cp = dram("d_cp", [16, NA], F32, "ExternalOutput")
        d_bet = dram("d_bet", [16, 4, 128], F32, "ExternalOutput")
        d_wout = dram("d_wout", [128, 8, 8, 128], BF16, "ExternalOutput")
        d_xfm = dram("d_xfm", [128, 8, NA], F32, "ExternalOutput")

    x2buf_v = x2buf.rearrange("(jb p) t -> p jb t", p=128)

    ctx = ExitStack()
    sb = lambda name, shape, dt: ctx.enter_context(nc.sbuf_tensor(name, shape, dt))

    ctxW = ExitStack()
    sw = lambda name, shape, dt: ctxW.enter_context(nc.sbuf_tensor(name, shape, dt, side="right"))
    it32 = sw("it32", [128, 128], I32)
    ehd = sw("ehd", [16, 8, 128], BF16)
    ones_col = sw("ones_col", [1, 128], F32)
    idn_bf = sb("idn_bf", [128, 128], BF16)
    idn_f32 = sb("idn_f32", [128, 128], F32)
    ones_bf = sb("ones_bf", [128, 128], BF16)
    c16512 = sb("c16512", [128, 8], I32)
    eps128 = sb("eps128", [128, 1], F32)
    ca1_fm = sb("ca1_fm", [128, 8], F32)
    sa_bf = sb("sa_bf", [128, 8], BF16)
    ga_fm = sb("ga_fm", [128, 8], F32)
    lng_fm = sb("lng_fm", [128, 8], F32)
    lnb_fm = sb("lnb_fm", [128, 8], F32)
    bgA = sb("bgA", [16, 1], F32)
    bgB = sb("bgB", [16, 1], F32)
    eig_v = sb("eig_v", [16, 1], F32)
    cbtm = sb("cbtm", [128, D], BF16)
    sbtm = sb("sbtm", [128, D], BF16)
    gbtm = sb("gbtm", [128, D], F32)

    ps = ctx.enter_context(nc.psum_tensor("ps", [128, 4096], F32))
    psbf = ps.bitcast(BF16)

    def bank(i, cols=512):
        return ps[:, 512 * i: 512 * i + cols]

    S = lambda name: Sem(nc, ctx, name)
    sem_names = ["s_ld", "s_ldg", "s_tp", "s_ev", "s_mm", "s_dv", "s_ac",
                 "s_x", "s_x2", "s_bx", "s_bidx", "s_bg", "s_gf", "s_out",
                 "s_ph", "s_gp0", "s_conv"]
    SM = {n: S(n) for n in sem_names}
    s_ld, s_ldg, s_tp, s_ev, s_mm, s_dv, s_ac = (SM[k] for k in sem_names[:7])
    (s_x, s_x2, s_bx, s_bidx, s_bg, s_gf, s_out, s_ph,
     s_gp0, s_conv) = (SM[k] for k in sem_names[7:])

    bankW = {}
    bankR = {}

    def guard_write(eng, i):
        for m in bankR.get(i, []):
            m[0].wait(eng, m[1])
        bankR[i] = []

    def guard_read(eng, i):
        m = bankW.get(i)
        if m is not None:
            m[0].wait(eng, m[1])

    TEN, GPS, SYN = nc.tensor, nc.gpsimd, nc.sync
    VEC = EngGuard(nc.vector, _VEC_IO)
    ACT2 = EngGuard(nc.scalar, _ACT_IO)

    def copy_ts(eng, out, in0, sem, k=1, dst=None):
        return sem.inc(eng.tensor_scalar(out=out, in0=in0, scalar1=0.0,
                                         scalar2=None, op0=ALU.add), k)

    block = ctx.enter_context(nc.Block())

    with ctx:
        GPS.iota(it32[:], pattern=[[1, 128]], base=0, channel_multiplier=-1)
        s_gp0.inc(GPS.iota(c16512[:], pattern=[[16512, 8]], base=0,
                           channel_multiplier=0))
        s_gp0.wait(VEC)
        VEC.tensor_scalar(out=idn_bf[:], in0=it32[:], scalar1=0.0,
                          scalar2=None, op0=ALU.is_equal)
        VEC.tensor_scalar(out=idn_f32[:], in0=it32[:], scalar1=0.0,
                          scalar2=None, op0=ALU.is_equal)
        VEC.memset(ones_bf[:], 1.0)
        VEC.memset(eps128[:], float(EPS))
        s_dv.inc(VEC.memset(ones_col[:], 1.0))
        s_dv.wait(GPS)
        for c in range(8):
            s_gp0.inc(GPS.iota(it32[0:16, 0:128], pattern=[[1, 2], [0, 64]],
                               base=2 * c, channel_multiplier=-1))
            s_gp0.wait(VEC)
            s_dv.inc(VEC.tensor_scalar(out=ehd[:, c, :], in0=it32[0:16, 0:128],
                                       scalar1=0.0, scalar2=None,
                                       op0=ALU.is_equal))
            s_dv.wait(GPS)
        GPS.load_library(library_config.mlp)
        conv_emitted = [0]

        def emit_conv(k):
            for _ in range(k):
                rb = conv_emitted[0]
                if rb >= 16:
                    return
                conv_emitted[0] += 1
                s_conv.inc(GPS.dma_start(
                    out=ptab_bf[1024 * rb:1024 * rb + 1024, :],
                    in_=ptab[1024 * rb:1024 * rb + 1024, :],
                    max_dma_last_dim=8192), 16)

        ctx0 = ExitStack()
        crow = ctx0.enter_context(nc.sbuf_tensor("crow", [8, 128], F32, side="right"))
        wmod_sb = ctx0.enter_context(nc.sbuf_tensor("wmod_sb", [128, 8, 6144], BF16, side="right"))
        modrow = ctx0.enter_context(nc.sbuf_tensor("modrow", [1, 6144], F32, side="right"))
        scfm_bf = ctx0.enter_context(nc.sbuf_tensor("scfm_bf", [128, 8], BF16, side="right"))

        s_ld.inc(SYN.dma_start(out=crow[:],
                               in_=cvec.rearrange("o (p f) -> (o p) f", p=8)), 16)
        s_ld.wait(ACT2)
        s_ac.inc(ACT2.activation(out=crow[:], in_=crow[:], func=AF.Silu))
        s_ac.wait(TEN)
        s_dv.wait(TEN)
        s_tp.inc(TEN.matmul(bank(6)[:, 0:8], crow[:], idn_f32[0:8, 0:8],
                            is_transpose=True, start=True, stop=True))
        bankW[6] = s_tp.mark()
        s_tp.wait(VEC)
        s_dv.inc(VEC.tensor_scalar(out=scfm_bf[:], in0=bank(6)[:, 0:8],
                                   scalar1=0.0, scalar2=None, op0=ALU.add))
        bankR.setdefault(6, []).append(s_dv.mark())

        for r in range(8):
            s_ldg.inc(GPS.dma_start(out=wmod_sb[:, r, :],
                                    in_=wmod[128 * r:128 * r + 128, :],
                                    max_dma_last_dim=8192), 16)
        s_ldg.wait(TEN)
        s_dv.wait(TEN)
        for s_i in range(12):
            bki = s_i % 2
            guard_write(TEN, bki)
            for r in range(8):
                mmi = TEN.matmul(bank(bki)[0:1, :], scfm_bf[:, r:r + 1],
                                 wmod_sb[:, r, 512 * s_i:512 * s_i + 512],
                                 start=(r == 0), stop=(r == 7))
            s_mm.inc(mmi)
            bankW[bki] = s_mm.mark()
            guard_read(VEC, bki)
            s_dv.inc(VEC.tensor_scalar(out=modrow[:, 512 * s_i:512 * s_i + 512],
                                       in0=bank(bki)[0:1, :], scalar1=0.0,
                                       scalar2=None, op0=ALU.add))
            bankR.setdefault(bki, []).append(s_dv.mark())
        s_dv.wait(SYN)
        s_ld.inc(SYN.dma_start(out=modscr, in_=modrow[:]), 16)
        s_ld.wait(SYN)
        s_ld.wait(GPS)
        ctx0.close()

        ctx0 = ExitStack()
        vrow = ctx0.enter_context(nc.sbuf_tensor("vrow", [8, 128], F32, side="right"))
        rowt = ctx0.enter_context(nc.sbuf_tensor("rowt", [1, 2, D], F32, side="right"))
        tmp8 = ctx0.enter_context(nc.sbuf_tensor("tmp8", [128, 4, 8], F32, side="right"))

        def vec_to_fm(src_ap, post):
            s_dv.wait(SYN)
            s_ld.inc(SYN.dma_start(out=vrow[:],
                                   in_=src_ap.rearrange("o (p f) -> (o p) f", p=8)), 16)
            s_ld.wait(TEN)
            guard_write(TEN, 7)
            s_tp.inc(TEN.matmul(bank(7)[:, 0:8], vrow[:], idn_f32[0:8, 0:8],
                                is_transpose=True, start=True, stop=True))
            bankW[7] = s_tp.mark()
            s_tp.wait(VEC)
            post(bank(7)[:, 0:8])
            bankR.setdefault(7, []).append(s_dv.mark())

        vec_to_fm(modscr[:, 0:D],
                  lambda p: copy_ts(VEC, sa_bf[:], p, s_dv))
        vec_to_fm(modscr[:, D:2 * D],
                  lambda p: s_dv.inc(VEC.tensor_scalar(out=ca1_fm[:], in0=p,
                                                       scalar1=1.0, scalar2=None,
                                                       op0=ALU.add)))
        vec_to_fm(ln_gd, lambda p: copy_ts(VEC, lng_fm[:], p, s_dv))
        vec_to_fm(ln_bd, lambda p: copy_ts(VEC, lnb_fm[:], p, s_dv))

        def ga_post(p):
            g0 = tmp8[:, 0, :]
            VEC.tensor_scalar(out=g0, in0=p, scalar1=0.0, scalar2=None, op0=ALU.add)
            VEC.tensor_scalar(out=tmp8[:, 1, :], in0=g0, scalar1=-1.0,
                              scalar2=None, op0=ALU.mult)
            VEC.tensor_tensor(out=tmp8[:, 1, :], in0=g0, in1=tmp8[:, 1, :], op=ALU.max)
            VEC.tensor_scalar(out=tmp8[:, 1, :], in0=tmp8[:, 1, :], scalar1=0.25,
                              scalar2=None, op0=ALU.max)
            VEC.tensor_scalar(out=tmp8[:, 2, :], in0=g0, scalar1=0.0, scalar2=2.0,
                              op0=ALU.is_ge, op1=ALU.mult)
            VEC.tensor_scalar(out=tmp8[:, 2, :], in0=tmp8[:, 2, :], scalar1=-1.0,
                              scalar2=None, op0=ALU.add)
            s_dv.inc(VEC.tensor_tensor(out=ga_fm[:], in0=tmp8[:, 1, :],
                                       in1=tmp8[:, 2, :], op=ALU.mult))

        vec_to_fm(modscr[:, 2 * D:3 * D], ga_post)

        def row_to_tm(src_ap, dst, mode):
            s_dv.wait(SYN)
            s_ld.inc(SYN.dma_start(out=rowt[:, 0, :], in_=src_ap), 16)
            s_ld.wait(VEC)
            r = rowt[:, 0, :]
            if mode == "add1":
                VEC.tensor_scalar(out=r, in0=r, scalar1=1.0, scalar2=None,
                                  op0=ALU.add)
            elif mode == "clamp":
                r2 = rowt[:, 1, :]
                VEC.tensor_scalar(out=r2, in0=r, scalar1=-1.0, scalar2=None,
                                  op0=ALU.mult)
                VEC.tensor_tensor(out=r2, in0=r, in1=r2, op=ALU.max)
                VEC.tensor_scalar(out=r2, in0=r2, scalar1=0.25, scalar2=None,
                                  op0=ALU.max)
                VEC.tensor_scalar(out=r, in0=r, scalar1=0.0, scalar2=2.0,
                                  op0=ALU.is_ge, op1=ALU.mult)
                VEC.tensor_scalar(out=r, in0=r, scalar1=-1.0, scalar2=None,
                                  op0=ALU.add)
                VEC.tensor_tensor(out=r, in0=r, in1=r2, op=ALU.mult)
            s_dv.inc(VEC.nop())
            s_dv.wait(TEN)
            for h in range(2):
                guard_write(TEN, 7)
                s_mm.inc(TEN.matmul(bank(7), ones_col[:],
                                    rowt[:, 0, 512 * h:512 * h + 512],
                                    start=True, stop=True))
                bankW[7] = s_mm.mark()
                s_mm.wait(VEC)
                copy_ts(VEC, dst[:, 512 * h:512 * h + 512], bank(7), s_dv)
                bankR.setdefault(7, []).append(s_dv.mark())

        row_to_tm(modscr[:, 4 * D:5 * D], cbtm, "add1")
        row_to_tm(modscr[:, 3 * D:4 * D], sbtm, "copy")
        row_to_tm(modscr[:, 5 * D:6 * D], gbtm, "clamp")

        s_dv.wait(SYN)
        s_ld.inc(SYN.dma_start(out=rowt[0:1, 0, 0:32], in_=b_gated), 16)
        s_ld.inc(SYN.dma_start(out=rowt[0:1, 1, 0:16], in_=eig_d), 16)
        s_ld.wait(TEN)
        guard_write(TEN, 7)
        TEN.matmul(bank(7)[0:16, 0:1], rowt[0:1, 0, 0:16], idn_f32[0:1, 0:1],
                   is_transpose=True, start=True, stop=False)
        TEN.matmul(bank(7)[0:16, 1:2], rowt[0:1, 0, 16:32], idn_f32[0:1, 0:1],
                   is_transpose=True, start=False, stop=False)
        s_tp.inc(TEN.matmul(bank(7)[0:16, 2:3], rowt[0:1, 1, 0:16],
                            idn_f32[0:1, 0:1], is_transpose=True, start=False,
                            stop=True))
        bankW[7] = s_tp.mark()
        s_tp.wait(VEC)
        copy_ts(VEC, bgA[:], bank(7)[0:16, 0:1], s_dv)
        copy_ts(VEC, bgB[:], bank(7)[0:16, 1:2], s_dv)
        copy_ts(VEC, eig_v[:], bank(7)[0:16, 2:3], s_dv)
        bankR.setdefault(7, []).append(s_dv.mark())
        s_dv.wait(ACT2)
        s_ac.inc(ACT2.activation(out=eig_v[:], in_=eig_v[:], func=AF.Tanh))
        ctx0.close()

        wp_sb = ctxW.enter_context(nc.sbuf_tensor("wp_sb", [128, 8, 8, 128], BF16, side="right"))
        woutT = ctxW.enter_context(nc.sbuf_tensor("woutT", [128, 8, 8, 128], BF16, side="right"))
        wgT = ctxW.enter_context(nc.sbuf_tensor("wgT", [128, 8, 32], BF16, side="right"))
        augw1 = ctxW.enter_context(nc.sbuf_tensor("augw1", [1, 8, 128], BF16, side="right"))
        augw2 = ctxW.enter_context(nc.sbuf_tensor("augw2", [1, 8, 128], BF16, side="right"))
        ctxWt = ExitStack()
        wrow = ctxWt.enter_context(nc.sbuf_tensor("wrow", [128, 2, D], BF16, side="right"))
        wubrow = ctxWt.enter_context(nc.sbuf_tensor("wubrow", [128, 2, 8, 128], BF16, side="right"))
        wcT = ctxWt.enter_context(nc.sbuf_tensor("wcT", [16, 128], BF16, side="right"))
        wgrow = ctxWt.enter_context(nc.sbuf_tensor("wgrow", [32, D], BF16, side="right"))
        w1f = ctxWt.enter_context(nc.sbuf_tensor("w1f", [128, 16], BF16, side="right"))
        wrow_guard = [None, None]
        wub_guard = [None, None]

        def tr_row(jb, src_ap):
            par = jb % 2
            bki = par
            if wrow_guard[par] is not None:
                m = wrow_guard[par]
                m[0].wait(GPS, m[1])
            s_ldg.inc(GPS.dma_start(out=wrow[:, par, :], in_=src_ap,
                                    max_dma_last_dim=8192), 16)
            s_ldg.wait(TEN)
            guard_write(TEN, bki)
            for q in range(8):
                mmi = TEN.matmul(psbf[:, 1024 * bki + 128 * q:
                                      1024 * bki + 128 * q + 128],
                                 wrow[:, par, 128 * q:128 * q + 128], idn_bf[:],
                                 is_transpose=True, start=(q == 0),
                                 stop=(q == 7))
            s_tp.inc(mmi)
            wrow_guard[par] = s_tp.mark()
            bankW[bki] = s_tp.mark()
            guard_read(VEC, bki)
            return bki, psbf[:, 1024 * bki:1024 * bki + 1024].rearrange(
                "p (a b) -> p a b", b=128)

        for jb in range(8):
            par = jb % 2
            bki, src3 = tr_row(jb, w_in[128 * jb:128 * jb + 128, :])
            for db in range(8):
                VEC.tensor_scalar(out=wp_sb[:, db, jb, :], in0=src3[:, db, :],
                                  scalar1=ca1_fm[:, db:db + 1],
                                  scalar2=None, op0=ALU.mult)
            if wub_guard[par] is not None:
                m = wub_guard[par]
                m[0].wait(VEC, m[1])
            s_dv.inc(VEC.tensor_scalar(out=wubrow[:, par, :, :], in0=src3,
                                       scalar1=0.0, scalar2=None, op0=ALU.add))
            bankR.setdefault(bki, []).append(s_dv.mark())
            s_dv.wait(TEN)
            if jb == 0:
                guard_write(TEN, 4)
            for db in range(8):
                mmi = TEN.matmul(ps[:, 2048 + jb:2049 + jb],
                                 wubrow[:, par, db, :], sa_bf[:, db:db + 1],
                                 start=(jb == 0 and db == 0),
                                 stop=(jb == 7 and db == 7))
            s_mm.inc(mmi)
            wub_guard[par] = s_mm.mark()
        bankW[4] = s_mm.mark()

        guard_write(TEN, 5)
        for jb in range(8):
            for db in range(8):
                mmi = TEN.matmul(ps[:, 2560 + jb:2561 + jb], wp_sb[:, db, jb, :],
                                 ones_bf[:, 0:1],
                                 start=(jb == 0 and db == 0),
                                 stop=(jb == 7 and db == 7))
            s_mm.inc(mmi)
        bankW[5] = s_mm.mark()
        guard_read(VEC, 4)
        guard_read(VEC, 5)
        copy_ts(VEC, w1f[:, 8:16], ps[:, 2048:2056], s_dv)
        copy_ts(VEC, w1f[:, 0:8], ps[:, 2560:2568], s_dv)
        bankR.setdefault(4, []).append(s_dv.mark())
        bankR.setdefault(5, []).append(s_dv.mark())
        s_dv.wait(TEN)
        guard_write(TEN, 6)
        s_tp.inc(TEN.matmul(psbf[0:16, 6144:6272], w1f[:], idn_bf[:],
                            is_transpose=True, start=True, stop=True))
        bankW[6] = s_tp.mark()
        s_tp.wait(VEC)
        copy_ts(VEC, wcT[:], psbf[0:16, 6144:6272], s_dv)
        bankR.setdefault(6, []).append(s_dv.mark())
        s_dv.wait(SYN)
        s_ld.inc(SYN.dma_start(out=augw1[:], in_=wcT[0:8, :]), 16)
        s_ld.inc(SYN.dma_start(out=augw2[:], in_=wcT[8:16, :]), 16)
        s_ld.wait(SYN)

        for jb in range(8):
            bki, src3 = tr_row(jb, w_out[128 * jb:128 * jb + 128, :])
            s_dv.inc(VEC.tensor_scalar(out=woutT[:, :, jb, :], in0=src3,
                                       scalar1=0.0, scalar2=None, op0=ALU.add))
            bankR.setdefault(bki, []).append(s_dv.mark())

        s_ldg.inc(GPS.dma_start(out=wgrow[:], in_=w_gate,
                                max_dma_last_dim=8192), 16)
        s_ldg.wait(TEN)
        for db in range(8):
            guard_write(TEN, 6)
            s_tp.inc(TEN.matmul(psbf[:, 6144:6176],
                                wgrow[:, 128 * db:128 * db + 128],
                                idn_bf[0:32, 0:32], is_transpose=True,
                                start=True, stop=True))
            bankW[6] = s_tp.mark()
            s_tp.wait(VEC)
            copy_ts(VEC, wgT[:, db, :], psbf[:, 6144:6176], s_dv)
            bankR.setdefault(6, []).append(s_dv.mark())

        def barrier():
            tgt = s_ph.n + 5
            for eng in (GPS, VEC, ACT2, TEN, SYN):
                s_ph.inc(eng.nop())
            for eng in (GPS, VEC, ACT2, TEN, SYN):
                eng.wait_ge(s_ph.h, tgt)
            bankW.clear()
            bankR.clear()

        barrier()
        ctxWt.close()
        if stages == 'p0':
            s_ld.wait(SYN)
            s_ldg.wait(GPS)
            s_conv.wait(GPS)
            mybir.codegen_inst_isa_subclasses(nc)
            return nc

        ctxA = ExitStack()
        sbA = lambda name, shape, dt: ctxA.enter_context(nc.sbuf_tensor(name, shape, dt, side="right"))
        x_tm = sbA("x_tm", [128, 2, 4, D], F32)
        xfm32 = sbA("xfm32", [128, 8, NA], F32)
        xfm_bf = sbA("xfm_bf", [128, 8, NA], BF16)
        x2_bf = sbA("x2_bf", [128, 8, NA], BF16)
        xp_bf = sbA("xp_bf", [128, 8, NA], BF16)
        xpln = sbA("xpln", [128, 8, NA], BF16)
        hall = sbA("hall", [128, 8, NA], BF16)
        b_bf = sbA("b_bf", [128, 8, NA], BF16)
        cb_bf = sbA("cb_bf", [128, 8, NA], BF16)
        u_bf = sbA("u_bf", [128, 8, NA], BF16)
        d0t = sbA("d0t", [128, NA], BF16)
        d1t = sbA("d1t", [128, NA], BF16)
        texq = sbA("texq", [128, NA], BF16)
        t5q = sbA("t5q", [128, NA], BF16)
        mu_t = sbA("mu_t", [128, NA], F32)
        ms_t = sbA("ms_t", [128, NA], F32)
        rs_t = sbA("rs_t", [128, NA], F32)
        sq_t = sbA("sq_t", [128, NA], F32)
        sdev_bf = sbA("sdev_bf", [128, NA], BF16)
        negmu_bf = sbA("negmu_bf", [128, NA], BF16)
        rs2_t = sbA("rs2_t", [128, NA], F32)
        m2r_t = sbA("m2r_t", [128, NA], F32)
        gtm = sbA("gtm", [128, 32], F32)
        alph = sbA("alph", [16, 4, 128], F32)
        bet = sbA("bet", [16, 4, 128], F32)
        a_t = sbA("a_t", [16, NA], F32)
        cum_t = sbA("cum_t", [16, NA], F32)
        cp_t = sbA("cp_t", [16, NA], F32)
        msk_t = sbA("msk_t", [16, NA], F32)
        mr_t = sbA("mr_t", [16, NA], F32)
        cum_bf = sbA("cum_bf", [16, NA], BF16)
        mrb_bf = sbA("mrb_bf", [16, NA], BF16)
        beta_bf = sbA("beta_bf", [16, NA], BF16)
        x2st = sbA("x2st", [128, 2, NA], F32)
        carry_cb = sbA("carry_cb", [128, 8], BF16)
        carry_b = sbA("carry_b", [128, 8], BF16)
        scanst = sbA("scanst", [128, 8], BF16)

        cum3 = cum_t.rearrange("p (n c) -> p n c", c=CS)
        cp3 = cp_t.rearrange("p (n c) -> p n c", c=CS)

        n_x2_stores = 0
        for a in range(TA):
            par = a % 2
            if a >= 2:
                s_tp.wait(SYN, s_tp.n)
            s_x.inc(SYN.dma_start(
                out=x_tm[:, par, :, :],
                in_=xin[NA * a:NA * a + NA, :].rearrange("(s p) d -> p s d", p=128)), 16)
            emit_conv((16 + TA - 1) // TA)

            s_x.wait(TEN)
            if a > 0:
                s_dv.wait(TEN, s_dv.n)
            for grp in range(8):
                s_i, hf = grp // 2, grp % 2
                bki = grp % 2
                guard_write(TEN, bki)
                for q in range(4):
                    db = hf * 4 + q
                    mmi = TEN.matmul(bank(bki)[:, 128 * q:128 * q + 128],
                                     x_tm[:, par, s_i, 128 * db:128 * db + 128],
                                     idn_f32[:], is_transpose=True,
                                     start=(q == 0), stop=(q == 3))
                s_tp.inc(mmi)
                bankW[bki] = s_tp.mark()
                src3 = bank(bki).rearrange("p (a b) -> p a b", b=128)
                guard_read(ACT2, bki)
                dst32 = xfm32[:, 4 * hf:4 * hf + 4, 128 * s_i:128 * s_i + 128]
                s_ac.inc(ACT2.activation(out=dst32, in_=src3, func=AF.Copy,
                                         bias=0.0))
                bankR.setdefault(bki, []).append(s_ac.mark())
                s_ac.wait(VEC)
                dstbf = xfm_bf[:, 4 * hf:4 * hf + 4, 128 * s_i:128 * s_i + 128]
                s_dv.inc(VEC.tensor_scalar(out=dstbf, in0=dst32, scalar1=0.0,
                                           scalar2=None, op0=ALU.add))

            if alev < 2:
                continue
            s_ac.wait(ACT2)
            for db in range(8):
                ii = ACT2.activation(out=x2_bf[:, db, :], in_=xfm32[:, db, :],
                                     func=AF.Square)
            s_ac.inc(ii)

            s_dv.wait(TEN)
            s_ac.wait(TEN)
            guard_write(TEN, 4)
            for db in range(8):
                m1 = TEN.matmul(bank(4), ones_bf[:], xfm_bf[:, db, :],
                                start=(db == 0), stop=(db == 7))
            s_mm.inc(m1)
            bankW[4] = s_mm.mark()
            guard_write(TEN, 5)
            for db in range(8):
                m2 = TEN.matmul(bank(5), ones_bf[:], x2_bf[:, db, :],
                                start=(db == 0), stop=(db == 7))
            s_mm.inc(m2)
            bankW[5] = s_mm.mark()

            def stats_chain(rs_dst):
                guard_read(VEC, 4)
                guard_read(VEC, 5)
                VEC.tensor_scalar(out=mu_t[:], in0=bank(4), scalar1=1.0 / D,
                                  scalar2=None, op0=ALU.mult)
                s_dv.inc(VEC.tensor_scalar(out=ms_t[:], in0=bank(5),
                                           scalar1=1.0 / D, scalar2=None,
                                           op0=ALU.mult))
                bankR.setdefault(4, []).append(s_dv.mark())
                bankR.setdefault(5, []).append(s_dv.mark())
                s_dv.wait(ACT2)
                s_ac.inc(ACT2.activation(out=sq_t[:], in_=mu_t[:], func=AF.Square))
                s_ac.wait(VEC)
                s_dv.inc(VEC.tensor_tensor(out=ms_t[:], in0=ms_t[:], in1=sq_t[:],
                                           op=ALU.subtract))
                s_dv.wait(ACT2)
                s_ac.inc(ACT2.activation(out=ms_t[:], in_=ms_t[:], func=AF.Sqrt,
                                         bias=eps128[:]))
                s_ac.wait(VEC)
                s_dv.inc(VEC.reciprocal(out=rs_dst[:], in_=ms_t[:]))

            stats_chain(rs_t)
            VEC.tensor_scalar(out=sdev_bf[:], in0=ms_t[:], scalar1=0.0,
                              scalar2=None, op0=ALU.add)
            s_dv.inc(VEC.tensor_scalar(out=negmu_bf[:], in0=mu_t[:],
                                       scalar1=-1.0, scalar2=None, op0=ALU.mult))

            s_dv.wait(TEN)
            for jb in range(8):
                bki = 2 + jb % 2
                guard_write(TEN, bki)
                for db in range(8):
                    TEN.matmul(bank(bki), wp_sb[:, db, jb, :], xfm_bf[:, db, :],
                               start=(db == 0), stop=False)
                TEN.matmul(bank(bki), augw1[0:1, jb, :], negmu_bf[0:1, :],
                           start=False, stop=False)
                s_mm.inc(TEN.matmul(bank(bki), augw2[0:1, jb, :],
                                    sdev_bf[0:1, :], start=False, stop=True))
                bankW[bki] = s_mm.mark()
                guard_read(VEC, bki)
                s_dv.inc(VEC.tensor_tensor(out=xp_bf[:, jb, :], in0=bank(bki),
                                           in1=rs_t[:], op=ALU.mult))
                bankR.setdefault(bki, []).append(s_dv.mark())

            if alev < 3:
                continue
            s_dv.wait(ACT2)
            for db in range(8):
                ii = ACT2.activation(out=x2_bf[:, db, :], in_=xp_bf[:, db, :],
                                     func=AF.Square)
            s_ac.inc(ii)
            s_ac.wait(TEN)
            s_dv.wait(TEN)
            guard_write(TEN, 4)
            for db in range(8):
                m1 = TEN.matmul(bank(4), ones_bf[:], xp_bf[:, db, :],
                                start=(db == 0), stop=(db == 7))
            s_mm.inc(m1)
            bankW[4] = s_mm.mark()
            guard_write(TEN, 5)
            for db in range(8):
                m2 = TEN.matmul(bank(5), ones_bf[:], x2_bf[:, db, :],
                                start=(db == 0), stop=(db == 7))
            s_mm.inc(m2)
            bankW[5] = s_mm.mark()
            stats_chain(rs2_t)
            VEC.tensor_scalar(out=m2r_t[:], in0=mu_t[:], scalar1=-1.0,
                              scalar2=None, op0=ALU.mult)
            s_dv.inc(VEC.tensor_tensor(out=m2r_t[:], in0=m2r_t[:], in1=rs2_t[:],
                                       op=ALU.mult))

            for jb in range(8):
                VEC.tensor_tensor(out=xpln[:, jb, :], in0=xp_bf[:, jb, :],
                                  in1=rs2_t[:], op=ALU.mult)
                VEC.tensor_tensor(out=xpln[:, jb, :], in0=xpln[:, jb, :],
                                  in1=m2r_t[:], op=ALU.add)
                iv = VEC.tensor_scalar(out=xpln[:, jb, :], in0=xpln[:, jb, :],
                                       scalar1=lng_fm[:, jb:jb + 1],
                                       scalar2=lnb_fm[:, jb:jb + 1],
                                       op0=ALU.mult, op1=ALU.add)
            s_dv.inc(iv)

            for s_i in range(4):
                s_dv.wait(TEN)
                guard_write(TEN, 6)
                for db in range(8):
                    mg = TEN.matmul(ps[:, 3072:3104],
                                    xpln[:, db, 128 * s_i:128 * s_i + 128],
                                    wgT[:, db, :], start=(db == 0),
                                    stop=(db == 7))
                s_mm.inc(mg)
                bankW[6] = s_mm.mark()
                guard_read(VEC, 6)
                copy_ts(VEC, gtm[:], ps[:, 3072:3104], s_dv)
                bankR.setdefault(6, []).append(s_dv.mark())
                s_dv.wait(TEN)
                TEN.matmul(ps[0:16, 3104:3232], gtm[:, 0:16], idn_f32[:],
                           is_transpose=True, start=True, stop=False)
                s_mm.inc(TEN.matmul(ps[0:16, 3232:3360], gtm[:, 16:32],
                                    idn_f32[:], is_transpose=True, start=False,
                                    stop=True))
                bankW[6] = s_mm.mark()
                guard_read(ACT2, 6)
                ACT2.activation(out=alph[:, s_i, :], in_=ps[0:16, 3104:3232],
                                func=AF.Sigmoid, bias=bgA[:])
                s_ac.inc(ACT2.activation(out=bet[:, s_i, :],
                                         in_=ps[0:16, 3232:3360],
                                         func=AF.Sigmoid, bias=bgB[:]))
                bankR.setdefault(6, []).append(s_ac.mark())

            if alev < 4:
                continue
            s_ac.wait(VEC)
            av = alph.rearrange("p s c -> p (s c)")
            bv = bet.rearrange("p s c -> p (s c)")
            VEC.tensor_scalar(out=a_t[:], in0=av, scalar1=eig_v[:],
                              scalar2=None, op0=ALU.mult)
            VEC.tensor_scalar(out=cum_t[:], in0=a_t[:], scalar1=0.0,
                              scalar2=None, op0=ALU.add)
            for i in range(1, CS):
                VEC.tensor_tensor(out=cum3[:, :, i], in0=cum3[:, :, i],
                                  in1=cum3[:, :, i - 1], op=ALU.mult)
            VEC.tensor_scalar(out=cp_t[:, 1:NA], in0=cum_t[:, 0:NA - 1],
                              scalar1=0.0, scalar2=None, op0=ALU.add)
            s_dv.inc(VEC.memset(cp3[:, :, 0], 1.0))
            s_dv.wait(ACT2)
            s_ac.inc(ACT2.activation(out=msk_t[:], in_=cp_t[:], func=AF.Abs))
            s_ac.wait(VEC)
            VEC.tensor_scalar(out=msk_t[:], in0=msk_t[:], scalar1=1e-8,
                              scalar2=None, op0=ALU.is_gt)
            VEC.tensor_scalar(out=mr_t[:], in0=msk_t[:], scalar1=-1.0,
                              scalar2=1.0, op0=ALU.mult, op1=ALU.add)
            VEC.tensor_tensor(out=cp_t[:], in0=cp_t[:], in1=mr_t[:], op=ALU.add)
            VEC.reciprocal(out=mr_t[:], in_=cp_t[:])
            VEC.tensor_tensor(out=mr_t[:], in0=mr_t[:], in1=msk_t[:], op=ALU.mult)
            VEC.tensor_scalar(out=beta_bf[:], in0=bv, scalar1=0.0,
                              scalar2=None, op0=ALU.add)
            VEC.tensor_tensor(out=mrb_bf[:], in0=mr_t[:], in1=bv, op=ALU.mult)
            s_dv.inc(VEC.tensor_scalar(out=cum_bf[:], in0=cum_t[:], scalar1=0.0,
                                       scalar2=None, op0=ALU.add))

            s_dv.wait(TEN)
            for db in range(8):
                bki = 2 + db % 2
                for srcv, dstv, isb in ((beta_bf, b_bf, True),
                                        (mrb_bf, u_bf, True),
                                        (cum_bf, cb_bf, False)):
                    guard_write(TEN, bki)
                    s_mm.inc(TEN.matmul(bank(bki), ehd[:, db, :], srcv[:],
                                        start=True, stop=True))
                    bankW[bki] = s_mm.mark()
                    guard_read(VEC, bki)
                    if isb:
                        s_dv.inc(VEC.tensor_tensor(out=dstv[:, db, :],
                                                   in0=bank(bki),
                                                   in1=xpln[:, db, :],
                                                   op=ALU.mult))
                    else:
                        copy_ts(VEC, dstv[:, db, :], bank(bki), s_dv)
                    bankR.setdefault(bki, []).append(s_dv.mark())

            for db in range(8):
                d03 = d0t.rearrange("p (n c) -> p n c", c=CS)
                d13 = d1t.rearrange("p (n c) -> p n c", c=CS)
                cbv = cb_bf[:, db, :]
                cb3 = cbv.rearrange("p (n c) -> p n c", c=CS)
                bv3 = b_bf[:, db, :].rearrange("p (n c) -> p n c", c=CS)
                VEC.memset(d0t[:], 1.0)
                VEC.tensor_scalar(out=d03[:, 1:NA // CS, 0],
                                  in0=cb3[:, 0:NA // CS - 1, CS - 1],
                                  scalar1=0.0, scalar2=None, op0=ALU.add)
                VEC.tensor_scalar(out=d1t[:, 1:NA], in0=u_bf[:, db, 0:NA - 1],
                                  scalar1=0.0, scalar2=None, op0=ALU.add)
                VEC.tensor_scalar(out=d13[:, 1:NA // CS, 0],
                                  in0=bv3[:, 0:NA // CS - 1, CS - 1],
                                  scalar1=0.0, scalar2=None, op0=ALU.add)
                if a == 0:
                    VEC.memset(d0t[:, 0:1], 0.0)
                    VEC.memset(d1t[:, 0:1], 0.0)
                else:
                    VEC.tensor_scalar(out=d0t[:, 0:1], in0=carry_cb[:, db:db + 1],
                                      scalar1=0.0, scalar2=None, op0=ALU.add)
                    VEC.tensor_scalar(out=d1t[:, 0:1], in0=carry_b[:, db:db + 1],
                                      scalar1=0.0, scalar2=None, op0=ALU.add)
                if dbg_x2 and a == 0 and db == 0:
                    s_dv.inc(VEC.nop())
                    s_dv.wait(SYN)
                    s_ld.inc(SYN.dma_start(out=d_d0, in_=d0t[:]), 16)
                    s_ld.inc(SYN.dma_start(out=d_d1, in_=d1t[:]), 16)
                    s_ld.inc(SYN.dma_start(out=d_u, in_=u_bf[:]), 16)
                    s_ld.inc(SYN.dma_start(out=d_mr, in_=mr_t[:]), 16)
                    s_ld.inc(SYN.dma_start(out=d_cp, in_=cp_t[:]), 16)
                    s_ld.inc(SYN.dma_start(out=d_bet, in_=bet[:]), 16)
                    s_ld.wait(SYN)
                    s_ld.wait(VEC)
                VEC.tensor_tensor_scan(out=texq[:], data0=d0t[:], data1=d1t[:],
                                       initial=(0.0 if a == 0
                                                else scanst[:, db:db + 1]),
                                       op0=ALU.mult, op1=ALU.add)
                if dbg_x2 and a == 0 and db == 0:
                    s_dv.inc(VEC.nop())
                    s_dv.wait(SYN)
                    s_ld.inc(SYN.dma_start(out=d_tex, in_=texq[:]), 16)
                    s_ld.wait(SYN)
                    s_ld.wait(VEC)
                VEC.tensor_scalar(out=scanst[:, db:db + 1],
                                  in0=texq[:, NA - 1:NA], scalar1=0.0,
                                  scalar2=None, op0=ALU.add)
                VEC.tensor_tensor(out=t5q[:], in0=cbv, in1=texq[:], op=ALU.mult)
                VEC.tensor_tensor(out=hall[:, db, :], in0=t5q[:],
                                  in1=b_bf[:, db, :], op=ALU.add)
            cb4 = cb_bf.rearrange("p a (n c) -> p a n c", c=CS)
            b4 = b_bf.rearrange("p a (n c) -> p a n c", c=CS)
            VEC.tensor_scalar(out=carry_cb[:], in0=cb4[:, :, NA // CS - 1, CS - 1],
                              scalar1=0.0, scalar2=None, op0=ALU.add)
            s_dv.inc(VEC.tensor_scalar(out=carry_b[:],
                                       in0=b4[:, :, NA // CS - 1, CS - 1],
                                       scalar1=0.0, scalar2=None, op0=ALU.add))

            if alev < 5:
                continue
            if dbg_x2:
                s_dv.wait(SYN)
                s_ac.wait(SYN)
                s_ld.inc(SYN.dma_start(out=d_wout, in_=woutT[:]), 16)
                s_ld.inc(SYN.dma_start(out=d_xfm, in_=xfm32[:]), 16)
                for nm, tile in ((d_xp, xp_bf), (d_xpln, xpln), (d_b, b_bf),
                                 (d_hall, hall)):
                    s_ld.inc(SYN.dma_start(out=nm[a], in_=tile[:]), 16)
                s_ld.inc(SYN.dma_start(out=d_cum[a], in_=cum_bf[:]), 16)
                s_ld.wait(SYN)
            s_dv.wait(TEN)
            for jb in range(8):
                bki = 2 + jb % 2
                guard_write(TEN, bki)
                for db in range(8):
                    mmi = TEN.matmul(bank(bki), woutT[:, db, jb, :],
                                     hall[:, db, :], start=(db == 0),
                                     stop=(db == 7))
                s_mm.inc(mmi)
                bankW[bki] = s_mm.mark()
                guard_read(VEC, bki)
                if n_x2_stores >= 2:
                    s_x2.wait(VEC, 16 * (n_x2_stores - 1))
                s_dv.inc(VEC.scalar_tensor_tensor(
                    out=x2st[:, jb % 2, :], in0=bank(bki),
                    scalar=ga_fm[:, jb:jb + 1], in1=xfm32[:, jb, :],
                    op0=ALU.mult, op1=ALU.add))
                bankR.setdefault(bki, []).append(s_dv.mark())
                s_dv.wait(SYN)
                s_x2.inc(SYN.dma_start(out=x2buf_v[:, jb, NA * a:NA * a + NA],
                                       in_=x2st[:, jb % 2, :]), 16)
                n_x2_stores += 1

        s_x.wait(SYN)
        s_x2.wait(SYN)
        ctxA.close()
        ctxW.close()
        barrier()
        if stages in ('a', 'apart'):
            mybir.codegen_inst_isa_subclasses(nc)
            return nc

        kwT = sb("kwT", [128, 8, 2048], BF16)
        winfT = sb("winfT", [128, 8, DI], BF16)
        woutfT = sb("woutfT", [128, 16, D], BF16)
        ctx0 = ExitStack()
        fwrow = ctx0.enter_context(nc.sbuf_tensor("fwrow", [128, 2, DI], BF16, side="right"))
        kblk = ctx0.enter_context(nc.sbuf_tensor("kblk", [128, 2, 512], BF16, side="right"))
        ktT = ctx0.enter_context(nc.sbuf_tensor("ktT", [128, 4, 128], BF16, side="right"))
        wqs = ctx0.enter_context(nc.sbuf_tensor("wqs", [128, 2, 4, 1024], BF16, side="right"))
        kwrow = ctx0.enter_context(nc.sbuf_tensor("kwrow", [128, 1024], BF16, side="right"))
        fw_guard = [None, None]

        def ftr_row(i, src_ap, nq, dst_evac):
            par = i % 2
            if fw_guard[par] is not None:
                m = fw_guard[par]
                m[0].wait(GPS, m[1])
            s_ldg.inc(GPS.dma_start(out=fwrow[:, par, 0:128 * nq], in_=src_ap,
                                    max_dma_last_dim=8192), 16)
            s_ldg.wait(TEN)
            for half in range((nq + 7) // 8):
                bk = par + 2 * half
                guard_write(TEN, bk)
                n_here = min(8, nq - 8 * half)
                for q in range(n_here):
                    j = 8 * half + q
                    mmi = TEN.matmul(
                        psbf[:, 1024 * bk + 128 * q:1024 * bk + 128 * q + 128],
                        fwrow[:, par, 128 * j:128 * j + 128], idn_bf[:],
                        is_transpose=True, start=(q == 0),
                        stop=(q == n_here - 1))
                s_tp.inc(mmi)
                bankW[bk] = s_tp.mark()
            fw_guard[par] = s_tp.mark()
            for half in range((nq + 7) // 8):
                bk = par + 2 * half
                guard_read(VEC, bk)
                s_dv.inc(dst_evac(half, psbf[:, 1024 * bk:1024 * bk + 1024]
                                  .rearrange("p (a b) -> p a b", b=128)))
                bankR.setdefault(bk, []).append(s_dv.mark())

        keys_v = keysd.rearrange("(h k p) d -> h k p d", h=8, k=128, p=2)
        kw_guard = [None, None]
        for hp in range(16):
            h_, p_ = hp // 2, hp % 2
            par = hp % 2
            if kw_guard[par] is not None:
                m = kw_guard[par]
                m[0].wait(GPS, m[1])
            s_ldg.inc(GPS.dma_start(out=kblk[:, par, :],
                                    in_=keys_v[h_, :, p_, :],
                                    max_dma_last_dim=8192), 16)
            f0 = p_ * 4096 + h_ * 512
            s_ldg.inc(GPS.dma_start(
                out=wqs[:, par, :, :],
                in_=wq[f0:f0 + 512, :].rearrange("(b q) c -> q b c", q=128),
                max_dma_last_dim=8192), 16)
            s_ldg.wait(TEN)
            guard_write(TEN, 4)
            for dkc in range(4):
                mmi = TEN.matmul(psbf[:, 4096 + 128 * dkc:4096 + 128 * dkc + 128],
                                 kblk[:, par, 128 * dkc:128 * dkc + 128],
                                 idn_bf[:], is_transpose=True, start=(dkc == 0),
                                 stop=(dkc == 3))
            s_tp.inc(mmi)
            bankW[4] = s_tp.mark()
            guard_read(VEC, 4)
            copy_ts(VEC, ktT.rearrange("a b c -> a (b c)"), psbf[:, 4096:4608],
                    s_dv)
            bankR.setdefault(4, []).append(s_dv.mark())
            s_dv.wait(TEN)
            for dt2 in range(2):
                bki = dt2
                guard_write(TEN, bki)
                for dkc in range(4):
                    mmi = TEN.matmul(bank(bki), ktT[:, dkc, :],
                                     wqs[:, par, dkc, 512 * dt2:512 * dt2 + 512],
                                     start=(dkc == 0), stop=(dkc == 3))
                s_mm.inc(mmi)
                bankW[bki] = s_mm.mark()
                guard_read(VEC, bki)
                copy_ts(VEC, kwrow[:, 512 * dt2:512 * dt2 + 512], bank(bki), s_dv)
                bankR.setdefault(bki, []).append(s_dv.mark())
            kw_guard[par] = s_mm.mark()
            s_dv.wait(TEN)
            guard_write(TEN, 5)
            for db in range(8):
                mmi = TEN.matmul(psbf[:, 5120 + 128 * db:5120 + 128 * db + 128],
                                 kwrow[:, 128 * db:128 * db + 128],
                                 idn_bf[:], is_transpose=True, start=(db == 0),
                                 stop=(db == 7))
            s_tp.inc(mmi)
            bankW[5] = s_tp.mark()
            guard_read(VEC, 5)
            sp = p_ * 8 + h_
            s_dv.inc(VEC.tensor_scalar(
                out=kwT[:, :, 128 * sp:128 * sp + 128],
                in0=psbf[:, 5120:6144].rearrange("p (a b) -> p a b", b=128),
                scalar1=0.0, scalar2=None, op0=ALU.add))
            bankR.setdefault(5, []).append(s_dv.mark())

        for jb in range(16):
            ftr_row(jb, w_inf[128 * jb:128 * jb + 128, :], 8,
                    lambda half, src3, jb=jb: VEC.tensor_scalar(
                        out=winfT[:, :, 128 * jb:128 * jb + 128], in0=src3,
                        scalar1=0.0, scalar2=None, op0=ALU.add))
        for db in range(8):
            ftr_row(db, w_outf[128 * db:128 * db + 128, :], 16,
                    lambda half, src3, db=db: VEC.tensor_scalar(
                        out=woutfT[:, 8 * half:8 * half + 8,
                                   128 * db:128 * db + 128],
                        in0=src3, scalar1=0.0, scalar2=None, op0=ALU.add))
        ctx0.close()
        barrier()
        if stages == 'w':
            mybir.codegen_inst_isa_subclasses(nc)
            return nc

        ctxB = ExitStack()
        sbB = lambda name, shape, dt: ctxB.enter_context(nc.sbuf_tensor(name, shape, dt, side="right"))
        x2fm = sbB("x2fm", [128, 8, NB], F32)
        x2tm = sbB("x2tm", [128, 2, D], F32)
        h2T = sbB("h2T", [128, 2, D], BF16)
        h2fm = sbB("h2fm", [128, 2, 8, NB], BF16)
        dmat = sbB("dmat", [128, 8, 128], BF16)
        ghT = sbB("ghT", [128, DI], BF16)
        gfm = sbB("gfm", [128, 16, 128], BF16)
        inA = sbB("inA", [128, 8, 1024], BF16)
        inBr = sbB("inBr", [128, 16384], BF16)
        x3t = ghT.bitcast(F32)
        outbuf = sbB("outbuf", [128, 3, 3072], BF16)
        thfbuf = sbB("thfbuf", [128, D], F32)
        eT = sbB("eT", [8, 128], I16)
        idx16 = sbB("idx16", [128, 2, 64], I16)
        mx = sbB("mx", [128, 16], F32)
        mix = sbB("mix", [128, 16], U32)
        gatev = sbB("gatev", [128, 2, 8], F32)
        bnout = sbB("bnout", [128, 12], F32)
        mvst = sbB("mvst", [128, 2], F32)
        rs3 = sbB("rs3", [128, 1], F32)
        p3 = sbB("p3", [128, 1], F32)
        ef = sbB("ef", [128, 8], F32)
        e16 = sbB("e16", [128, 8], I16)
        w64 = sbB("w64", [64, 16], I16)
        lin = sbB("lin", [128, 8], F32)
        ling = sbB("ling", [128, 8], F32)
        vB = inBr.rearrange("p (h c) -> p h c", c=2048)
        vO = inBr[:, 0:15360].rearrange("p (h c) -> p h c", c=3072)
        nidx_reg = GPS.alloc_register()
        GPS.reg_mov(nidx_reg, 1024)
        nidx_reg2 = GPS.alloc_register()
        GPS.reg_mov(nidx_reg2, 384)
        nidx_reg3 = GPS.alloc_register()
        GPS.reg_mov(nidx_reg3, 640)

        M = {}

        def setm(name, sem):
            M[name] = sem.mark()

        def wm(eng, name):
            mk = M.get(name)
            if mk is not None:
                mk[0].wait(eng, mk[1])

        VEC.memset(idx16[:], 0)
        thf = thfbuf
        dscr = thfbuf.bitcast(BF16)

        def front(b):
            par = b % 2
            wm(SYN, "x2fm_cons")
            s_bx.inc(SYN.dma_start(out=x2fm[:],
                                   in_=x2buf_v[:, :, NB * b:NB * b + NB]), 16)
            setm("x2fm_prod", s_bx)
            wm(TEN, "x2fm_prod")
            for hf in range(2):
                bki = 4 + hf
                guard_write(TEN, bki)
                for q in range(4):
                    db = hf * 4 + q
                    mmi = TEN.matmul(bank(bki)[:, 128 * q:128 * q + 128],
                                     x2fm[:, db, :], idn_f32[:],
                                     is_transpose=True, start=(q == 0),
                                     stop=(q == 3))
                s_tp.inc(mmi)
                bankW[bki] = s_tp.mark()
                if hf == 1:
                    setm("x2fm_cons", s_tp)
                guard_read(ACT2, bki)
                wm(ACT2, "x2tm_cons%d" % par)
                s_ac.inc(ACT2.activation(out=x2tm[:, par, 512 * hf:512 * hf + 512],
                                         in_=bank(bki), func=AF.Copy, bias=0.0))
                bankR.setdefault(bki, []).append(s_ac.mark())
            setm("x2tm_prod%d" % par, s_ac)
            wm(VEC, "x2tm_prod%d" % par)
            VEC.bn_stats(out=bnout[:, 0:6], in_=x2tm[:, par, 0:512])
            VEC.bn_stats(out=bnout[:, 6:12], in_=x2tm[:, par, 512:1024])
            s_dv.inc(VEC.bn_aggr(out=mvst[:], in_=bnout[:]))
            s_dv.wait(ACT2, s_dv.n)
            s_ac.inc(ACT2.activation(out=rs3[:], in_=mvst[:, 1:2], func=AF.Sqrt,
                                     bias=eps128[:]))
            s_ac.wait(VEC, s_ac.n)
            VEC.reciprocal(out=rs3[:], in_=rs3[:])
            VEC.tensor_tensor(out=p3[:], in0=mvst[:, 0:1], in1=rs3[:], op=ALU.mult)
            VEC.tensor_scalar(out=p3[:], in0=p3[:], scalar1=-1.0, scalar2=None,
                              op0=ALU.mult)
            VEC.tensor_scalar(out=thf[:], in0=x2tm[:, par, :], scalar1=rs3[:],
                              scalar2=p3[:], op0=ALU.mult, op1=ALU.add)
            VEC.tensor_tensor(out=thf[:], in0=thf[:], in1=cbtm[:], op=ALU.mult)
            s_dv.inc(VEC.tensor_tensor(out=h2T[:, par, :], in0=thf[:],
                                       in1=sbtm[:], op=ALU.add))
            setm("thf_done", s_dv)
            setm("h2T_prod%d" % par, s_dv)
            wm(TEN, "h2T_prod%d" % par)
            wm(TEN, "h2fm_cons%d" % par)
            for hf in range(2):
                guard_write(TEN, 4)
                for q in range(4):
                    db = hf * 4 + q
                    mmi = TEN.matmul(psbf[:, 4096 + 128 * q:4096 + 128 * q + 128],
                                     h2T[:, par, 128 * db:128 * db + 128],
                                     idn_bf[:], is_transpose=True,
                                     start=(q == 0), stop=(q == 3))
                s_tp.inc(mmi)
                bankW[4] = s_tp.mark()
                guard_read(VEC, 4)
                s_dv.inc(VEC.tensor_scalar(
                    out=h2fm[:, par, 4 * hf:4 * hf + 4, :],
                    in0=psbf[:, 4096:4608].rearrange("p (a b) -> p a b", b=128),
                    scalar1=0.0, scalar2=None, op0=ALU.add))
                bankR.setdefault(4, []).append(s_dv.mark())
            setm("h2fm_prod%d" % par, s_dv)
            wm(TEN, "h2fm_prod%d" % par)
            for half in range(2):
                b0 = 4 - 2 * half
                guard_write(TEN, b0)
                guard_write(TEN, b0 + 1)
                for ks in range(2):
                    for db in range(8):
                        mmi = TEN.matmul(
                            bank(b0 + ks), h2fm[:, par, db, :],
                            kwT[:, db, 1024 * half + 512 * ks:
                                1024 * half + 512 * ks + 512],
                            start=(db == 0), stop=(db == 7))
                    s_mm.inc(mmi)
                    bankW[b0 + ks] = s_mm.mark()
                guard_read(VEC, b0)
                guard_read(VEC, b0 + 1)
                VEC.tensor_reduce(out=mx[:, 8 * half:8 * half + 8],
                                  in_=ps[:, 512 * b0:512 * b0 + 1024].rearrange(
                                      "p (s k) -> p s k", k=128),
                                  axis=AX.X, op=ALU.max)
                s_dv.inc(VEC.max_index(out=mix[:, 8 * half:8 * half + 8],
                                       in_max=mx[:, 8 * half:8 * half + 8],
                                       in_values=ps[:, 512 * b0:512 * b0 + 1024]))
                bankR.setdefault(b0, []).append(s_dv.mark())
                bankR.setdefault(b0 + 1, []).append(s_dv.mark())
            VEC.tensor_tensor(out=gatev[:, par, :], in0=mx[:, 0:8],
                              in1=mx[:, 8:16], op=ALU.add)
            VEC.tensor_scalar(out=gatev[:, par, :], in0=gatev[:, par, :],
                              scalar1=0.0, scalar2=None, op0=ALU.max)
            VEC.tensor_scalar(out=ef[:], in0=mix[:, 0:8], scalar1=128.0,
                              scalar2=None, op0=ALU.mult)
            VEC.tensor_tensor(out=ef[:], in0=ef[:], in1=mix[:, 8:16], op=ALU.add)
            VEC.tensor_tensor(out=ef[:], in0=ef[:], in1=c16512[:], op=ALU.subtract)
            s_dv.inc(VEC.tensor_scalar(out=e16[:], in0=ef[:], scalar1=0,
                                       scalar2=None, op0=ALU.add))
            s_dv.wait(TEN, s_dv.n)
            guard_write(TEN, 5)
            s_tp.inc(TEN.matmul(psbf[0:8, 5120:5248], e16.bitcast(BF16)[:],
                                idn_bf[:], is_transpose=True, start=True,
                                stop=True))
            bankW[5] = s_tp.mark()

        def fidx(b):
            par = b % 2
            s_tp.wait(VEC, s_tp.n)
            wm(VEC, "eT_cons")
            s_dv.inc(VEC.tensor_scalar(out=eT[:],
                                       in0=ps.bitcast(I16)[0:8, 5120:5248],
                                       scalar1=0, scalar2=None, op0=ALU.add))
            bankR.setdefault(5, []).append(s_dv.mark())
            s_dv.wait(SYN, s_dv.n)
            wm(SYN, "w64_cons")
            s_bidx.inc(SYN.dma_start(out=w64[:], in_=eT[:]), 16)
            setm("eT_cons", s_bidx)
            s_bidx.wait(TEN)
            guard_write(TEN, 5)
            s_tp.inc(TEN.matmul(psbf[0:16, 5376:5440], w64.bitcast(BF16)[:],
                                idn_bf[0:64, 0:64], is_transpose=True,
                                start=True, stop=True))
            setm("w64_cons", s_tp)
            bankW[5] = s_tp.mark()
            s_tp.wait(VEC, s_tp.n)
            wm(VEC, "idx_cons%d" % par)
            s_dv.inc(VEC.tensor_scalar(out=idx16[0:16, par, :],
                                       in0=ps.bitcast(I16)[0:16, 5376:5440],
                                       scalar1=0, scalar2=None, op0=ALU.add))
            bankR.setdefault(5, []).append(s_dv.mark())
            s_dv.wait(SYN, s_dv.n)
            s_bidx.inc(SYN.dma_start(out=idx16[16:32, par, :],
                                     in_=idx16[0:16, par, :]), 16)
            setm("idx_prod%d" % par, s_bidx)

        def gather_A(b):
            par = b % 2
            wm(GPS, "idx_prod%d" % par)
            wm(GPS, "inA_free")
            if b == 0:
                emit_conv(16)
                s_conv.wait(GPS)
            s_bg.inc(GPS.dma_gather(out_ap=inA[:], in_ap=ptab_bf[:, 0:1024],
                                    idxs_ap=idx16[:, par, :], num_idxs=1024,
                                    num_idxs_reg=nidx_reg, elem_size=1024,
                                    elem_step=6144, single_packet=False), 16)
            setm("gA_prod", s_bg)

        def gather_B_out0(b):
            par = b % 2
            wm(GPS, "idx_prod%d" % par)
            wm(GPS, "inB_free")
            s_bg.inc(GPS.dma_gather(out_ap=vB, in_ap=ptab_bf[:, 1024:3072],
                                    idxs_ap=idx16[:, par, :], num_idxs=1024,
                                    num_idxs_reg=nidx_reg, elem_size=2048,
                                    elem_step=6144, single_packet=False), 16)
            setm("gB_prod", s_bg)
            wm(GPS, "outbuf_free")
            s_bg.inc(GPS.dma_gather(out_ap=outbuf[:],
                                    in_ap=ptab_bf[:, 3072:6144],
                                    idxs_ap=idx16[:, par, 0:24], num_idxs=384,
                                    num_idxs_reg=nidx_reg2, elem_size=3072,
                                    elem_step=6144, single_packet=False), 16)
            setm("gout0_prod", s_bg)

        def emit_final(b):
            par = b % 2
            guard_read(VEC, 6)
            guard_read(VEC, 7)
            if b >= 1:
                s_out.wait(VEC, 16 * b)
            VEC.tensor_tensor(out=x3t[:], in0=ps[:, 3072:4096], in1=gbtm[:],
                              op=ALU.mult)
            s_dv.inc(VEC.tensor_tensor(out=x3t[:], in0=x3t[:],
                                       in1=x2tm[:, par, :], op=ALU.add))
            bankR.setdefault(6, []).append(s_dv.mark())
            bankR.setdefault(7, []).append(s_dv.mark())
            setm("x2tm_cons%d" % par, s_dv)
            s_dv.wait(SYN, s_dv.n)
            s_out.inc(SYN.dma_start(out=outd[NB * b:NB * b + NB, :], in_=x3t[:]),
                      16)
            setm("x3_store", s_out)

        def back1(b):
            par = b % 2
            wm(VEC, "gA_prod")
            for h in range(8):
                VEC.scalar_tensor_tensor(out=dscr[:, 0:1024],
                                         in0=inA[:, h, :], scalar=1.0,
                                         in1=h2T[:, par, :], op0=ALU.mult,
                                         op1=ALU.mult, accum_out=lin[:, h:h + 1])
            VEC.tensor_tensor(out=ling[:], in0=lin[:], in1=gatev[:, par, :],
                              op=ALU.mult)
            wm(VEC, "dmat_cons")
            for h in range(8):
                iv = VEC.tensor_scalar(out=dmat[:, h, :], in0=idn_bf[:],
                                       scalar1=ling[:, h:h + 1], scalar2=None,
                                       op0=ALU.mult)
            s_dv.inc(iv)
            setm("dmat_prod", s_dv)
            setm("inA_free", s_dv)
            if b >= 1:
                emit_final(b - 1)
            wm(TEN, "h2fm_prod%d" % par)
            for js in range(4):
                guard_write(TEN, js)
                for db in range(8):
                    TEN.matmul(bank(js), h2fm[:, par, db, :],
                               winfT[:, db, 512 * js:512 * js + 512],
                               start=(db == 0), stop=False)
            wm(TEN, "gB_prod")
            wm(TEN, "dmat_prod")
            for js in range(4):
                for h in range(8):
                    mmi = TEN.matmul(bank(js), dmat[:, h, :],
                                     vB[:, h, 512 * js:512 * js + 512],
                                     start=False, stop=(h == 7))
                s_mm.inc(mmi)
                bankW[js] = s_mm.mark()
            setm("h2fm_cons%d" % par, s_mm)
            setm("dmat_cons", s_mm)
            setm("vB_read", s_mm)
            wm(GPS, "vB_read")
            s_bg.inc(GPS.dma_gather(out_ap=vO,
                                    in_ap=ptab_bf[:, 3072:6144],
                                    idxs_ap=idx16[:, par, 24:64], num_idxs=640,
                                    num_idxs_reg=nidx_reg3, elem_size=3072,
                                    elem_step=6144, single_packet=False), 16)
            setm("gout1_prod", s_bg)
            setm("idx_cons%d" % par, s_bg)
            wm(ACT2, "ghT_cons_t")
            wm(ACT2, "ghT_cons_v")
            wm(ACT2, "x3_store")
            for k in range(4):
                guard_read(ACT2, k)
                s_ac.inc(ACT2.activation(out=ghT[:, 512 * k:512 * k + 512],
                                         in_=bank(k), func=AF.Gelu))
                bankR.setdefault(k, []).append(s_ac.mark())
            setm("ghT_prod", s_ac)

        def back2pre(b):
            wm(VEC, "ghT_prod")
            wm(VEC, "gout0_prod")
            for h in range(3):
                VEC.scalar_tensor_tensor(out=dscr[:],
                                         in0=outbuf[:, h, 0:2048], scalar=1.0,
                                         in1=ghT[:], op0=ALU.mult,
                                         op1=ALU.mult, accum_out=lin[:, h:h + 1])
            wm(TEN, "ghT_prod")
            for bki in (6, 7):
                guard_write(TEN, bki)
                for jb in range(8):
                    j = (bki - 6) * 8 + jb
                    mmi = TEN.matmul(
                        psbf[:, 1024 * bki + 128 * jb:1024 * bki + 128 * jb + 128],
                        ghT[:, 128 * j:128 * j + 128], idn_bf[:],
                        is_transpose=True, start=(jb == 0), stop=(jb == 7))
                s_tp.inc(mmi)
                bankW[bki] = s_tp.mark()
            setm("ghT_cons_t", s_tp)

        def back2(b):
            par = b % 2
            for bki in (6, 7):
                guard_read(VEC, bki)
                s_dv.inc(VEC.tensor_scalar(
                    out=gfm[:, 8 * (bki - 6):8 * (bki - 6) + 8, :],
                    in0=psbf[:, 1024 * bki:1024 * bki + 1024].rearrange(
                        "p (a b) -> p a b", b=128),
                    scalar1=0.0, scalar2=None, op0=ALU.add))
                bankR.setdefault(bki, []).append(s_dv.mark())
            setm("gfm_prod", s_dv)
            wm(VEC, "gout1_prod")
            for h in range(5):
                VEC.scalar_tensor_tensor(out=dscr[:],
                                         in0=vO[:, h, 0:2048], scalar=1.0,
                                         in1=ghT[:], op0=ALU.mult,
                                         op1=ALU.mult,
                                         accum_out=lin[:, 3 + h:4 + h])
            VEC.tensor_tensor(out=ling[:], in0=lin[:], in1=gatev[:, par, :],
                              op=ALU.mult)
            wm(VEC, "dmat_cons")
            for h in range(8):
                iv = VEC.tensor_scalar(out=dmat[:, h, :], in0=idn_bf[:],
                                       scalar1=ling[:, h:h + 1], scalar2=None,
                                       op0=ALU.mult)
            s_dv.inc(iv)
            setm("dmat_prod", s_dv)
            setm("dots_out_done", s_dv)
            setm("ghT_cons_v", s_dv)
            wm(TEN, "gfm_prod")
            for js in range(2):
                guard_write(TEN, 6 + js)
                for jb in range(16):
                    TEN.matmul(bank(6 + js), gfm[:, jb, :],
                               woutfT[:, jb, 512 * js:512 * js + 512],
                               start=(jb == 0), stop=False)
            wm(TEN, "dmat_prod")
            wm(TEN, "gout0_prod")
            wm(TEN, "gout1_prod")
            for js in range(2):
                for h in range(3):
                    TEN.matmul(bank(6 + js), dmat[:, h, :],
                               outbuf[:, h, 2048 + 512 * js:
                                      2048 + 512 * js + 512],
                               start=False, stop=False)
                for h in range(5):
                    mmi = TEN.matmul(bank(6 + js), dmat[:, 3 + h, :],
                                     vO[:, h, 2048 + 512 * js:
                                        2048 + 512 * js + 512],
                                     start=False, stop=(h == 4))
                s_mm.inc(mmi)
                bankW[6 + js] = s_mm.mark()
            setm("outbuf_free", s_mm)
            setm("inB_free", s_mm)
            setm("dmat_cons", s_mm)

        front(0)
        fidx(0)
        gather_A(0)
        gather_B_out0(0)
        for b in range(TB):
            back1(b)
            if b + 1 < TB:
                front(b + 1)
            back2pre(b)
            if b + 1 < TB:
                fidx(b + 1)
            back2(b)
            if b + 1 < TB:
                gather_A(b + 1)
                gather_B_out0(b + 1)
        emit_final(TB - 1)

        s_out.wait(SYN)
        ctxB.close()

    mybir.codegen_inst_isa_subclasses(nc)
    return nc


_NC_CACHE = {}


def kernel(L0_attn_x, L0_attn_W_in, L0_attn_ln_g, L0_attn_ln_b, L0_attn_W_gate,
           L0_attn_b_gate, L0_attn_eig_raw, L0_attn_W_out, L0_ffn_Wq,
           L0_ffn_keys, L0_ffn_W_in, L0_ffn_W_out, L0_ffn_in_a, L0_ffn_in_b,
           L0_ffn_out_a, L0_ffn_out_b, L0_c, L0_wmod):
    cc = lambda x: np.ascontiguousarray(np.asarray(x, dtype=np.float32))
    ptab = np.ascontiguousarray(np.concatenate(
        [cc(L0_ffn_in_a), cc(L0_ffn_in_b), cc(L0_ffn_out_a), cc(L0_ffn_out_b)],
        axis=1))
    shared = {
        "w_in": cc(L0_attn_W_in), "ln_gd": cc(L0_attn_ln_g).reshape(1, D),
        "ln_bd": cc(L0_attn_ln_b).reshape(1, D),
        "w_gate": cc(L0_attn_W_gate),
        "b_gated": cc(L0_attn_b_gate).reshape(1, 32),
        "eig_d": cc(L0_attn_eig_raw).reshape(1, H16),
        "w_out": cc(L0_attn_W_out), "wq": cc(L0_ffn_Wq),
        "keysd": cc(L0_ffn_keys).reshape(2048, DK),
        "w_inf": cc(L0_ffn_W_in), "w_outf": cc(L0_ffn_W_out), "ptab": ptab,
        "wmod": cc(L0_wmod),
    }
    x = cc(L0_attn_x)
    c = cc(L0_c)
    if "nc" not in _NC_CACHE:
        _NC_CACHE["nc"] = build()
    nc = _NC_CACHE["nc"]
    in_maps = []
    for b in range(NCORES):
        m = dict(shared)
        m["xin"] = np.ascontiguousarray(x[b])
        m["cvec"] = np.ascontiguousarray(c[b:b + 1])
        in_maps.append(m)
    res = run_bass_kernel_spmd(nc, in_maps, core_ids=list(range(NCORES)))
    return np.stack([np.asarray(r["outd"], dtype=np.float32)
                     for r in res.results])


# version 10
# speedup vs baseline: 1.2256x; 1.0632x over previous
import numpy as np
import concourse.bass as bass
import concourse.mybir as mybir
from concourse.bass_utils import run_bass_kernel_spmd
from concourse import library_config
from contextlib import ExitStack

F32 = mybir.dt.float32
F32R = mybir.dt.float32r
BF16 = mybir.dt.bfloat16
I16 = mybir.dt.int16
I32 = mybir.dt.int32
U32 = mybir.dt.uint32
AF = mybir.ActivationFunctionType
ALU = mybir.AluOpType
AX = mybir.AxisListType

T, D = 4096, 1024
H16, CS = 16, 16
HEADS, NK, DK, DI, NE = 8, 128, 512, 2048, 16384
NA, NB = 512, 128
EPS = 1e-5
NCORES = 8



_VEC_IO = {
    "tensor_scalar": (("out",), ("in0", "scalar1", "scalar2")),
    "tensor_tensor": (("out",), ("in0", "in1")),
    "tensor_tensor_scan": (("out",), ("data0", "data1", "initial")),
    "scalar_tensor_tensor": (("out", "accum_out"), ("in0", "scalar", "in1")),
    "tensor_reduce": (("out",), ("in_",)),
    "reduce_max": (("out",), ("in_",)),
    "reduce_sum": (("out",), ("in_",)),
    "reciprocal": (("out",), ("in_",)),
    "memset": (("ap",), ()),
    "tensor_copy": (("dest",), ("src",)),
    "bn_stats": (("out",), ("in_",)),
    "bn_aggr": (("out",), ("in_",)),
    "max_index": (("out",), ("in_max", "in_values")),
    "max": (("out",), ("in_",)),
}
_ACT_IO = {
    "activation": (("out", "accum_out"), ("in_", "bias", "scale")),
    "copy": (("out",), ("in_",)),
}


class EngGuard:

    def __init__(self, eng, iomap):
        self._eng = eng
        self._iomap = iomap
        self._dirty = set()
        import inspect as _insp
        self._sigs = {}
        for m in iomap:
            try:
                self._sigs[m] = _insp.signature(getattr(eng, m))
            except (ValueError, TypeError):
                self._sigs[m] = None

    @staticmethod
    def _name_of(v):
        t = getattr(v, "tensor", None)
        return getattr(t, "name", None)

    def drain(self):
        self._dirty.clear()
        return self._eng.drain()

    def __getattr__(self, name):
        fn = getattr(self._eng, name)
        io = self._iomap.get(name)
        if io is None:
            return fn
        outs_k, ins_k = io
        sig = self._sigs.get(name)

        def wrapped(*args, **kwargs):
            if sig is not None:
                ba = sig.bind_partial(*args, **kwargs)
                argmap = ba.arguments
            else:
                argmap = kwargs
            rd = set()
            for k in ins_k:
                n = self._name_of(argmap.get(k))
                if n:
                    rd.add(n)
            if rd & self._dirty:
                self._eng.drain()
                self._dirty.clear()
            r = fn(*args, **kwargs)
            for k in outs_k:
                n = self._name_of(argmap.get(k))
                if n:
                    self._dirty.add(n)
            return r

        return wrapped


class Sem:
    def __init__(self, nc, ctx, name):
        self.h = ctx.enter_context(nc.semaphore(name))
        self.n = 0

    def inc(self, instr, k=1):
        instr.then_inc(self.h, k)
        self.n += k
        return instr

    def wait(self, eng, at=None):
        eng.wait_ge(self.h, self.n if at is None else at)

    def mark(self):
        return (self, self.n)


def build(n_tokens=T, stages='b', dbg_x2=False):
    alev = {'a1': 1, 'a2': 2, 'a3': 3, 'a4': 4}.get(stages, 5)
    if stages.startswith('a') and stages != 'a':
        stages = 'apart'
    nc = bass.Bass("TRN2", target_bir_lowering=False,
                   detect_race_conditions=False)
    TA = n_tokens // NA
    TB = n_tokens // NB

    dram = lambda name, shape, dt, kind: nc.dram_tensor(name, shape, dt, kind=kind).ap()
    xin = dram("xin", [T, D], F32, "ExternalInput")
    w_in = dram("w_in", [D, D], F32, "ExternalInput")
    ln_gd = dram("ln_gd", [1, D], F32, "ExternalInput")
    ln_bd = dram("ln_bd", [1, D], F32, "ExternalInput")
    w_gate = dram("w_gate", [32, D], F32, "ExternalInput")
    b_gated = dram("b_gated", [1, 32], F32, "ExternalInput")
    eig_d = dram("eig_d", [1, H16], F32, "ExternalInput")
    w_out = dram("w_out", [D, D], F32, "ExternalInput")
    wq = dram("wq", [8192, D], F32, "ExternalInput")
    keysd = dram("keysd", [2048, DK], F32, "ExternalInput")
    w_inf = dram("w_inf", [DI, D], F32, "ExternalInput")
    w_outf = dram("w_outf", [D, DI], F32, "ExternalInput")
    ptab = dram("ptab", [NE, 6144], F32, "ExternalInput")
    cvec = dram("cvec", [1, D], F32, "ExternalInput")
    wmod = dram("wmod", [D, 6144], F32, "ExternalInput")
    outd = dram("outd", [T, D], F32, "ExternalOutput")
    x2buf = dram("x2buf", [D, T], F32,
                 "ExternalOutput" if dbg_x2 else "Internal")
    ptab_bf = dram("ptab_bf", [NE, 6144], BF16, "Internal")
    modscr = dram("modscr", [1, 6144], F32, "Internal")
    widx = dram("widx", [4, NB * HEADS], I16, "Internal")
    if dbg_x2:
        d_xp = dram("d_xp", [T // NA, 128, 8, NA], BF16, "ExternalOutput")
        d_xpln = dram("d_xpln", [T // NA, 128, 8, NA], BF16, "ExternalOutput")
        d_cum = dram("d_cum", [T // NA, 16, NA], BF16, "ExternalOutput")
        d_b = dram("d_b", [T // NA, 128, 8, NA], BF16, "ExternalOutput")
        d_hall = dram("d_hall", [T // NA, 128, 8, NA], BF16, "ExternalOutput")
        d_d0 = dram("d_d0", [128, NA], BF16, "ExternalOutput")
        d_d1 = dram("d_d1", [128, NA], BF16, "ExternalOutput")
        d_tex = dram("d_tex", [128, NA], BF16, "ExternalOutput")
        d_u = dram("d_u", [128, 8, NA], BF16, "ExternalOutput")
        d_mr = dram("d_mr", [16, NA], F32, "ExternalOutput")
        d_cp = dram("d_cp", [16, NA], F32, "ExternalOutput")
        d_bet = dram("d_bet", [16, 4, 128], F32, "ExternalOutput")
        d_wout = dram("d_wout", [128, 8, 8, 128], BF16, "ExternalOutput")
        d_xfm = dram("d_xfm", [128, 8, NA], F32, "ExternalOutput")

    x2buf_v = x2buf.rearrange("(jb p) t -> p jb t", p=128)

    ctx = ExitStack()
    sb = lambda name, shape, dt: ctx.enter_context(nc.sbuf_tensor(name, shape, dt))

    ctxW = ExitStack()
    sw = lambda name, shape, dt: ctxW.enter_context(nc.sbuf_tensor(name, shape, dt, side="right"))
    pad32 = sw("pad32", [128, 8], I32)
    it32 = sw("it32", [128, 128], I32)
    ehd = sw("ehd", [16, 8, 128], BF16)
    ones_col = sw("ones_col", [1, 128], F32)
    idn_bf = sb("idn_bf", [128, 128], BF16)
    idn_f32 = sb("idn_f32", [128, 128], F32)
    ones_bf = sb("ones_bf", [128, 128], BF16)
    c16512 = sb("c16512", [128, 8], I32)
    eps128 = sb("eps128", [128, 1], F32)
    ca1_fm = sb("ca1_fm", [128, 8], F32)
    sa_bf = sb("sa_bf", [128, 8], BF16)
    ga_fm = sb("ga_fm", [128, 8], F32)
    lng_fm = sb("lng_fm", [128, 8], F32)
    lnb_fm = sb("lnb_fm", [128, 8], F32)
    bgA = sb("bgA", [16, 1], F32)
    bgB = sb("bgB", [16, 1], F32)
    eig_v = sb("eig_v", [16, 1], F32)
    cbtm = sb("cbtm", [128, D], BF16)
    sbtm = sb("sbtm", [128, D], BF16)
    gbtm = sb("gbtm", [128, D], F32)

    ps = ctx.enter_context(nc.psum_tensor("ps", [128, 4096], F32))
    psbf = ps.bitcast(BF16)

    def bank(i, cols=512):
        return ps[:, 512 * i: 512 * i + cols]

    S = lambda name: Sem(nc, ctx, name)
    sem_names = ["s_ld", "s_ldg", "s_tp", "s_ev", "s_mm", "s_dv", "s_ac",
                 "s_x", "s_x2", "s_bx", "s_bidx", "s_bg", "s_gf", "s_out",
                 "s_ph", "s_gp0", "s_conv"]
    SM = {n: S(n) for n in sem_names}
    s_ld, s_ldg, s_tp, s_ev, s_mm, s_dv, s_ac = (SM[k] for k in sem_names[:7])
    (s_x, s_x2, s_bx, s_bidx, s_bg, s_gf, s_out, s_ph,
     s_gp0, s_conv) = (SM[k] for k in sem_names[7:])

    bankW = {}
    bankR = {}

    def guard_write(eng, i):
        for m in bankR.get(i, []):
            m[0].wait(eng, m[1])
        bankR[i] = []

    def guard_read(eng, i):
        m = bankW.get(i)
        if m is not None:
            m[0].wait(eng, m[1])

    TEN, GPS, SYN = nc.tensor, nc.gpsimd, nc.sync
    VEC = EngGuard(nc.vector, _VEC_IO)
    ACT2 = EngGuard(nc.scalar, _ACT_IO)

    def copy_ts(eng, out, in0, sem, k=1, dst=None):
        return sem.inc(eng.tensor_scalar(out=out, in0=in0, scalar1=0.0,
                                         scalar2=None, op0=ALU.add), k)

    block = ctx.enter_context(nc.Block())

    with ctx:
        GPS.iota(it32[:], pattern=[[1, 128]], base=0, channel_multiplier=-1)
        s_gp0.inc(GPS.iota(c16512[:], pattern=[[16512, 8]], base=0,
                           channel_multiplier=0))
        s_gp0.wait(VEC)
        VEC.tensor_scalar(out=idn_bf[:], in0=it32[:], scalar1=0.0,
                          scalar2=None, op0=ALU.is_equal)
        VEC.tensor_scalar(out=idn_f32[:], in0=it32[:], scalar1=0.0,
                          scalar2=None, op0=ALU.is_equal)
        VEC.memset(ones_bf[:], 1.0)
        VEC.memset(eps128[:], float(EPS))
        s_dv.inc(VEC.memset(ones_col[:], 1.0))
        s_dv.wait(GPS)
        for c in range(8):
            s_gp0.inc(GPS.iota(it32[0:16, 0:128], pattern=[[1, 2], [0, 64]],
                               base=2 * c, channel_multiplier=-1))
            s_gp0.wait(VEC)
            s_dv.inc(VEC.tensor_scalar(out=ehd[:, c, :], in0=it32[0:16, 0:128],
                                       scalar1=0.0, scalar2=None,
                                       op0=ALU.is_equal))
            s_dv.wait(GPS)
        GPS.load_library(library_config.mlp)
        conv_emitted = [0]

        def emit_conv(k):
            for _ in range(k):
                rb = conv_emitted[0]
                if rb >= 16:
                    return
                conv_emitted[0] += 1
                s_conv.inc(GPS.dma_start(
                    out=ptab_bf[1024 * rb:1024 * rb + 1024, :],
                    in_=ptab[1024 * rb:1024 * rb + 1024, :],
                    max_dma_last_dim=8192), 16)

        ctx0 = ExitStack()
        crow = ctx0.enter_context(nc.sbuf_tensor("crow", [8, 128], F32, side="right"))
        wmod_sb = ctx0.enter_context(nc.sbuf_tensor("wmod_sb", [128, 8, 6144], BF16, side="right"))
        modrow = ctx0.enter_context(nc.sbuf_tensor("modrow", [1, 6144], F32, side="right"))
        scfm_bf = ctx0.enter_context(nc.sbuf_tensor("scfm_bf", [128, 8], BF16, side="right"))

        s_ld.inc(SYN.dma_start(out=crow[:],
                               in_=cvec.rearrange("o (p f) -> (o p) f", p=8)), 16)
        s_ld.wait(ACT2)
        s_ac.inc(ACT2.activation(out=crow[:], in_=crow[:], func=AF.Silu))
        s_ac.wait(TEN)
        s_dv.wait(TEN)
        s_tp.inc(TEN.matmul(bank(6)[:, 0:8], crow[:], idn_f32[0:8, 0:8],
                            is_transpose=True, start=True, stop=True))
        bankW[6] = s_tp.mark()
        s_tp.wait(VEC)
        s_dv.inc(VEC.tensor_scalar(out=scfm_bf[:], in0=bank(6)[:, 0:8],
                                   scalar1=0.0, scalar2=None, op0=ALU.add))
        bankR.setdefault(6, []).append(s_dv.mark())

        for r in range(8):
            s_ldg.inc(GPS.dma_start(out=wmod_sb[:, r, :],
                                    in_=wmod[128 * r:128 * r + 128, :],
                                    max_dma_last_dim=8192), 16)
        s_ldg.wait(TEN)
        s_dv.wait(TEN)
        for s_i in range(12):
            bki = s_i % 2
            guard_write(TEN, bki)
            for r in range(8):
                mmi = TEN.matmul(bank(bki)[0:1, :], scfm_bf[:, r:r + 1],
                                 wmod_sb[:, r, 512 * s_i:512 * s_i + 512],
                                 start=(r == 0), stop=(r == 7))
            s_mm.inc(mmi)
            bankW[bki] = s_mm.mark()
            guard_read(VEC, bki)
            s_dv.inc(VEC.tensor_scalar(out=modrow[:, 512 * s_i:512 * s_i + 512],
                                       in0=bank(bki)[0:1, :], scalar1=0.0,
                                       scalar2=None, op0=ALU.add))
            bankR.setdefault(bki, []).append(s_dv.mark())
        s_dv.wait(SYN)
        s_ld.inc(SYN.dma_start(out=modscr, in_=modrow[:]), 16)
        s_ld.wait(SYN)
        s_ld.wait(GPS)
        ctx0.close()

        ctx0 = ExitStack()
        vrow = ctx0.enter_context(nc.sbuf_tensor("vrow", [8, 128], F32, side="right"))
        rowt = ctx0.enter_context(nc.sbuf_tensor("rowt", [1, 2, D], F32, side="right"))
        tmp8 = ctx0.enter_context(nc.sbuf_tensor("tmp8", [128, 4, 8], F32, side="right"))

        def vec_to_fm(src_ap, post):
            s_dv.wait(SYN)
            s_ld.inc(SYN.dma_start(out=vrow[:],
                                   in_=src_ap.rearrange("o (p f) -> (o p) f", p=8)), 16)
            s_ld.wait(TEN)
            guard_write(TEN, 7)
            s_tp.inc(TEN.matmul(bank(7)[:, 0:8], vrow[:], idn_f32[0:8, 0:8],
                                is_transpose=True, start=True, stop=True))
            bankW[7] = s_tp.mark()
            s_tp.wait(VEC)
            post(bank(7)[:, 0:8])
            bankR.setdefault(7, []).append(s_dv.mark())

        vec_to_fm(modscr[:, 0:D],
                  lambda p: copy_ts(VEC, sa_bf[:], p, s_dv))
        vec_to_fm(modscr[:, D:2 * D],
                  lambda p: s_dv.inc(VEC.tensor_scalar(out=ca1_fm[:], in0=p,
                                                       scalar1=1.0, scalar2=None,
                                                       op0=ALU.add)))
        vec_to_fm(ln_gd, lambda p: copy_ts(VEC, lng_fm[:], p, s_dv))
        vec_to_fm(ln_bd, lambda p: copy_ts(VEC, lnb_fm[:], p, s_dv))

        def ga_post(p):
            g0 = tmp8[:, 0, :]
            VEC.tensor_scalar(out=g0, in0=p, scalar1=0.0, scalar2=None, op0=ALU.add)
            VEC.tensor_scalar(out=tmp8[:, 1, :], in0=g0, scalar1=-1.0,
                              scalar2=None, op0=ALU.mult)
            VEC.tensor_tensor(out=tmp8[:, 1, :], in0=g0, in1=tmp8[:, 1, :], op=ALU.max)
            VEC.tensor_scalar(out=tmp8[:, 1, :], in0=tmp8[:, 1, :], scalar1=0.25,
                              scalar2=None, op0=ALU.max)
            VEC.tensor_scalar(out=tmp8[:, 2, :], in0=g0, scalar1=0.0, scalar2=2.0,
                              op0=ALU.is_ge, op1=ALU.mult)
            VEC.tensor_scalar(out=tmp8[:, 2, :], in0=tmp8[:, 2, :], scalar1=-1.0,
                              scalar2=None, op0=ALU.add)
            s_dv.inc(VEC.tensor_tensor(out=ga_fm[:], in0=tmp8[:, 1, :],
                                       in1=tmp8[:, 2, :], op=ALU.mult))

        vec_to_fm(modscr[:, 2 * D:3 * D], ga_post)

        def row_to_tm(src_ap, dst, mode):
            s_dv.wait(SYN)
            s_ld.inc(SYN.dma_start(out=rowt[:, 0, :], in_=src_ap), 16)
            s_ld.wait(VEC)
            r = rowt[:, 0, :]
            if mode == "add1":
                VEC.tensor_scalar(out=r, in0=r, scalar1=1.0, scalar2=None,
                                  op0=ALU.add)
            elif mode == "clamp":
                r2 = rowt[:, 1, :]
                VEC.tensor_scalar(out=r2, in0=r, scalar1=-1.0, scalar2=None,
                                  op0=ALU.mult)
                VEC.tensor_tensor(out=r2, in0=r, in1=r2, op=ALU.max)
                VEC.tensor_scalar(out=r2, in0=r2, scalar1=0.25, scalar2=None,
                                  op0=ALU.max)
                VEC.tensor_scalar(out=r, in0=r, scalar1=0.0, scalar2=2.0,
                                  op0=ALU.is_ge, op1=ALU.mult)
                VEC.tensor_scalar(out=r, in0=r, scalar1=-1.0, scalar2=None,
                                  op0=ALU.add)
                VEC.tensor_tensor(out=r, in0=r, in1=r2, op=ALU.mult)
            s_dv.inc(VEC.nop())
            s_dv.wait(TEN)
            for h in range(2):
                guard_write(TEN, 7)
                s_mm.inc(TEN.matmul(bank(7), ones_col[:],
                                    rowt[:, 0, 512 * h:512 * h + 512],
                                    start=True, stop=True))
                bankW[7] = s_mm.mark()
                s_mm.wait(VEC)
                copy_ts(VEC, dst[:, 512 * h:512 * h + 512], bank(7), s_dv)
                bankR.setdefault(7, []).append(s_dv.mark())

        row_to_tm(modscr[:, 4 * D:5 * D], cbtm, "add1")
        row_to_tm(modscr[:, 3 * D:4 * D], sbtm, "copy")
        row_to_tm(modscr[:, 5 * D:6 * D], gbtm, "clamp")

        s_dv.wait(SYN)
        s_ld.inc(SYN.dma_start(out=rowt[0:1, 0, 0:32], in_=b_gated), 16)
        s_ld.inc(SYN.dma_start(out=rowt[0:1, 1, 0:16], in_=eig_d), 16)
        s_ld.wait(TEN)
        guard_write(TEN, 7)
        TEN.matmul(bank(7)[0:16, 0:1], rowt[0:1, 0, 0:16], idn_f32[0:1, 0:1],
                   is_transpose=True, start=True, stop=False)
        TEN.matmul(bank(7)[0:16, 1:2], rowt[0:1, 0, 16:32], idn_f32[0:1, 0:1],
                   is_transpose=True, start=False, stop=False)
        s_tp.inc(TEN.matmul(bank(7)[0:16, 2:3], rowt[0:1, 1, 0:16],
                            idn_f32[0:1, 0:1], is_transpose=True, start=False,
                            stop=True))
        bankW[7] = s_tp.mark()
        s_tp.wait(VEC)
        copy_ts(VEC, bgA[:], bank(7)[0:16, 0:1], s_dv)
        copy_ts(VEC, bgB[:], bank(7)[0:16, 1:2], s_dv)
        copy_ts(VEC, eig_v[:], bank(7)[0:16, 2:3], s_dv)
        bankR.setdefault(7, []).append(s_dv.mark())
        s_dv.wait(ACT2)
        s_ac.inc(ACT2.activation(out=eig_v[:], in_=eig_v[:], func=AF.Tanh))
        ctx0.close()

        wp_sb = ctxW.enter_context(nc.sbuf_tensor("wp_sb", [128, 8, 8, 128], BF16, side="right"))
        woutT = ctxW.enter_context(nc.sbuf_tensor("woutT", [128, 8, 8, 128], BF16, side="right"))
        wgT = ctxW.enter_context(nc.sbuf_tensor("wgT", [128, 8, 32], BF16, side="right"))
        augw1 = ctxW.enter_context(nc.sbuf_tensor("augw1", [1, 8, 128], BF16, side="right"))
        augw2 = ctxW.enter_context(nc.sbuf_tensor("augw2", [1, 8, 128], BF16, side="right"))
        ctxWt = ExitStack()
        wrow = ctxWt.enter_context(nc.sbuf_tensor("wrow", [128, 2, D], BF16, side="right"))
        wubrow = ctxWt.enter_context(nc.sbuf_tensor("wubrow", [128, 2, 8, 128], BF16, side="right"))
        wcT = ctxWt.enter_context(nc.sbuf_tensor("wcT", [16, 128], BF16, side="right"))
        wgrow = ctxWt.enter_context(nc.sbuf_tensor("wgrow", [32, D], BF16, side="right"))
        w1f = ctxWt.enter_context(nc.sbuf_tensor("w1f", [128, 16], BF16, side="right"))
        wrow_guard = [None, None]
        wub_guard = [None, None]

        def tr_row(jb, src_ap):
            par = jb % 2
            bki = par
            if wrow_guard[par] is not None:
                m = wrow_guard[par]
                m[0].wait(GPS, m[1])
            s_ldg.inc(GPS.dma_start(out=wrow[:, par, :], in_=src_ap,
                                    max_dma_last_dim=8192), 16)
            s_ldg.wait(TEN)
            guard_write(TEN, bki)
            for q in range(8):
                mmi = TEN.matmul(psbf[:, 1024 * bki + 128 * q:
                                      1024 * bki + 128 * q + 128],
                                 wrow[:, par, 128 * q:128 * q + 128], idn_bf[:],
                                 is_transpose=True, start=(q == 0),
                                 stop=(q == 7))
            s_tp.inc(mmi)
            wrow_guard[par] = s_tp.mark()
            bankW[bki] = s_tp.mark()
            guard_read(VEC, bki)
            return bki, psbf[:, 1024 * bki:1024 * bki + 1024].rearrange(
                "p (a b) -> p a b", b=128)

        for jb in range(8):
            par = jb % 2
            bki, src3 = tr_row(jb, w_in[128 * jb:128 * jb + 128, :])
            for db in range(8):
                VEC.tensor_scalar(out=wp_sb[:, db, jb, :], in0=src3[:, db, :],
                                  scalar1=ca1_fm[:, db:db + 1],
                                  scalar2=None, op0=ALU.mult)
            if wub_guard[par] is not None:
                m = wub_guard[par]
                m[0].wait(VEC, m[1])
            s_dv.inc(VEC.tensor_scalar(out=wubrow[:, par, :, :], in0=src3,
                                       scalar1=0.0, scalar2=None, op0=ALU.add))
            bankR.setdefault(bki, []).append(s_dv.mark())
            s_dv.wait(TEN)
            if jb == 0:
                guard_write(TEN, 4)
            for db in range(8):
                mmi = TEN.matmul(ps[:, 2048 + jb:2049 + jb],
                                 wubrow[:, par, db, :], sa_bf[:, db:db + 1],
                                 start=(jb == 0 and db == 0),
                                 stop=(jb == 7 and db == 7))
            s_mm.inc(mmi)
            wub_guard[par] = s_mm.mark()
        bankW[4] = s_mm.mark()

        guard_write(TEN, 5)
        for jb in range(8):
            for db in range(8):
                mmi = TEN.matmul(ps[:, 2560 + jb:2561 + jb], wp_sb[:, db, jb, :],
                                 ones_bf[:, 0:1],
                                 start=(jb == 0 and db == 0),
                                 stop=(jb == 7 and db == 7))
            s_mm.inc(mmi)
        bankW[5] = s_mm.mark()
        guard_read(VEC, 4)
        guard_read(VEC, 5)
        copy_ts(VEC, w1f[:, 8:16], ps[:, 2048:2056], s_dv)
        copy_ts(VEC, w1f[:, 0:8], ps[:, 2560:2568], s_dv)
        bankR.setdefault(4, []).append(s_dv.mark())
        bankR.setdefault(5, []).append(s_dv.mark())
        s_dv.wait(TEN)
        guard_write(TEN, 6)
        s_tp.inc(TEN.matmul(psbf[0:16, 6144:6272], w1f[:], idn_bf[:],
                            is_transpose=True, start=True, stop=True))
        bankW[6] = s_tp.mark()
        s_tp.wait(VEC)
        copy_ts(VEC, wcT[:], psbf[0:16, 6144:6272], s_dv)
        bankR.setdefault(6, []).append(s_dv.mark())
        s_dv.wait(SYN)
        s_ld.inc(SYN.dma_start(out=augw1[:], in_=wcT[0:8, :]), 16)
        s_ld.inc(SYN.dma_start(out=augw2[:], in_=wcT[8:16, :]), 16)
        s_ld.wait(SYN)

        for jb in range(8):
            bki, src3 = tr_row(jb, w_out[128 * jb:128 * jb + 128, :])
            s_dv.inc(VEC.tensor_scalar(out=woutT[:, :, jb, :], in0=src3,
                                       scalar1=0.0, scalar2=None, op0=ALU.add))
            bankR.setdefault(bki, []).append(s_dv.mark())

        s_ldg.inc(GPS.dma_start(out=wgrow[:], in_=w_gate,
                                max_dma_last_dim=8192), 16)
        s_ldg.wait(TEN)
        for db in range(8):
            guard_write(TEN, 6)
            s_tp.inc(TEN.matmul(psbf[:, 6144:6176],
                                wgrow[:, 128 * db:128 * db + 128],
                                idn_bf[0:32, 0:32], is_transpose=True,
                                start=True, stop=True))
            bankW[6] = s_tp.mark()
            s_tp.wait(VEC)
            copy_ts(VEC, wgT[:, db, :], psbf[:, 6144:6176], s_dv)
            bankR.setdefault(6, []).append(s_dv.mark())

        def barrier():
            tgt = s_ph.n + 5
            for eng in (GPS, VEC, ACT2, TEN, SYN):
                s_ph.inc(eng.nop())
            for eng in (GPS, VEC, ACT2, TEN, SYN):
                eng.wait_ge(s_ph.h, tgt)
            bankW.clear()
            bankR.clear()

        barrier()
        ctxWt.close()
        if stages == 'p0':
            s_ld.wait(SYN)
            s_ldg.wait(GPS)
            s_conv.wait(GPS)
            mybir.codegen_inst_isa_subclasses(nc)
            return nc

        ctxA = ExitStack()
        sbA = lambda name, shape, dt: ctxA.enter_context(nc.sbuf_tensor(name, shape, dt, side="right"))
        x_tm = sbA("x_tm", [128, 2, 4, D], F32)
        xfm32 = sbA("xfm32", [128, 8, NA], F32)
        xfm_bf = sbA("xfm_bf", [128, 8, NA], BF16)
        x2_bf = sbA("x2_bf", [128, 8, NA], BF16)
        xp_bf = sbA("xp_bf", [128, 8, NA], BF16)
        xpln = sbA("xpln", [128, 8, NA], BF16)
        hall = sbA("hall", [128, 8, NA], BF16)
        b_bf = sbA("b_bf", [128, 8, NA], BF16)
        cb_bf = sbA("cb_bf", [128, 8, NA], BF16)
        u_bf = sbA("u_bf", [128, 8, NA], BF16)
        d0t = sbA("d0t", [128, NA], BF16)
        d1t = sbA("d1t", [128, NA], BF16)
        texq = sbA("texq", [128, NA], BF16)
        t5q = sbA("t5q", [128, NA], BF16)
        mu_t = sbA("mu_t", [128, NA], F32)
        ms_t = sbA("ms_t", [128, NA], F32)
        rs_t = sbA("rs_t", [128, NA], F32)
        sq_t = sbA("sq_t", [128, NA], F32)
        sdev_bf = sbA("sdev_bf", [128, NA], BF16)
        negmu_bf = sbA("negmu_bf", [128, NA], BF16)
        rs2_t = sbA("rs2_t", [128, NA], F32)
        m2r_t = sbA("m2r_t", [128, NA], F32)
        gtm = sbA("gtm", [128, 32], F32)
        alph = sbA("alph", [16, 4, 128], F32)
        bet = sbA("bet", [16, 4, 128], F32)
        a_t = sbA("a_t", [16, NA], F32)
        cum_t = sbA("cum_t", [16, NA], F32)
        cp_t = sbA("cp_t", [16, NA], F32)
        msk_t = sbA("msk_t", [16, NA], F32)
        mr_t = sbA("mr_t", [16, NA], F32)
        cum_bf = sbA("cum_bf", [16, NA], BF16)
        mrb_bf = sbA("mrb_bf", [16, NA], BF16)
        beta_bf = sbA("beta_bf", [16, NA], BF16)
        x2st = sbA("x2st", [128, 2, NA], F32)
        carry_cb = sbA("carry_cb", [128, 8], BF16)
        carry_b = sbA("carry_b", [128, 8], BF16)
        scanst = sbA("scanst", [128, 8], BF16)

        cum3 = cum_t.rearrange("p (n c) -> p n c", c=CS)
        cp3 = cp_t.rearrange("p (n c) -> p n c", c=CS)

        n_x2_stores = 0
        for a in range(TA):
            par = a % 2
            if a >= 2:
                s_tp.wait(SYN, s_tp.n)
            s_x.inc(SYN.dma_start(
                out=x_tm[:, par, :, :],
                in_=xin[NA * a:NA * a + NA, :].rearrange("(s p) d -> p s d", p=128)), 16)
            emit_conv((16 + TA - 1) // TA)

            s_x.wait(TEN)
            if a > 0:
                s_dv.wait(TEN, s_dv.n)
            for grp in range(8):
                s_i, hf = grp // 2, grp % 2
                bki = grp % 2
                guard_write(TEN, bki)
                for q in range(4):
                    db = hf * 4 + q
                    mmi = TEN.matmul(bank(bki)[:, 128 * q:128 * q + 128],
                                     x_tm[:, par, s_i, 128 * db:128 * db + 128],
                                     idn_f32[:], is_transpose=True,
                                     start=(q == 0), stop=(q == 3))
                s_tp.inc(mmi)
                bankW[bki] = s_tp.mark()
                src3 = bank(bki).rearrange("p (a b) -> p a b", b=128)
                guard_read(ACT2, bki)
                dst32 = xfm32[:, 4 * hf:4 * hf + 4, 128 * s_i:128 * s_i + 128]
                s_ac.inc(ACT2.activation(out=dst32, in_=src3, func=AF.Copy,
                                         bias=0.0))
                bankR.setdefault(bki, []).append(s_ac.mark())
                s_ac.wait(VEC)
                dstbf = xfm_bf[:, 4 * hf:4 * hf + 4, 128 * s_i:128 * s_i + 128]
                s_dv.inc(VEC.tensor_scalar(out=dstbf, in0=dst32, scalar1=0.0,
                                           scalar2=None, op0=ALU.add))

            if alev < 2:
                continue
            s_ac.wait(ACT2)
            for db in range(8):
                ii = ACT2.activation(out=x2_bf[:, db, :], in_=xfm32[:, db, :],
                                     func=AF.Square)
            s_ac.inc(ii)

            s_dv.wait(TEN)
            s_ac.wait(TEN)
            guard_write(TEN, 4)
            for db in range(8):
                m1 = TEN.matmul(bank(4), ones_bf[:], xfm_bf[:, db, :],
                                start=(db == 0), stop=(db == 7))
            s_mm.inc(m1)
            bankW[4] = s_mm.mark()
            guard_write(TEN, 5)
            for db in range(8):
                m2 = TEN.matmul(bank(5), ones_bf[:], x2_bf[:, db, :],
                                start=(db == 0), stop=(db == 7))
            s_mm.inc(m2)
            bankW[5] = s_mm.mark()

            def stats_chain(rs_dst):
                guard_read(VEC, 4)
                guard_read(VEC, 5)
                VEC.tensor_scalar(out=mu_t[:], in0=bank(4), scalar1=1.0 / D,
                                  scalar2=None, op0=ALU.mult)
                s_dv.inc(VEC.tensor_scalar(out=ms_t[:], in0=bank(5),
                                           scalar1=1.0 / D, scalar2=None,
                                           op0=ALU.mult))
                bankR.setdefault(4, []).append(s_dv.mark())
                bankR.setdefault(5, []).append(s_dv.mark())
                s_dv.wait(ACT2)
                s_ac.inc(ACT2.activation(out=sq_t[:], in_=mu_t[:], func=AF.Square))
                s_ac.wait(VEC)
                s_dv.inc(VEC.tensor_tensor(out=ms_t[:], in0=ms_t[:], in1=sq_t[:],
                                           op=ALU.subtract))
                s_dv.wait(ACT2)
                s_ac.inc(ACT2.activation(out=ms_t[:], in_=ms_t[:], func=AF.Sqrt,
                                         bias=eps128[:]))
                s_ac.wait(VEC)
                s_dv.inc(VEC.reciprocal(out=rs_dst[:], in_=ms_t[:]))

            stats_chain(rs_t)
            VEC.tensor_scalar(out=sdev_bf[:], in0=ms_t[:], scalar1=0.0,
                              scalar2=None, op0=ALU.add)
            s_dv.inc(VEC.tensor_scalar(out=negmu_bf[:], in0=mu_t[:],
                                       scalar1=-1.0, scalar2=None, op0=ALU.mult))

            s_dv.wait(TEN)
            for jb in range(8):
                bki = 2 + jb % 2
                guard_write(TEN, bki)
                for db in range(8):
                    TEN.matmul(bank(bki), wp_sb[:, db, jb, :], xfm_bf[:, db, :],
                               start=(db == 0), stop=False)
                TEN.matmul(bank(bki), augw1[0:1, jb, :], negmu_bf[0:1, :],
                           start=False, stop=False)
                s_mm.inc(TEN.matmul(bank(bki), augw2[0:1, jb, :],
                                    sdev_bf[0:1, :], start=False, stop=True))
                bankW[bki] = s_mm.mark()
                guard_read(VEC, bki)
                s_dv.inc(VEC.tensor_tensor(out=xp_bf[:, jb, :], in0=bank(bki),
                                           in1=rs_t[:], op=ALU.mult))
                bankR.setdefault(bki, []).append(s_dv.mark())

            if alev < 3:
                continue
            s_dv.wait(ACT2)
            for db in range(8):
                ii = ACT2.activation(out=x2_bf[:, db, :], in_=xp_bf[:, db, :],
                                     func=AF.Square)
            s_ac.inc(ii)
            s_ac.wait(TEN)
            s_dv.wait(TEN)
            guard_write(TEN, 4)
            for db in range(8):
                m1 = TEN.matmul(bank(4), ones_bf[:], xp_bf[:, db, :],
                                start=(db == 0), stop=(db == 7))
            s_mm.inc(m1)
            bankW[4] = s_mm.mark()
            guard_write(TEN, 5)
            for db in range(8):
                m2 = TEN.matmul(bank(5), ones_bf[:], x2_bf[:, db, :],
                                start=(db == 0), stop=(db == 7))
            s_mm.inc(m2)
            bankW[5] = s_mm.mark()
            stats_chain(rs2_t)
            VEC.tensor_scalar(out=m2r_t[:], in0=mu_t[:], scalar1=-1.0,
                              scalar2=None, op0=ALU.mult)
            s_dv.inc(VEC.tensor_tensor(out=m2r_t[:], in0=m2r_t[:], in1=rs2_t[:],
                                       op=ALU.mult))

            for jb in range(8):
                VEC.tensor_tensor(out=xpln[:, jb, :], in0=xp_bf[:, jb, :],
                                  in1=rs2_t[:], op=ALU.mult)
                VEC.tensor_tensor(out=xpln[:, jb, :], in0=xpln[:, jb, :],
                                  in1=m2r_t[:], op=ALU.add)
                iv = VEC.tensor_scalar(out=xpln[:, jb, :], in0=xpln[:, jb, :],
                                       scalar1=lng_fm[:, jb:jb + 1],
                                       scalar2=lnb_fm[:, jb:jb + 1],
                                       op0=ALU.mult, op1=ALU.add)
            s_dv.inc(iv)

            for s_i in range(4):
                s_dv.wait(TEN)
                guard_write(TEN, 6)
                for db in range(8):
                    mg = TEN.matmul(ps[:, 3072:3104],
                                    xpln[:, db, 128 * s_i:128 * s_i + 128],
                                    wgT[:, db, :], start=(db == 0),
                                    stop=(db == 7))
                s_mm.inc(mg)
                bankW[6] = s_mm.mark()
                guard_read(VEC, 6)
                copy_ts(VEC, gtm[:], ps[:, 3072:3104], s_dv)
                bankR.setdefault(6, []).append(s_dv.mark())
                s_dv.wait(TEN)
                TEN.matmul(ps[0:16, 3104:3232], gtm[:, 0:16], idn_f32[:],
                           is_transpose=True, start=True, stop=False)
                s_mm.inc(TEN.matmul(ps[0:16, 3232:3360], gtm[:, 16:32],
                                    idn_f32[:], is_transpose=True, start=False,
                                    stop=True))
                bankW[6] = s_mm.mark()
                guard_read(ACT2, 6)
                ACT2.activation(out=alph[:, s_i, :], in_=ps[0:16, 3104:3232],
                                func=AF.Sigmoid, bias=bgA[:])
                s_ac.inc(ACT2.activation(out=bet[:, s_i, :],
                                         in_=ps[0:16, 3232:3360],
                                         func=AF.Sigmoid, bias=bgB[:]))
                bankR.setdefault(6, []).append(s_ac.mark())

            if alev < 4:
                continue
            s_ac.wait(VEC)
            av = alph.rearrange("p s c -> p (s c)")
            bv = bet.rearrange("p s c -> p (s c)")
            VEC.tensor_scalar(out=a_t[:], in0=av, scalar1=eig_v[:],
                              scalar2=None, op0=ALU.mult)
            VEC.tensor_scalar(out=cum_t[:], in0=a_t[:], scalar1=0.0,
                              scalar2=None, op0=ALU.add)
            for i in range(1, CS):
                VEC.tensor_tensor(out=cum3[:, :, i], in0=cum3[:, :, i],
                                  in1=cum3[:, :, i - 1], op=ALU.mult)
            VEC.tensor_scalar(out=cp_t[:, 1:NA], in0=cum_t[:, 0:NA - 1],
                              scalar1=0.0, scalar2=None, op0=ALU.add)
            s_dv.inc(VEC.memset(cp3[:, :, 0], 1.0))
            s_dv.wait(ACT2)
            s_ac.inc(ACT2.activation(out=msk_t[:], in_=cp_t[:], func=AF.Abs))
            s_ac.wait(VEC)
            VEC.tensor_scalar(out=msk_t[:], in0=msk_t[:], scalar1=1e-8,
                              scalar2=None, op0=ALU.is_gt)
            VEC.tensor_scalar(out=mr_t[:], in0=msk_t[:], scalar1=-1.0,
                              scalar2=1.0, op0=ALU.mult, op1=ALU.add)
            VEC.tensor_tensor(out=cp_t[:], in0=cp_t[:], in1=mr_t[:], op=ALU.add)
            VEC.reciprocal(out=mr_t[:], in_=cp_t[:])
            VEC.tensor_tensor(out=mr_t[:], in0=mr_t[:], in1=msk_t[:], op=ALU.mult)
            VEC.tensor_scalar(out=beta_bf[:], in0=bv, scalar1=0.0,
                              scalar2=None, op0=ALU.add)
            VEC.tensor_tensor(out=mrb_bf[:], in0=mr_t[:], in1=bv, op=ALU.mult)
            s_dv.inc(VEC.tensor_scalar(out=cum_bf[:], in0=cum_t[:], scalar1=0.0,
                                       scalar2=None, op0=ALU.add))

            s_dv.wait(TEN)
            for db in range(8):
                bki = 2 + db % 2
                for srcv, dstv, isb in ((beta_bf, b_bf, True),
                                        (mrb_bf, u_bf, True),
                                        (cum_bf, cb_bf, False)):
                    guard_write(TEN, bki)
                    s_mm.inc(TEN.matmul(bank(bki), ehd[:, db, :], srcv[:],
                                        start=True, stop=True))
                    bankW[bki] = s_mm.mark()
                    guard_read(VEC, bki)
                    if isb:
                        s_dv.inc(VEC.tensor_tensor(out=dstv[:, db, :],
                                                   in0=bank(bki),
                                                   in1=xpln[:, db, :],
                                                   op=ALU.mult))
                    else:
                        copy_ts(VEC, dstv[:, db, :], bank(bki), s_dv)
                    bankR.setdefault(bki, []).append(s_dv.mark())

            for db in range(8):
                d03 = d0t.rearrange("p (n c) -> p n c", c=CS)
                d13 = d1t.rearrange("p (n c) -> p n c", c=CS)
                cbv = cb_bf[:, db, :]
                cb3 = cbv.rearrange("p (n c) -> p n c", c=CS)
                bv3 = b_bf[:, db, :].rearrange("p (n c) -> p n c", c=CS)
                VEC.memset(d0t[:], 1.0)
                VEC.tensor_scalar(out=d03[:, 1:NA // CS, 0],
                                  in0=cb3[:, 0:NA // CS - 1, CS - 1],
                                  scalar1=0.0, scalar2=None, op0=ALU.add)
                VEC.tensor_scalar(out=d1t[:, 1:NA], in0=u_bf[:, db, 0:NA - 1],
                                  scalar1=0.0, scalar2=None, op0=ALU.add)
                VEC.tensor_scalar(out=d13[:, 1:NA // CS, 0],
                                  in0=bv3[:, 0:NA // CS - 1, CS - 1],
                                  scalar1=0.0, scalar2=None, op0=ALU.add)
                if a == 0:
                    VEC.memset(d0t[:, 0:1], 0.0)
                    VEC.memset(d1t[:, 0:1], 0.0)
                else:
                    VEC.tensor_scalar(out=d0t[:, 0:1], in0=carry_cb[:, db:db + 1],
                                      scalar1=0.0, scalar2=None, op0=ALU.add)
                    VEC.tensor_scalar(out=d1t[:, 0:1], in0=carry_b[:, db:db + 1],
                                      scalar1=0.0, scalar2=None, op0=ALU.add)
                if dbg_x2 and a == 0 and db == 0:
                    s_dv.inc(VEC.nop())
                    s_dv.wait(SYN)
                    s_ld.inc(SYN.dma_start(out=d_d0, in_=d0t[:]), 16)
                    s_ld.inc(SYN.dma_start(out=d_d1, in_=d1t[:]), 16)
                    s_ld.inc(SYN.dma_start(out=d_u, in_=u_bf[:]), 16)
                    s_ld.inc(SYN.dma_start(out=d_mr, in_=mr_t[:]), 16)
                    s_ld.inc(SYN.dma_start(out=d_cp, in_=cp_t[:]), 16)
                    s_ld.inc(SYN.dma_start(out=d_bet, in_=bet[:]), 16)
                    s_ld.wait(SYN)
                    s_ld.wait(VEC)
                VEC.tensor_tensor_scan(out=texq[:], data0=d0t[:], data1=d1t[:],
                                       initial=(0.0 if a == 0
                                                else scanst[:, db:db + 1]),
                                       op0=ALU.mult, op1=ALU.add)
                if dbg_x2 and a == 0 and db == 0:
                    s_dv.inc(VEC.nop())
                    s_dv.wait(SYN)
                    s_ld.inc(SYN.dma_start(out=d_tex, in_=texq[:]), 16)
                    s_ld.wait(SYN)
                    s_ld.wait(VEC)
                VEC.tensor_scalar(out=scanst[:, db:db + 1],
                                  in0=texq[:, NA - 1:NA], scalar1=0.0,
                                  scalar2=None, op0=ALU.add)
                VEC.tensor_tensor(out=t5q[:], in0=cbv, in1=texq[:], op=ALU.mult)
                VEC.tensor_tensor(out=hall[:, db, :], in0=t5q[:],
                                  in1=b_bf[:, db, :], op=ALU.add)
            cb4 = cb_bf.rearrange("p a (n c) -> p a n c", c=CS)
            b4 = b_bf.rearrange("p a (n c) -> p a n c", c=CS)
            VEC.tensor_scalar(out=carry_cb[:], in0=cb4[:, :, NA // CS - 1, CS - 1],
                              scalar1=0.0, scalar2=None, op0=ALU.add)
            s_dv.inc(VEC.tensor_scalar(out=carry_b[:],
                                       in0=b4[:, :, NA // CS - 1, CS - 1],
                                       scalar1=0.0, scalar2=None, op0=ALU.add))

            if alev < 5:
                continue
            if dbg_x2:
                s_dv.wait(SYN)
                s_ac.wait(SYN)
                s_ld.inc(SYN.dma_start(out=d_wout, in_=woutT[:]), 16)
                s_ld.inc(SYN.dma_start(out=d_xfm, in_=xfm32[:]), 16)
                for nm, tile in ((d_xp, xp_bf), (d_xpln, xpln), (d_b, b_bf),
                                 (d_hall, hall)):
                    s_ld.inc(SYN.dma_start(out=nm[a], in_=tile[:]), 16)
                s_ld.inc(SYN.dma_start(out=d_cum[a], in_=cum_bf[:]), 16)
                s_ld.wait(SYN)
            s_dv.wait(TEN)
            for jb in range(8):
                bki = 2 + jb % 2
                guard_write(TEN, bki)
                for db in range(8):
                    mmi = TEN.matmul(bank(bki), woutT[:, db, jb, :],
                                     hall[:, db, :], start=(db == 0),
                                     stop=(db == 7))
                s_mm.inc(mmi)
                bankW[bki] = s_mm.mark()
                guard_read(VEC, bki)
                if n_x2_stores >= 2:
                    s_x2.wait(VEC, 16 * (n_x2_stores - 1))
                s_dv.inc(VEC.scalar_tensor_tensor(
                    out=x2st[:, jb % 2, :], in0=bank(bki),
                    scalar=ga_fm[:, jb:jb + 1], in1=xfm32[:, jb, :],
                    op0=ALU.mult, op1=ALU.add))
                bankR.setdefault(bki, []).append(s_dv.mark())
                s_dv.wait(SYN)
                s_x2.inc(SYN.dma_start(out=x2buf_v[:, jb, NA * a:NA * a + NA],
                                       in_=x2st[:, jb % 2, :]), 16)
                n_x2_stores += 1

        s_x.wait(SYN)
        s_x2.wait(SYN)
        ctxA.close()
        ctxW.close()
        barrier()
        if stages in ('a', 'apart'):
            mybir.codegen_inst_isa_subclasses(nc)
            return nc

        kwT = sb("kwT", [128, 8, 2048], BF16)
        winfT = sb("winfT", [128, 8, DI], BF16)
        woutfT = sb("woutfT", [128, 16, D], BF16)
        ctx0 = ExitStack()
        fwrow = ctx0.enter_context(nc.sbuf_tensor("fwrow", [128, 2, DI], BF16, side="right"))
        kblk = ctx0.enter_context(nc.sbuf_tensor("kblk", [128, 2, 512], BF16, side="right"))
        ktT = ctx0.enter_context(nc.sbuf_tensor("ktT", [128, 4, 128], BF16, side="right"))
        wqs = ctx0.enter_context(nc.sbuf_tensor("wqs", [128, 2, 4, 1024], BF16, side="right"))
        kwrow = ctx0.enter_context(nc.sbuf_tensor("kwrow", [128, 1024], BF16, side="right"))
        fw_guard = [None, None]

        def ftr_row(i, src_ap, nq, dst_evac):
            par = i % 2
            if fw_guard[par] is not None:
                m = fw_guard[par]
                m[0].wait(GPS, m[1])
            s_ldg.inc(GPS.dma_start(out=fwrow[:, par, 0:128 * nq], in_=src_ap,
                                    max_dma_last_dim=8192), 16)
            s_ldg.wait(TEN)
            for half in range((nq + 7) // 8):
                bk = par + 2 * half
                guard_write(TEN, bk)
                n_here = min(8, nq - 8 * half)
                for q in range(n_here):
                    j = 8 * half + q
                    mmi = TEN.matmul(
                        psbf[:, 1024 * bk + 128 * q:1024 * bk + 128 * q + 128],
                        fwrow[:, par, 128 * j:128 * j + 128], idn_bf[:],
                        is_transpose=True, start=(q == 0),
                        stop=(q == n_here - 1))
                s_tp.inc(mmi)
                bankW[bk] = s_tp.mark()
            fw_guard[par] = s_tp.mark()
            for half in range((nq + 7) // 8):
                bk = par + 2 * half
                guard_read(VEC, bk)
                s_dv.inc(dst_evac(half, psbf[:, 1024 * bk:1024 * bk + 1024]
                                  .rearrange("p (a b) -> p a b", b=128)))
                bankR.setdefault(bk, []).append(s_dv.mark())

        keys_v = keysd.rearrange("(h k p) d -> h k p d", h=8, k=128, p=2)
        kw_guard = [None, None]
        for hp in range(16):
            h_, p_ = hp // 2, hp % 2
            par = hp % 2
            if kw_guard[par] is not None:
                m = kw_guard[par]
                m[0].wait(GPS, m[1])
            s_ldg.inc(GPS.dma_start(out=kblk[:, par, :],
                                    in_=keys_v[h_, :, p_, :],
                                    max_dma_last_dim=8192), 16)
            f0 = p_ * 4096 + h_ * 512
            s_ldg.inc(GPS.dma_start(
                out=wqs[:, par, :, :],
                in_=wq[f0:f0 + 512, :].rearrange("(b q) c -> q b c", q=128),
                max_dma_last_dim=8192), 16)
            s_ldg.wait(TEN)
            guard_write(TEN, 4)
            for dkc in range(4):
                mmi = TEN.matmul(psbf[:, 4096 + 128 * dkc:4096 + 128 * dkc + 128],
                                 kblk[:, par, 128 * dkc:128 * dkc + 128],
                                 idn_bf[:], is_transpose=True, start=(dkc == 0),
                                 stop=(dkc == 3))
            s_tp.inc(mmi)
            bankW[4] = s_tp.mark()
            guard_read(VEC, 4)
            copy_ts(VEC, ktT.rearrange("a b c -> a (b c)"), psbf[:, 4096:4608],
                    s_dv)
            bankR.setdefault(4, []).append(s_dv.mark())
            s_dv.wait(TEN)
            for dt2 in range(2):
                bki = dt2
                guard_write(TEN, bki)
                for dkc in range(4):
                    mmi = TEN.matmul(bank(bki), ktT[:, dkc, :],
                                     wqs[:, par, dkc, 512 * dt2:512 * dt2 + 512],
                                     start=(dkc == 0), stop=(dkc == 3))
                s_mm.inc(mmi)
                bankW[bki] = s_mm.mark()
                guard_read(VEC, bki)
                copy_ts(VEC, kwrow[:, 512 * dt2:512 * dt2 + 512], bank(bki), s_dv)
                bankR.setdefault(bki, []).append(s_dv.mark())
            kw_guard[par] = s_mm.mark()
            s_dv.wait(TEN)
            guard_write(TEN, 5)
            for db in range(8):
                mmi = TEN.matmul(psbf[:, 5120 + 128 * db:5120 + 128 * db + 128],
                                 kwrow[:, 128 * db:128 * db + 128],
                                 idn_bf[:], is_transpose=True, start=(db == 0),
                                 stop=(db == 7))
            s_tp.inc(mmi)
            bankW[5] = s_tp.mark()
            guard_read(VEC, 5)
            sp = p_ * 8 + h_
            s_dv.inc(VEC.tensor_scalar(
                out=kwT[:, :, 128 * sp:128 * sp + 128],
                in0=psbf[:, 5120:6144].rearrange("p (a b) -> p a b", b=128),
                scalar1=0.0, scalar2=None, op0=ALU.add))
            bankR.setdefault(5, []).append(s_dv.mark())

        for jb in range(16):
            ftr_row(jb, w_inf[128 * jb:128 * jb + 128, :], 8,
                    lambda half, src3, jb=jb: VEC.tensor_scalar(
                        out=winfT[:, :, 128 * jb:128 * jb + 128], in0=src3,
                        scalar1=0.0, scalar2=None, op0=ALU.add))
        for db in range(8):
            ftr_row(db, w_outf[128 * db:128 * db + 128, :], 16,
                    lambda half, src3, db=db: VEC.tensor_scalar(
                        out=woutfT[:, 8 * half:8 * half + 8,
                                   128 * db:128 * db + 128],
                        in0=src3, scalar1=0.0, scalar2=None, op0=ALU.add))
        ctx0.close()
        barrier()
        if stages == 'w':
            mybir.codegen_inst_isa_subclasses(nc)
            return nc

        ctxB = ExitStack()
        sbB = lambda name, shape, dt: ctxB.enter_context(nc.sbuf_tensor(name, shape, dt, side="right"))
        x2fm = sbB("x2fm", [128, 8, NB], F32)
        x2tm = sbB("x2tm", [128, 2, D], F32)
        h2T = sbB("h2T", [128, 2, D], BF16)
        h2fm = sbB("h2fm", [128, 2, 8, NB], BF16)
        dmat = sbB("dmat", [128, 8, 128], BF16)
        ghT = sbB("ghT", [128, DI], BF16)
        gfm = sbB("gfm", [128, 16, 128], BF16)
        inA = sbB("inA", [128, 8, 1024], BF16)
        inBr = sbB("inBr", [128, 16384], BF16)
        x3t = ghT.bitcast(F32)
        outbuf = sbB("outbuf", [128, 3, 3072], BF16)
        thfbuf = sbB("thfbuf", [128, D], F32)
        eT = sbB("eT", [8, 128], I16)
        idx16 = sbB("idx16", [128, 2, 64], I16)
        mx = sbB("mx", [128, 16], F32)
        mix = sbB("mix", [128, 16], U32)
        gatev = sbB("gatev", [128, 2, 8], F32)
        bnout = sbB("bnout", [128, 12], F32)
        mvst = sbB("mvst", [128, 2], F32)
        rs3 = sbB("rs3", [128, 1], F32)
        p3 = sbB("p3", [128, 1], F32)
        ef = sbB("ef", [128, 8], F32)
        e16 = sbB("e16", [128, 8], I16)
        w64 = sbB("w64", [64, 16], I16)
        lin = sbB("lin", [128, 8], F32)
        ling = sbB("ling", [128, 8], F32)
        vB = inBr.rearrange("p (h c) -> p h c", c=2048)
        vO = inBr[:, 0:15360].rearrange("p (h c) -> p h c", c=3072)
        nidx_reg = GPS.alloc_register()
        GPS.reg_mov(nidx_reg, 1024)
        nidx_reg2 = GPS.alloc_register()
        GPS.reg_mov(nidx_reg2, 384)
        nidx_reg3 = GPS.alloc_register()
        GPS.reg_mov(nidx_reg3, 640)

        M = {}

        def setm(name, sem):
            M[name] = sem.mark()

        def wm(eng, name):
            mk = M.get(name)
            if mk is not None:
                mk[0].wait(eng, mk[1])

        VEC.memset(idx16[:], 0)
        thf = thfbuf
        dscr = thfbuf.bitcast(BF16)

        def front(b):
            par = b % 2
            wm(SYN, "x2fm_cons")
            s_bx.inc(SYN.dma_start(out=x2fm[:],
                                   in_=x2buf_v[:, :, NB * b:NB * b + NB]), 16)
            setm("x2fm_prod", s_bx)
            wm(TEN, "x2fm_prod")
            for hf in range(2):
                bki = 4 + hf
                guard_write(TEN, bki)
                for q in range(4):
                    db = hf * 4 + q
                    mmi = TEN.matmul(bank(bki)[:, 128 * q:128 * q + 128],
                                     x2fm[:, db, :], idn_f32[:],
                                     is_transpose=True, start=(q == 0),
                                     stop=(q == 3))
                s_tp.inc(mmi)
                bankW[bki] = s_tp.mark()
                if hf == 1:
                    setm("x2fm_cons", s_tp)
                guard_read(ACT2, bki)
                wm(ACT2, "x2tm_cons%d" % par)
                s_ac.inc(ACT2.activation(out=x2tm[:, par, 512 * hf:512 * hf + 512],
                                         in_=bank(bki), func=AF.Copy, bias=0.0))
                bankR.setdefault(bki, []).append(s_ac.mark())
            setm("x2tm_prod%d" % par, s_ac)
            wm(VEC, "x2tm_prod%d" % par)
            VEC.bn_stats(out=bnout[:, 0:6], in_=x2tm[:, par, 0:512])
            VEC.bn_stats(out=bnout[:, 6:12], in_=x2tm[:, par, 512:1024])
            s_dv.inc(VEC.bn_aggr(out=mvst[:], in_=bnout[:]))
            s_dv.wait(ACT2, s_dv.n)
            s_ac.inc(ACT2.activation(out=rs3[:], in_=mvst[:, 1:2], func=AF.Sqrt,
                                     bias=eps128[:]))
            s_ac.wait(VEC, s_ac.n)
            VEC.reciprocal(out=rs3[:], in_=rs3[:])
            s_dv.inc(VEC.scalar_tensor_tensor(out=p3[:], in0=mvst[:, 0:1],
                                              scalar=-1.0, in1=rs3[:],
                                              op0=ALU.mult, op1=ALU.mult))
            s_dv.wait(ACT2, s_dv.n)
            s_ac.inc(ACT2.activation(out=thf[:], in_=x2tm[:, par, :],
                                     func=AF.Identity, bias=p3[:],
                                     scale=rs3[:]))
            s_ac.wait(VEC, s_ac.n)
            VEC.tensor_tensor(out=thf[:], in0=thf[:], in1=cbtm[:], op=ALU.mult)
            s_dv.inc(VEC.tensor_tensor(out=h2T[:, par, :], in0=thf[:],
                                       in1=sbtm[:], op=ALU.add))
            setm("thf_done", s_dv)
            setm("h2T_prod%d" % par, s_dv)
            wm(TEN, "h2T_prod%d" % par)
            wm(TEN, "h2fm_cons%d" % par)
            for hf in range(2):
                guard_write(TEN, 4)
                for q in range(4):
                    db = hf * 4 + q
                    mmi = TEN.matmul(psbf[:, 4096 + 128 * q:4096 + 128 * q + 128],
                                     h2T[:, par, 128 * db:128 * db + 128],
                                     idn_bf[:], is_transpose=True,
                                     start=(q == 0), stop=(q == 3))
                s_tp.inc(mmi)
                bankW[4] = s_tp.mark()
                guard_read(ACT2, 4)
                s_ac.inc(ACT2.activation(
                    out=h2fm[:, par, 4 * hf:4 * hf + 4, :],
                    in_=psbf[:, 4096:4608].rearrange("p (a b) -> p a b", b=128),
                    func=AF.Copy, bias=0.0))
                bankR.setdefault(4, []).append(s_ac.mark())
            setm("h2fm_prod%d" % par, s_ac)
            wm(TEN, "h2fm_prod%d" % par)
            for half in range(2):
                b0 = 4 - 2 * half
                guard_write(TEN, b0)
                guard_write(TEN, b0 + 1)
                for ks in range(2):
                    for db in range(8):
                        mmi = TEN.matmul(
                            bank(b0 + ks), h2fm[:, par, db, :],
                            kwT[:, db, 1024 * half + 512 * ks:
                                1024 * half + 512 * ks + 512],
                            start=(db == 0), stop=(db == 7))
                    s_mm.inc(mmi)
                    bankW[b0 + ks] = s_mm.mark()
                guard_read(VEC, b0)
                guard_read(VEC, b0 + 1)
                VEC.tensor_reduce(out=mx[:, 8 * half:8 * half + 8],
                                  in_=ps[:, 512 * b0:512 * b0 + 1024].rearrange(
                                      "p (s k) -> p s k", k=128),
                                  axis=AX.X, op=ALU.max)
                s_dv.inc(VEC.max_index(out=mix[:, 8 * half:8 * half + 8],
                                       in_max=mx[:, 8 * half:8 * half + 8],
                                       in_values=ps[:, 512 * b0:512 * b0 + 1024]))
                bankR.setdefault(b0, []).append(s_dv.mark())
                bankR.setdefault(b0 + 1, []).append(s_dv.mark())
            VEC.tensor_tensor(out=gatev[:, par, :], in0=mx[:, 0:8],
                              in1=mx[:, 8:16], op=ALU.add)
            VEC.scalar_tensor_tensor(out=ef[:], in0=mix[:, 0:8], scalar=128.0,
                                     in1=mix[:, 8:16], op0=ALU.mult,
                                     op1=ALU.add)
            VEC.tensor_scalar(out=gatev[:, par, :], in0=gatev[:, par, :],
                              scalar1=0.0, scalar2=None, op0=ALU.max)
            s_dv.inc(VEC.tensor_tensor(out=e16[:], in0=ef[:], in1=c16512[:],
                                       op=ALU.subtract))
            s_dv.wait(TEN, s_dv.n)
            guard_write(TEN, 5)
            s_tp.inc(TEN.matmul(psbf[0:8, 5120:5248], e16.bitcast(BF16)[:],
                                idn_bf[:], is_transpose=True, start=True,
                                stop=True))
            bankW[5] = s_tp.mark()

        def fidx(b):
            par = b % 2
            s_tp.wait(VEC, s_tp.n)
            wm(VEC, "eT_cons")
            s_dv.inc(VEC.tensor_scalar(out=eT[:],
                                       in0=ps.bitcast(I16)[0:8, 5120:5248],
                                       scalar1=0, scalar2=None, op0=ALU.add))
            bankR.setdefault(5, []).append(s_dv.mark())
            s_dv.wait(SYN, s_dv.n)
            wm(SYN, "w64_cons")
            s_bidx.inc(SYN.dma_start(out=w64[:], in_=eT[:]), 16)
            setm("eT_cons", s_bidx)
            s_bidx.wait(TEN)
            guard_write(TEN, 5)
            s_tp.inc(TEN.matmul(psbf[0:16, 5376:5440], w64.bitcast(BF16)[:],
                                idn_bf[0:64, 0:64], is_transpose=True,
                                start=True, stop=True))
            setm("w64_cons", s_tp)
            bankW[5] = s_tp.mark()
            s_tp.wait(VEC, s_tp.n)
            wm(VEC, "idx_cons%d" % par)
            s_dv.inc(VEC.tensor_scalar(out=idx16[0:16, par, :],
                                       in0=ps.bitcast(I16)[0:16, 5376:5440],
                                       scalar1=0, scalar2=None, op0=ALU.add))
            bankR.setdefault(5, []).append(s_dv.mark())
            s_dv.wait(SYN, s_dv.n)
            s_bidx.inc(SYN.dma_start(out=idx16[16:32, par, :],
                                     in_=idx16[0:16, par, :]), 16)
            setm("idx_prod%d" % par, s_bidx)

        def gather_A(b):
            par = b % 2
            wm(GPS, "idx_prod%d" % par)
            wm(GPS, "inA_free")
            if b == 0:
                emit_conv(16)
                s_conv.wait(GPS)
            s_bg.inc(GPS.dma_gather(out_ap=inA[:], in_ap=ptab_bf[:, 0:1024],
                                    idxs_ap=idx16[:, par, :], num_idxs=1024,
                                    num_idxs_reg=nidx_reg, elem_size=1024,
                                    elem_step=6144, single_packet=False), 16)
            setm("gA_prod", s_bg)

        def gather_B_out0(b):
            par = b % 2
            wm(GPS, "idx_prod%d" % par)
            wm(GPS, "inB_free")
            s_bg.inc(GPS.dma_gather(out_ap=vB, in_ap=ptab_bf[:, 1024:3072],
                                    idxs_ap=idx16[:, par, :], num_idxs=1024,
                                    num_idxs_reg=nidx_reg, elem_size=2048,
                                    elem_step=6144, single_packet=False), 16)
            setm("gB_prod", s_bg)
            wm(GPS, "outbuf_free")
            s_bg.inc(GPS.dma_gather(out_ap=outbuf[:],
                                    in_ap=ptab_bf[:, 3072:6144],
                                    idxs_ap=idx16[:, par, 0:24], num_idxs=384,
                                    num_idxs_reg=nidx_reg2, elem_size=3072,
                                    elem_step=6144, single_packet=False), 16)
            setm("gout0_prod", s_bg)

        def emit_final(b):
            par = b % 2
            guard_read(VEC, 6)
            guard_read(VEC, 7)
            if b >= 1:
                s_out.wait(VEC, 16 * b)
            VEC.tensor_tensor(out=x3t[:], in0=ps[:, 3072:4096], in1=gbtm[:],
                              op=ALU.mult)
            s_dv.inc(VEC.tensor_tensor(out=x3t[:], in0=x3t[:],
                                       in1=x2tm[:, par, :], op=ALU.add))
            bankR.setdefault(6, []).append(s_dv.mark())
            bankR.setdefault(7, []).append(s_dv.mark())
            setm("x2tm_cons%d" % par, s_dv)
            s_dv.wait(SYN, s_dv.n)
            s_out.inc(SYN.dma_start(out=outd[NB * b:NB * b + NB, :], in_=x3t[:]),
                      16)
            setm("x3_store", s_out)

        def back1(b):
            par = b % 2
            wm(VEC, "gA_prod")
            for h in range(8):
                VEC.scalar_tensor_tensor(out=dscr[:, 0:1024],
                                         in0=inA[:, h, :],
                                         scalar=gatev[:, par, h:h + 1],
                                         in1=h2T[:, par, :], op0=ALU.mult,
                                         op1=ALU.mult, accum_out=lin[:, h:h + 1])
            s_dv.inc(VEC.nop())
            setm("inA_free", s_dv)
            s_dv.wait(ACT2, s_dv.n)
            wm(ACT2, "dmat_cons")
            for h in range(8):
                ia = ACT2.activation(out=dmat[:, h, :], in_=idn_bf[:],
                                     func=AF.Copy, bias=0.0,
                                     scale=lin[:, h:h + 1])
            s_ac.inc(ia)
            setm("dmat_prod", s_ac)
            if b >= 1:
                emit_final(b - 1)
            wm(TEN, "h2fm_prod%d" % par)
            for js in range(4):
                guard_write(TEN, js)
                for db in range(8):
                    TEN.matmul(bank(js), h2fm[:, par, db, :],
                               winfT[:, db, 512 * js:512 * js + 512],
                               start=(db == 0), stop=False)
            wm(TEN, "gB_prod")
            wm(TEN, "dmat_prod")
            for js in range(4):
                for h in range(8):
                    mmi = TEN.matmul(bank(js), dmat[:, h, :],
                                     vB[:, h, 512 * js:512 * js + 512],
                                     start=False, stop=(h == 7))
                s_mm.inc(mmi)
                bankW[js] = s_mm.mark()
            setm("h2fm_cons%d" % par, s_mm)
            setm("dmat_cons", s_mm)
            setm("vB_read", s_mm)
            wm(GPS, "vB_read")
            s_bg.inc(GPS.dma_gather(out_ap=vO,
                                    in_ap=ptab_bf[:, 3072:6144],
                                    idxs_ap=idx16[:, par, 24:64], num_idxs=640,
                                    num_idxs_reg=nidx_reg3, elem_size=3072,
                                    elem_step=6144, single_packet=False), 16)
            setm("gout1_prod", s_bg)
            setm("idx_cons%d" % par, s_bg)
            wm(ACT2, "ghT_cons_t")
            wm(ACT2, "ghT_cons_v")
            wm(ACT2, "x3_store")
            for k in range(4):
                guard_read(ACT2, k)
                s_ac.inc(ACT2.activation(out=ghT[:, 512 * k:512 * k + 512],
                                         in_=bank(k), func=AF.Gelu))
                bankR.setdefault(k, []).append(s_ac.mark())
            setm("ghT_prod", s_ac)

        def back2pre(b):
            par = b % 2
            wm(VEC, "ghT_prod")
            wm(VEC, "gout0_prod")
            for h in range(3):
                VEC.scalar_tensor_tensor(out=dscr[:],
                                         in0=outbuf[:, h, 0:2048],
                                         scalar=gatev[:, par, h:h + 1],
                                         in1=ghT[:], op0=ALU.mult,
                                         op1=ALU.mult, accum_out=lin[:, h:h + 1])
            wm(TEN, "ghT_prod")
            for bki in (6, 7):
                guard_write(TEN, bki)
                for jb in range(8):
                    j = (bki - 6) * 8 + jb
                    mmi = TEN.matmul(
                        psbf[:, 1024 * bki + 128 * jb:1024 * bki + 128 * jb + 128],
                        ghT[:, 128 * j:128 * j + 128], idn_bf[:],
                        is_transpose=True, start=(jb == 0), stop=(jb == 7))
                s_tp.inc(mmi)
                bankW[bki] = s_tp.mark()
            setm("ghT_cons_t", s_tp)

        def back2(b):
            par = b % 2
            for bki in (6, 7):
                guard_read(ACT2, bki)
                s_ac.inc(ACT2.activation(
                    out=gfm[:, 8 * (bki - 6):8 * (bki - 6) + 8, :],
                    in_=psbf[:, 1024 * bki:1024 * bki + 1024].rearrange(
                        "p (a b) -> p a b", b=128),
                    func=AF.Copy, bias=0.0))
                bankR.setdefault(bki, []).append(s_ac.mark())
            setm("gfm_prod", s_ac)
            wm(VEC, "gout1_prod")
            for h in range(5):
                VEC.scalar_tensor_tensor(out=dscr[:],
                                         in0=vO[:, h, 0:2048],
                                         scalar=gatev[:, par, 3 + h:4 + h],
                                         in1=ghT[:], op0=ALU.mult,
                                         op1=ALU.mult,
                                         accum_out=lin[:, 3 + h:4 + h])
            s_dv.inc(VEC.nop())
            setm("dots_out_done", s_dv)
            setm("ghT_cons_v", s_dv)
            s_dv.wait(ACT2, s_dv.n)
            wm(ACT2, "dmat_cons")
            for h in range(8):
                ia = ACT2.activation(out=dmat[:, h, :], in_=idn_bf[:],
                                     func=AF.Copy, bias=0.0,
                                     scale=lin[:, h:h + 1])
            s_ac.inc(ia)
            setm("dmat_prod", s_ac)
            wm(TEN, "gfm_prod")
            for js in range(2):
                guard_write(TEN, 6 + js)
                for jb in range(16):
                    TEN.matmul(bank(6 + js), gfm[:, jb, :],
                               woutfT[:, jb, 512 * js:512 * js + 512],
                               start=(jb == 0), stop=False)
            wm(TEN, "dmat_prod")
            wm(TEN, "gout0_prod")
            wm(TEN, "gout1_prod")
            for js in range(2):
                for h in range(3):
                    TEN.matmul(bank(6 + js), dmat[:, h, :],
                               outbuf[:, h, 2048 + 512 * js:
                                      2048 + 512 * js + 512],
                               start=False, stop=False)
                for h in range(5):
                    mmi = TEN.matmul(bank(6 + js), dmat[:, 3 + h, :],
                                     vO[:, h, 2048 + 512 * js:
                                        2048 + 512 * js + 512],
                                     start=False, stop=(h == 4))
                s_mm.inc(mmi)
                bankW[6 + js] = s_mm.mark()
            setm("outbuf_free", s_mm)
            setm("inB_free", s_mm)
            setm("dmat_cons", s_mm)

        front(0)
        fidx(0)
        gather_A(0)
        gather_B_out0(0)
        for b in range(TB):
            back1(b)
            if b + 1 < TB:
                front(b + 1)
            back2pre(b)
            if b + 1 < TB:
                fidx(b + 1)
            back2(b)
            if b + 1 < TB:
                gather_A(b + 1)
                gather_B_out0(b + 1)
        emit_final(TB - 1)

        s_out.wait(SYN)
        ctxB.close()

    mybir.codegen_inst_isa_subclasses(nc)
    return nc


_NC_CACHE = {}


def kernel(L0_attn_x, L0_attn_W_in, L0_attn_ln_g, L0_attn_ln_b, L0_attn_W_gate,
           L0_attn_b_gate, L0_attn_eig_raw, L0_attn_W_out, L0_ffn_Wq,
           L0_ffn_keys, L0_ffn_W_in, L0_ffn_W_out, L0_ffn_in_a, L0_ffn_in_b,
           L0_ffn_out_a, L0_ffn_out_b, L0_c, L0_wmod):
    cc = lambda x: np.ascontiguousarray(np.asarray(x, dtype=np.float32))
    ptab = np.ascontiguousarray(np.concatenate(
        [cc(L0_ffn_in_a), cc(L0_ffn_in_b), cc(L0_ffn_out_a), cc(L0_ffn_out_b)],
        axis=1))
    shared = {
        "w_in": cc(L0_attn_W_in), "ln_gd": cc(L0_attn_ln_g).reshape(1, D),
        "ln_bd": cc(L0_attn_ln_b).reshape(1, D),
        "w_gate": cc(L0_attn_W_gate),
        "b_gated": cc(L0_attn_b_gate).reshape(1, 32),
        "eig_d": cc(L0_attn_eig_raw).reshape(1, H16),
        "w_out": cc(L0_attn_W_out), "wq": cc(L0_ffn_Wq),
        "keysd": cc(L0_ffn_keys).reshape(2048, DK),
        "w_inf": cc(L0_ffn_W_in), "w_outf": cc(L0_ffn_W_out), "ptab": ptab,
        "wmod": cc(L0_wmod),
    }
    x = cc(L0_attn_x)
    c = cc(L0_c)
    if "nc" not in _NC_CACHE:
        _NC_CACHE["nc"] = build()
    nc = _NC_CACHE["nc"]
    in_maps = []
    for b in range(NCORES):
        m = dict(shared)
        m["xin"] = np.ascontiguousarray(x[b])
        m["cvec"] = np.ascontiguousarray(c[b:b + 1])
        in_maps.append(m)
    res = run_bass_kernel_spmd(nc, in_maps, core_ids=list(range(NCORES)))
    return np.stack([np.asarray(r["outd"], dtype=np.float32)
                     for r in res.results])
